# version 34
# baseline (speedup 1.0000x reference)
import numpy as np
import ml_dtypes

import concourse.bass as bass
import concourse.mybir as mybir
from concourse.bass_isa import ReduceOp
from concourse import library_config
from concourse.bass_utils import run_bass_kernel_spmd

F32 = mybir.dt.float32
BF16 = mybir.dt.bfloat16
AF = mybir.ActivationFunctionType
ALU = mybir.AluOpType
AX = mybir.AxisListType
BF = ml_dtypes.bfloat16

ENGINES = ("sync", "scalar", "vector", "tensor", "gpsimd")

D = 1024
H = 4
DK = 256
DV = 512
DVT = H * DV
PC = D // 128
DPC = DVT // 128
CH = 128
GR = 256
ROPE_BASE = 10000.0
EPS = 1e-5
VELT_C = float(2 ** 20 + 1)
FP8_MAX = 448.0


def lamh(h):
    return 1.0 - 2.0 ** (-5.0 - h)


class Sem:
    def __init__(self, nc, name, dma=False):
        self.name = name
        self.dma = dma
        self.n = 0


class Prog:

    def __init__(self, nc):
        self.nc = nc
        self.n_ops = 0
        self.engine = []
        self.fn = []
        self.deps = []
        self.dur = []
        self.isdma = []
        self.isasync = []
        self.tag = []
        self.run = []
        self.boost = []
        self.n_runs = 0
        self.sched = None

    def new_run(self):
        self.n_runs += 1
        return self.n_runs

    def sem(self, name, dma=False):
        return Sem(self.nc, name, dma)

    def op(self, eng, fn, waits=(), inc=None, amt=1, ns=None, dma=False,
           async_op=False, run=None, boost=0.0):
        flat = set()

        def _fl(w):
            if w is None:
                return
            if isinstance(w, (list, tuple)):
                for x in w:
                    _fl(x)
            else:
                flat.add(w)
        _fl(list(waits))
        deps = sorted(flat)
        i = self.n_ops
        self.n_ops += 1
        isdma = dma or amt == 16
        self.isasync.append(isdma or async_op)
        self.run.append(None if isdma else run)
        self.boost.append(boost)
        self.engine.append(eng)
        self.fn.append(fn)
        self.deps.append(deps)
        self.dur.append(ns if ns is not None else (1500 if isdma else 350))
        self.isdma.append(isdma)
        self.tag.append(inc if isdma else None)
        return i

    def dma(self, eng, out, in_, waits=(), inc=None, boost=0.0):
        assert out.size() == in_.size(), (
            f"dma size mismatch {out.size()} vs {in_.size()} "
            f"out={out.tensor.name} in={in_.tensor.name}")
        assert inc is not None
        ns = 900 + in_.nbytes() // 300
        return self.op(eng, lambda e, o=out, i=in_: e.dma_start(out=o, in_=i),
                       waits=waits, inc=inc, ns=ns, dma=True, boost=boost)

    def emit(self, reorder=True):
        n = self.n_ops
        mid_of = [0] * n
        members = []
        runmap = {}
        for i in range(n):
            r = self.run[i]
            if r is not None and (self.engine[i], r) in runmap:
                m = runmap[(self.engine[i], r)]
                members[m].append(i)
                mid_of[i] = m
            else:
                m = len(members)
                members.append([i])
                mid_of[i] = m
                if r is not None:
                    runmap[(self.engine[i], r)] = m
        nm = len(members)
        mdeps = [set() for _ in range(nm)]
        for i in range(n):
            for d in self.deps[i]:
                if mid_of[d] != mid_of[i]:
                    mdeps[mid_of[i]].add(mid_of[d])
        mdur = [sum(self.dur[i] for i in mem) for mem in members]
        meng = [self.engine[mem[0]] for mem in members]
        misdma = [self.isdma[mem[0]] for mem in members]
        mchild = [[] for _ in range(nm)]
        for m in range(nm):
            for d in mdeps[m]:
                mchild[d].append(m)
        mprio = [0.0] * nm
        mboost = [max(self.boost[i] for i in mem) for mem in members]
        for m in range(nm - 1, -1, -1):
            mx = 0.0
            for c in mchild[m]:
                if mprio[c] > mx:
                    mx = mprio[c]
            mprio[m] = mdur[m] + mx + mboost[m]
        import heapq
        engs = sorted(set(self.engine))
        etime = {e: 0.0 for e in engs}
        done_t = [0.0] * nm
        ndep = [len(mdeps[m]) for m in range(nm)]
        ready = {e: [] for e in engs}
        for m in range(nm):
            if ndep[m] == 0:
                heapq.heappush(ready[meng[m]], (-mprio[m], m))
        morder = []
        scheduled = 0
        if not reorder:
            morder = list(range(nm))
        else:
            while scheduled < nm:
                best = None
                for e in engs:
                    if not ready[e]:
                        continue
                    pr_, m = ready[e][0]
                    st = max(etime[e],
                             max((done_t[d] + (250.0 if meng[d] != e else 0.0)
                                  for d in mdeps[m]), default=0.0))
                    key = (st, pr_)
                    if best is None or key < best[0]:
                        best = (key, e, m)
                assert best is not None, "scheduler stuck"
                _, e, m = best
                heapq.heappop(ready[e])
                st = max(etime[e],
                         max((done_t[d] + (250.0 if meng[d] != e else 0.0)
                              for d in mdeps[m]), default=0.0))
                if misdma[m]:
                    etime[e] = st + 150.0
                    done_t[m] = st + mdur[m]
                else:
                    etime[e] = st + mdur[m]
                    done_t[m] = etime[e]
                morder.append(m)
                scheduled += 1
                for c in mchild[m]:
                    ndep[c] -= 1
                    if ndep[c] == 0:
                        heapq.heappush(ready[meng[c]], (-mprio[c], c))
        order = []
        for m in morder:
            order.extend(members[m])
        streams = {e: [] for e in engs}
        for i in order:
            streams[self.engine[i]].append(i)

        def semkey(i):
            return (self.engine[i], self.tag[i]) if self.isdma[i] \
                else (self.engine[i], None)

        def sem_dep(d, i):
            return self.isasync[d] or self.isasync[i] or \
                not (self.engine[d] == self.engine[i] == "tensor")

        need_inc = [False] * n
        for i in range(n):
            for d in self.deps[i]:
                if sem_dep(d, i):
                    need_inc[d] = True
        for i in range(n):
            if self.isasync[i]:
                need_inc[i] = True

        sems = {}
        for i in order:
            if not need_inc[i]:
                continue
            k = semkey(i)
            if k not in sems:
                nm = f"s{len(sems)}_{k[0]}" + ("d" if self.isdma[i] else "")
                sems[k] = self.nc.alloc_semaphore(nm)
        cum = [0] * n
        scount = {}
        for e in engs:
            for i in streams[e]:
                k = semkey(i)
                if need_inc[i]:
                    scount[k] = scount.get(k, 0) + \
                        (16 if self.isdma[i] else 1)
                cum[i] = scount.get(k, 0)
        waited = {e: {} for e in engs}

        def wait_list(i):
            out = []
            for d in self.deps[i]:
                if not sem_dep(d, i):
                    continue
                out.append((semkey(d), cum[d]))
            if self.isdma[i] and cum[i] > 16:
                out.append((semkey(i), cum[i] - 16))
            return out

        import os
        if os.environ.get("PROG_DEBUG"):
            import pickle
            with open("/tmp/prog_dbg.pkl", "wb") as f:
                pickle.dump(dict(order=order, streams=streams, cum=cum,
                                 need_inc=need_inc, deps=self.deps,
                                 engine=self.engine, isdma=self.isdma,
                                 tag={i: str(self.tag[i]) for i in range(n)
                                      if self.isdma[i]}), f)
        with self.nc.Block() as blk:
            for eng_name in engs:
                ops_ = streams[eng_name]
                if not ops_:
                    continue

                def make(ops_=ops_, eng_name=eng_name):
                    def body(eng):
                        w8 = waited[eng_name]
                        for i in ops_:
                            for key, v in wait_list(i):
                                if w8.get(key, 0) < v:
                                    eng.wait_ge(sems[key], v)
                                    w8[key] = v
                            ins = self.fn[i](eng)
                            if need_inc[i]:
                                ins.then_inc(sems[semkey(i)],
                                             16 if self.isdma[i] else 1)
                    return body

                getattr(blk, eng_name)(make())
def ap(t, off, dims):
    return bass.AP(t, off, [list(d) for d in dims])


def build(T, n_cores):
    NG = T // GR
    nc = bass.Bass("TRN2", target_bir_lowering=False)

    xT = nc.dram_tensor("xT", [D, T], F32, kind="ExternalInput")
    wq_t = nc.dram_tensor("wq_t", [D, D], F32, kind="ExternalInput")
    wk_t = nc.dram_tensor("wk_t", [D, D], F32, kind="ExternalInput")
    wv_t = nc.dram_tensor("wv_t", [D, DVT], F32, kind="ExternalInput")
    wg_t = nc.dram_tensor("wg_t", [DVT, D], F32, kind="ExternalInput")
    woT = nc.dram_tensor("woT", [DVT, D], F32, kind="ExternalInput")
    wf_t = nc.dram_tensor("wf_t", [D, D], F32, kind="ExternalInput")
    wmod = nc.dram_tensor("wmod", [D, 6 * D], F32, kind="ExternalInput")
    cT = nc.dram_tensor("cT", [128, PC], F32, kind="ExternalInput")
    gnw = nc.dram_tensor("gnw", [128, DPC], F32, kind="ExternalInput")
    bfc = nc.dram_tensor("bfc", [128, PC], F32, kind="ExternalInput")
    cosT = nc.dram_tensor("cosT", [128, T], BF16, kind="ExternalInput")
    sinT = nc.dram_tensor("sinT", [128, T], BF16, kind="ExternalInput")
    lamqI = nc.dram_tensor("lamqI", [128, 1024], BF16, kind="ExternalInput")
    lamsigI = nc.dram_tensor("lamsigI", [128, H], F32, kind="ExternalInput")
    lamkI = nc.dram_tensor("lamkI", [128, H], F32, kind="ExternalInput")
    maskI = nc.dram_tensor("maskI", [128, 128], BF16, kind="ExternalInput")
    identI = nc.dram_tensor("identI", [128, 128], BF16, kind="ExternalInput")
    onesI = nc.dram_tensor("onesI", [128, 128], BF16, kind="ExternalInput")
    constsI = nc.dram_tensor("constsI", [128, 8], F32, kind="ExternalInput")

    yT = nc.dram_tensor("yT", [D, T], F32, kind="ExternalOutput")

    wq_bf = nc.dram_tensor("wq_bf", [D, D], BF16)
    wk_bf = nc.dram_tensor("wk_bf", [D, D], BF16)
    wv_bf = nc.dram_tensor("wv_bf", [D, DVT], BF16)
    wg_bf = nc.dram_tensor("wg_bf", [DVT, D], BF16)
    wo_m = nc.dram_tensor("wo_m", [PC * 128, DVT], BF16)
    wf_q = nc.dram_tensor("wf_q", [D, D], BF16)
    mod_dram = nc.dram_tensor("mod_dram", [1, 6 * D], F32)
    wrow = nc.dram_tensor("wrow", [1, 128], F32)
    srow = nc.dram_tensor("srow", [1, 8], F32)
    cc_in = nc.dram_tensor("cc_in", [1, 128], F32)
    cc_out = nc.dram_tensor("cc_out", [1, 128], F32)
    stats_dram = nc.dram_tensor("stats_dram", [128, (T // CH) * 256], F32)
    ln1_dram = nc.dram_tensor("ln1_dram", [128, (T // CH) * 256], F32)

    A = nc.alloc_sbuf_tensor
    x_res = A("x_res", [128, PC * T], F32)
    slabP = A("slabP", [128, 6144], BF16)
    xh = A("xh", [128, PC * GR], BF16)
    xh2 = xh
    qt = A("qt", [128, PC * GR], BF16)
    kt = A("kt", [128, PC * GR], BF16)
    v_tm = A("v_tm", [128, 2 * DVT], BF16)
    gsil = A("gsil", [128, DPC * GR], BF16)
    og2 = A("og2", [128, DPC * 128], BF16)
    S = A("S", [128, H * 2 * DV], F32)
    k_tm = A("k_tm", [128, H * DK], BF16)
    sco_f = A("sco_f", [128, 256], F32)
    sco = sco_f.ap().bitcast(BF16)
    ropeB = A("ropeB", [128, H * GR], BF16)
    cos_sl = A("cos_sl", [128, GR], BF16)
    sin_sl = A("sin_sl", [128, GR], BF16)
    lamq_s = A("lamq_s", [128, 1024], BF16)
    mask_s = A("mask_s", [128, 128], BF16)
    ident_s = A("ident_s", [128, 128], BF16)
    ones_s = A("ones_s", [128, 128], BF16)
    lamsig_s = A("lamsig_s", [128, H], F32)
    lamk_s = A("lamk_s", [128, H], F32)
    consts = A("consts", [128, 8], F32)
    modv = A("modv", [128, 48], F32)
    one_ca = A("one_ca", [128, PC], F32)
    one_cb = A("one_cb", [128, PC], F32)
    ga2 = A("ga2", [128, PC], F32)
    gb2 = A("gb2", [128, PC], F32)
    ocbs = A("ocbs", [128, PC], F32)
    sbs = A("sbs", [128, PC], F32)
    gnw_s = A("gnw_s", [128, DPC], F32)
    bf_s = A("bf_s", [128, PC], F32)
    cT_s = A("cT_s", [128, PC], F32)
    silu_c = A("silu_c", [128, PC], F32)
    RB = A("RB", [128, 512], F32)
    SCR_F = A("SCR_F", [128, 1024], F32)
    SCR_B = A("SCR_B", [128, 2048], BF16)
    amax_c = A("amax_c", [128, 1], F32)
    red_c = A("red_c", [128, 1], F32)
    small = A("small", [128, 16], F32)
    S_bf = ropeB
    VT = v_tm.ap().bitcast(F32)[:, 0:1024]
    VT2 = v_tm.ap().bitcast(F32)[:, 1024:2048]
    OS = [qt.ap().bitcast(F32), kt.ap().bitcast(F32)]

    sbuf_left = nc.sbuf_bytes_remaining

    P = nc.alloc_psum_tensor
    PB = [P("PB0", [128, 512], F32), P("PB1", [128, 512], F32)]
    PO = [P("POA", [128, 512], F32), P("POB", [128, 512], F32)]
    PX = P("PX", [128, 512], F32)
    PTR = P("PTR", [128, 1024], BF16)
    PXS = [P("PX2", [128, 512], F32), P("PX3", [128, 512], F32)]

    pr = Prog(nc)
    sMisc = pr.sem("sMisc", dma=True)
    sXq = [pr.sem("sX0", dma=True), pr.sem("sX1", dma=True),
           pr.sem("sX2", dma=True)]
    sTab = pr.sem("sTab", dma=True)
    sSlab = [pr.sem("sSlab0", dma=True), pr.sem("sSlab1", dma=True),
             pr.sem("sSlab2", dma=True)]
    sSlabX = [pr.sem("sSlabX0", dma=True), pr.sem("sSlabX1", dma=True),
              pr.sem("sSlabX2", dma=True)]
    sCF = pr.sem("sCF", dma=True)
    sCG = pr.sem("sCG", dma=True)
    sOF = pr.sem("sOF", dma=True)
    sOG = pr.sem("sOG", dma=True)
    sOmod = pr.sem("sOmod", dma=True)
    sOout = pr.sem("sOout", dma=True)
    sG = pr.sem("sG", dma=True)
    sStatO = pr.sem("sStatO", dma=True)
    sStatI = pr.sem("sStatI", dma=True)
    sPO = pr.sem("sPO", dma=True)
    sPI = pr.sem("sPI", dma=True)
    sWM1 = pr.sem("sWM1", dma=True)
    sWM2 = pr.sem("sWM2", dma=True)
    sB1 = pr.sem("sB1", dma=True)
    sB2 = pr.sem("sB2", dma=True)
    sSlabS = [pr.sem("sSlabS0", dma=True), pr.sem("sSlabS1", dma=True),
              pr.sem("sSlabS2", dma=True)]
    sGP = pr.sem("sGP")
    sCC = pr.sem("sCC")
    sT = pr.sem("sT")
    sA = pr.sem("sA")
    sV = pr.sem("sV")

    ZC = lambda: consts[:, 6:7]
    OC = lambda: consts[:, 7:8]

    misc_tok = []
    for dst, src in [(consts, constsI), (mask_s, maskI), (ident_s, identI),
                     (ones_s, onesI), (lamq_s, lamqI), (lamsig_s, lamsigI),
                     (lamk_s, lamkI), (gnw_s, gnw), (bf_s, bfc), (cT_s, cT)]:
        misc_tok.append(pr.dma("sync", dst[:, :], src[:, :], inc=sMisc,
                                boost=1e7))

    xloads = []
    for p in range(PC):
        xq = ["sync", "scalar", "gpsimd"][p % 3]
        xloads.append(pr.dma(xq, x_res[:, p * T:(p + 1) * T],
                             xT[p * 128:(p + 1) * 128, :],
                             inc=sXq[p % 3], boost=1e7))

    z_am = pr.op("vector", lambda e: e.memset(amax_c[:, :], 0.0), inc=sV)

    w_cast = {}
    _bsts = [4e6, 3e6, 2e6, 1e6]
    for _j, (csrc, cdst) in enumerate([(wq_t, wq_bf), (wk_t, wk_bf),
                                       (wv_t, wv_bf), (wg_t, wg_bf)]):
        w_cast[cdst.name] = pr.dma("gpsimd", cdst[:, :], csrc[:, :],
                                   inc=sG, boost=_bsts[_j])
    cast_jobs = []
    for r in range(DVT // 128):
        cast_jobs.append((woT, None, r, 0, r))

    scr = [SCR_F, VT]
    sC = [sCF, sCG]
    sO2 = [sOF, sOG]
    scr_tok = [None, None]
    sb_tok = [None, None]
    cast_last = [None, None]
    for j, (src, dst, r, c0, gcol) in enumerate(cast_jobs):
        s2 = j % 2
        lt = pr.dma("sync", scr[s2][:, :],
                    src[r * 128:(r + 1) * 128, c0:c0 + 1024],
                    waits=[scr_tok[s2]], inc=sC[s2])
        dd = SCR_B[:, s2 * 1024:(s2 + 1) * 1024]
        if gcol is None:
            ct = pr.op("vector",
                       lambda e, o=dd, i=scr[s2][:, :]: e.tensor_copy(o, i),
                       waits=[lt, sb_tok[s2]], inc=sV)
        else:
            ct = pr.op("scalar", lambda e, o=dd, i=scr[s2][:, :],
                       g=gnw_s[:, gcol:gcol + 1]:
                       e.activation(o, i, AF.Identity, bias=ZC(), scale=g),
                       waits=[lt, sb_tok[s2], misc_tok], inc=sA)
        if dst is None:
            dap = ap(wo_m, r * 128, [[DVT, 128], [128 * DVT, 8], [1, 128]])
            st = pr.op("scalar", lambda e, o=dap, i=dd:
                       e.dma_start(out=o, in_=i.rearrange(
                           "p (a b) -> p a b", a=8)),
                       waits=[ct], inc=sO2[s2], amt=16)
        else:
            st = pr.dma("scalar", dst[r * 128:(r + 1) * 128, c0:c0 + 1024],
                        dd, waits=[ct], inc=sO2[s2])
        scr_tok[s2] = ct
        sb_tok[s2] = st
        cast_last[s2] = st

    gsf = gsil.ap().bitcast(F32)
    t_sc = pr.op("scalar", lambda e: e.activation(silu_c[:, :], cT_s[:, :],
                                                  AF.Silu),
                 waits=[misc_tok], inc=sA)
    banks4 = [PB[0], PB[1], PO[0], PO[1]]
    bank4_tok = [None] * 4
    mod_st = []
    mm_prev = [None] * 4
    for rnd in range(3):
        mmtoks = [None] * 4
        for k in range(PC):
            lt = pr.dma("scalar", SCR_F[:, :],
                        wmod[k * 128:(k + 1) * 128,
                             rnd * 2048:rnd * 2048 + 1024],
                        waits=[scr_tok[0], mm_prev[0], mm_prev[1]], inc=sWM1)
            lt2 = pr.dma("scalar", VT[:, :],
                         wmod[k * 128:(k + 1) * 128,
                              rnd * 2048 + 1024:rnd * 2048 + 2048],
                         waits=[scr_tok[1], mm_prev[2], mm_prev[3]],
                         inc=sWM2)
            for b4 in range(4):
                src34 = SCR_F if b4 < 2 else VT
                mmtoks[b4] = pr.op(
                    "tensor",
                    lambda e, bk=banks4[b4], kk=k, bb=b4, sc3=src34:
                    e.matmul(bk[0:1, 0:512], silu_c[:, kk:kk + 1],
                             sc3[:, (bb % 2) * 512:(bb % 2 + 1) * 512],
                             start=(kk == 0), stop=(kk == PC - 1)),
                    waits=[lt if b4 < 2 else lt2, t_sc, bank4_tok[b4],
                           mm_prev[b4]], ns=600)
            for _b in range(4):
                mm_prev[_b] = mmtoks[_b]
        for b4 in range(4):
            seg = rnd * 4 + b4
            et = pr.op("scalar", lambda e, bk=banks4[b4]: e.activation(
                RB[0:1, 0:512], bk[0:1, 0:512], AF.Identity,
                bias=consts[0:1, 6:7], scale=consts[0:1, 7:8]),
                waits=[mmtoks[b4], misc_tok,
                       mod_st[-1] if mod_st else None], inc=sA)
            st = pr.dma("scalar", mod_dram[0:1, seg * 512:(seg + 1) * 512],
                        RB[0:1, 0:512], waits=[et], inc=sOmod)
            bank4_tok[b4] = et
            mod_st.append(st)
    zgE = pr.op("vector", lambda e: e.memset(gsf[:, 1024:1152], 0.0),
                inc=sV)
    ltE = pr.dma("sync", gsf[0:16, 1024:1152],
                 ap(mod_dram, 0, [[128, 16], [1, 128]]),
                 waits=mod_st[0:4] + [zgE], inc=sCG)
    idfE = pr.op("scalar", lambda e: e.activation(gsf[:, 1152:1280],
                                                  ident_s[:, :],
                                                  AF.Identity),
                 waits=[misc_tok], inc=sA)
    ttrE = pr.op("tensor", lambda e: e.transpose(PX[:, 256:384],
                                                 gsf[:, 1024:1152],
                                                 gsf[:, 1152:1280]),
                 waits=[ltE, idfE, zgE], inc=sT)
    tmvE = pr.op("scalar", lambda e: e.activation(modv[:, 0:16],
                                                  PX[:, 256:272],
                                                  AF.Identity),
                 waits=[ttrE], inc=sA)
    zg = pr.op("vector", lambda e: e.memset(VT[:, 0:128], 0.0),
               waits=[scr_tok[1], mm_prev[2], mm_prev[3]], inc=sV)
    lt = pr.dma("sync", VT[0:32, 0:128],
                ap(mod_dram, 2048, [[128, 32], [1, 128]]),
                waits=mod_st + [zg], inc=sCG)
    idf = pr.op("scalar", lambda e: e.activation(SCR_F[:, 0:128],
                                                 ident_s[:, :], AF.Identity),
                waits=[misc_tok, scr_tok[0], mm_prev[0], mm_prev[1]], inc=sA)
    ttr = pr.op("tensor", lambda e: e.transpose(PX[:, 0:128], VT[:, 0:128],
                                                SCR_F[:, 0:128]),
                waits=[lt, idf, zg, tmvE], inc=sT)
    tmv = pr.op("scalar", lambda e: e.activation(modv[:, 16:48],
                                                 PX[:, 0:32],
                                                 AF.Identity),
                waits=[ttr], inc=sA)
    scr_tok = [tmv, tmv]

    t_oca = pr.op("vector", lambda e: e.tensor_scalar_add(one_ca[:, :],
                                                          modv[:, 8:16], 1.0),
                  waits=[tmvE], inc=sV)
    t_ocb = pr.op("vector", lambda e: e.tensor_scalar_add(one_cb[:, :],
                                                          modv[:, 32:40], 1.0),
                  waits=[tmv], inc=sV)

    def sign_clamp(dstv, col0, w0):
        a = pr.op("scalar", lambda e: e.activation(SCR_F[:, 0:PC],
                                                   modv[:, col0:col0 + PC],
                                                   AF.Abs),
                  waits=[tmv, w0], inc=sA)
        b = pr.op("vector", lambda e: e.tensor_scalar_max(
            SCR_F[:, 0:PC], SCR_F[:, 0:PC], 0.25), waits=[a], inc=sV)
        c1 = pr.op("vector", lambda e: e.tensor_single_scalar(
            VT[:, 0:PC], modv[:, col0:col0 + PC], 0.0, ALU.is_ge),
            waits=[tmv, w0], inc=sV)
        c2 = pr.op("vector", lambda e: e.tensor_scalar(
            VT[:, 0:PC], VT[:, 0:PC], 2.0, -1.0, ALU.mult, ALU.add),
            waits=[c1], inc=sV)
        return pr.op("vector", lambda e: e.tensor_mul(
            dstv[:, :], SCR_F[:, 0:PC], VT[:, 0:PC]),
            waits=[b, c2], inc=sV)

    t_ga = sign_clamp(ga2, 16, scr_tok[0])
    t_gb = sign_clamp(gb2, 40, t_ga)
    scr_tok = [t_gb, t_gb]

    wfa = z_am
    for k in range(PC):
        lt = pr.dma("sync", SCR_F[:, :], wf_t[k * 128:(k + 1) * 128, :],
                    waits=[scr_tok[0] if k == 0 else wfa], inc=sCF, boost=-3e6)
        rt = pr.op("vector", lambda e: e.reduce_max(
            red_c[:, :], SCR_F[:, :], axis=AX.X, apply_absolute_value=True),
            waits=[lt, z_am], inc=sV, boost=-3e6)
        wfa = pr.op("vector", lambda e: e.tensor_max(
            amax_c[:, :], amax_c[:, :], red_c[:, :]), waits=[rt], inc=sV, boost=-3e6)
    d1 = pr.dma("scalar", wrow[0, :], amax_c[:, 0], waits=[wfa], inc=sB1, boost=-3e6)
    d2 = pr.dma("sync", RB[0:1, 0:128], wrow[:, :],
                waits=[d1, mod_st[-1]], inc=sB2, boost=-3e6)
    r1 = pr.op("vector", lambda e: e.reduce_max(small[0:1, 0:1],
                                                RB[0:1, 0:128], axis=AX.X),
               waits=[d2], inc=sV, boost=-3e6)
    r2 = pr.op("vector", lambda e: e.tensor_scalar_max(
        small[0:1, 1:2], small[0:1, 0:1], 1e-12), waits=[r1], inc=sV, boost=-3e6)
    r3 = pr.op("vector", lambda e: e.reciprocal(small[0:1, 2:3],
                                                small[0:1, 1:2]),
               waits=[r2], inc=sV, boost=-3e6)
    r4 = pr.op("vector", lambda e: e.tensor_scalar(
        small[0:1, 3:4], small[0:1, 2:3], FP8_MAX, FP8_MAX, ALU.mult,
        ALU.min), waits=[r3], inc=sV, boost=-3e6)
    r5 = pr.op("vector", lambda e: e.reciprocal(small[0:1, 4:5],
                                                small[0:1, 3:4]),
               waits=[r4], inc=sV, boost=-3e6)
    d3 = pr.dma("scalar", srow[0:1, 0:2], small[0:1, 3:5], waits=[r5],
                inc=sB1, boost=-3e6)
    d4 = pr.dma("sync", small[:, 5:6], ap(srow, 0, [[0, 128], [1, 1]]),
                waits=[d3], inc=sB2, boost=-3e6)
    wqp = None
    rbtok = d2
    for k in range(PC):
        lt = pr.dma("sync", SCR_F[:, :], wf_t[k * 128:(k + 1) * 128, :],
                    waits=[wqp, d4], inc=sCF, boost=-3e6)
        y = pr.op("vector", lambda e: e.tensor_scalar_mul(
            VT[:, :], SCR_F[:, :], small[:, 5:6]),
            waits=[lt, scr_tok[1] if k == 0 else None, d4], inc=sV, boost=-3e6)
        tt = pr.op("vector", lambda e: e.tensor_scalar_mul(
            SCR_F[:, :], VT[:, :], VELT_C), waits=[y], inc=sV, boost=-3e6)
        v1 = pr.op("vector", lambda e: e.tensor_sub(
            RB[:, 0:512], SCR_F[:, 0:512], VT[:, 0:512]),
            waits=[tt, rbtok], inc=sV, boost=-3e6)
        h1 = pr.op("vector", lambda e: e.tensor_sub(
            SCR_B[:, 0:512], SCR_F[:, 0:512], RB[:, 0:512]),
            waits=[v1, sb_tok[0]], inc=sV, boost=-3e6)
        v2 = pr.op("vector", lambda e: e.tensor_sub(
            RB[:, 0:512], SCR_F[:, 512:1024], VT[:, 512:1024]),
            waits=[h1], inc=sV, boost=-3e6)
        h2_ = pr.op("vector", lambda e: e.tensor_sub(
            SCR_B[:, 512:1024], SCR_F[:, 512:1024], RB[:, 0:512]),
            waits=[v2], inc=sV, boost=-3e6)
        st = pr.dma("scalar", wf_q[k * 128:(k + 1) * 128, :],
                    SCR_B[:, 0:1024], waits=[h2_], inc=sOF, boost=-3e6)
        wqp = st
        rbtok = h2_
        sb_tok[0] = st
    wfq_done = wqp

    z_S = pr.op("vector", lambda e: e.memset(S[:, :], 0.0),
                waits=[sb_tok[0], sb_tok[1], cast_last[1]], inc=sV)
    z_am2 = pr.op("vector", lambda e: e.memset(amax_c[:, :], 0.0),
                  waits=[d1], inc=sV)

    NCH = T // CH
    og2f = og2.ap().bitcast(F32)
    prol_rows = {}
    prol_store = {}
    stage_tok = [None, None]
    bank_chain = [None, None]
    for cc in range(NCH):
        sb2 = cc % 2
        stb = og2 if sb2 == 0 else gsil
        stf = og2f if sb2 == 0 else gsf
        bank = (cc // 2) % 2
        PXB = PXS[bank]
        s0 = (cc % 2) * 256
        xapc = ap(x_res, cc * CH, [[PC * T, 128], [T, PC], [1, CH]])
        cst = pr.op("scalar", lambda e, i=xapc, o=stb: e.activation(
            ap(o, 0, [[o.shape[1], 128], [CH, PC], [1, CH]]), i,
            AF.Identity), waits=xloads + [stage_tok[sb2]], inc=sA, ns=900)
        sq = pr.op("vector", lambda e, i=xapc, o=stb: e.tensor_tensor(
            ap(o, 1024, [[o.shape[1], 128], [CH, PC], [1, CH]]), i, i,
            ALU.mult), waits=xloads + [stage_tok[sb2]], inc=sV, ns=1250)
        mm1 = mm2 = None
        rn1 = pr.new_run()
        for k in range(PC):
            mm1 = pr.op("tensor", lambda e, kk=k, o=stb, b=PXB, s=s0:
                        e.matmul(b[:, s:s + 128], ones_s[:, :],
                                 o[:, kk * 128:(kk + 1) * 128],
                                 start=(kk == 0), stop=(kk == PC - 1)),
                        waits=[cst, misc_tok, bank_chain[bank], mm1],
                        ns=230, run=rn1)
        rn2 = pr.new_run()
        for k in range(PC):
            mm2 = pr.op("tensor", lambda e, kk=k, o=stb, b=PXB, s=s0:
                        e.matmul(b[:, s + 128:s + 256], ones_s[:, :],
                                 o[:, 1024 + kk * 128:1024 + (kk + 1) * 128],
                                 start=(kk == 0), stop=(kk == PC - 1)),
                        waits=[sq, mm1, mm2], ns=230, run=rn2)
        rm = pr.op("scalar", lambda e, b=PXB, s=s0: e.activation(
            b[:, s:s + 128], b[:, s:s + 128], AF.Identity, bias=ZC(),
            scale=consts[:, 1:2]), waits=[mm2], inc=sA, ns=420)
        rq = pr.op("scalar", lambda e, b=PXB, s=s0: e.activation(
            b[:, s + 128:s + 256], b[:, s + 128:s + 256], AF.Identity,
            bias=ZC(), scale=consts[:, 1:2]), waits=[mm2, rm], inc=sA,
            ns=420)
        t1 = pr.op("scalar", lambda e, b=PXB, s=s0, f=stf: e.activation(
            f[:, 0:128], b[:, s:s + 128], AF.Square),
            waits=[rm, mm2], inc=sA, ns=420)
        a2 = pr.op("vector", lambda e, b=PXB, s=s0, f=stf: e.tensor_sub(
            f[:, 0:128], b[:, s + 128:s + 256], f[:, 0:128]),
            waits=[t1, rq], inc=sV, ns=200)
        a3 = pr.op("scalar", lambda e, f=stf: e.activation(
            f[:, 0:128], f[:, 0:128], AF.Sqrt, bias=consts[:, 0:1],
            scale=OC()), waits=[a2], inc=sA, ns=520)
        a4 = pr.op("vector", lambda e, b=PXB, s=s0, f=stf: e.reciprocal(
            b[:, s + 128:s + 256], f[:, 0:128]), waits=[a3], inc=sV,
            ns=900)
        cp = pr.op("scalar", lambda e, b=PXB, s=s0, f=stf: e.activation(
            f[:, 128:384], b[:, s:s + 256], AF.Identity),
            waits=[a4, mm2], inc=sA, ns=450)
        st = pr.dma("scalar", ln1_dram[:, cc * 256:(cc + 1) * 256],
                    stf[:, 128:384], waits=[cp], inc=sPO)
        prol_rows[cc] = [rm, a4]
        prol_store[cc] = st
        stage_tok[sb2] = [mm2, a4, st]
        bank_chain[bank] = cp

    PB4 = PB + PXS
    pb_free = [bank4_tok[0], bank4_tok[1],
               [bank_chain[0]], [bank_chain[1]]]
    po_free = [bank4_tok[2], bank4_tok[3]]
    px_chain = [tmv, tmvE]
    ptr_chain = [None]
    pass_i = [0]
    pool_pos = [0]

    tok = {
        "xh": None, "xh2": None, "qt_w": None, "kt_w": None,
        "qt_r": None, "kt_r": None,
        "v_w": [None, None], "v_r": [None, None], "RBr": rbtok,
        "gsil_w": None, "gsil_r": [stage_tok[1], ttrE],
        "og2_w": None, "og2_r": [stage_tok[0]],
        "S": z_S, "Sbf_w": None, "Sbf_r": None,
        "ktm_w": None, "ktm_r": None, "sco_w": None, "sco_r": None,
        "ropeB": None, "tab": None, "RB": rbtok,
        "RBv": rbtok, "RBs": rbtok,
        "SCRF": z_S, "SB": [z_S, z_S, z_S, z_S], "VT": z_S, "VT2": z_S,
        "RBst": rbtok, "redc": d1,
        "slab": [cast_last[0], cast_last[1]] * 3, "xmid": {},
    }
    f1_done = []
    stat_st = {}
    res_toks = {}
    out_tok = [None]

    def ln_stats(xap, waits_read):
        xb = pr.op("scalar", lambda e, i=xap: e.activation(
            ap(SCR_B, 0, [[2048, 128], [CH, PC], [1, CH]]), i, AF.Identity),
            waits=waits_read + [tok["SB"][0], tok["SB"][1]], inc=sA, ns=900)
        x2 = pr.op("vector", lambda e, i=xap: e.tensor_tensor(
            ap(SCR_B, 1024, [[2048, 128], [CH, PC], [1, CH]]), i, i,
            ALU.mult),
            waits=waits_read + [tok["SB"][2], tok["SB"][3], xb], inc=sV,
            ns=1250)
        slA = pass_i[0] % 4
        pass_i[0] += 1
        slB = pass_i[0] % 4
        pass_i[0] += 1
        mm1 = mm2 = None
        rn1 = pr.new_run()
        for k in range(PC):
            mm1 = pr.op("tensor", lambda e, kk=k, s2=slA: e.matmul(
                PB4[s2][:, 0:128], ones_s[:, :],
                SCR_B[:, kk * 128:(kk + 1) * 128],
                start=(kk == 0), stop=(kk == PC - 1)),
                waits=[xb, misc_tok, pb_free[slA], mm1], ns=230, run=rn1)
        rn2 = pr.new_run()
        for k in range(PC):
            mm2 = pr.op("tensor", lambda e, kk=k, s2=slB: e.matmul(
                PB4[s2][:, 0:128], ones_s[:, :],
                SCR_B[:, 1024 + kk * 128:1024 + (kk + 1) * 128],
                start=(kk == 0), stop=(kk == PC - 1)),
                waits=[x2, pb_free[slB], mm2], ns=230, run=rn2)
        rm = pr.op("scalar", lambda e, s2=slA: e.activation(
            RB[:, 0:128], PB4[s2][:, 0:128], AF.Identity, bias=ZC(),
            scale=consts[:, 1:2]), waits=[mm1, tok["RB"], tok["RBst"]],
            inc=sA, ns=420)
        rq = pr.op("scalar", lambda e, s2=slB: e.activation(
            RB[:, 128:256], PB4[s2][:, 0:128], AF.Identity, bias=ZC(),
            scale=consts[:, 1:2]), waits=[mm2, rm], inc=sA, ns=420)
        pb_free[slA] = rm
        pb_free[slB] = rq
        a1 = pr.op("vector", lambda e: e.tensor_mul(
            RB[:, 256:384], RB[:, 0:128], RB[:, 0:128]),
            waits=[rm, tok["RBv"]], inc=sV, ns=200)
        a2 = pr.op("vector", lambda e: e.tensor_sub(
            RB[:, 256:384], RB[:, 128:256], RB[:, 256:384]),
            waits=[a1, rq], inc=sV, ns=200)
        a3 = pr.op("scalar", lambda e: e.activation(
            RB[:, 256:384], RB[:, 256:384], AF.Sqrt, bias=consts[:, 0:1],
            scale=OC()), waits=[a2], inc=sA, ns=520)
        a4 = pr.op("vector", lambda e: e.reciprocal(
            RB[:, 128:256], RB[:, 256:384]), waits=[a3], inc=sV, ns=900)
        tok["RB"] = a4
        tok["RBv"] = a4
        tok["SB"][0] = mm1
        tok["SB"][1] = mm1
        tok["SB"][2] = mm2
        tok["SB"][3] = mm2
        return a4

    def ln_norm(xap, rows_tok):
        mb = ap(RB, 0, [[512, 128], [0, PC], [1, CH]])
        rb_ = ap(RB, 128, [[512, 128], [0, PC], [1, CH]])
        tt0 = pr.op("vector", lambda e, i=xap: e.tensor_tensor(
            ap(SCR_F, 0, [[1024, 128], [CH, PC], [1, CH]]), i, mb,
            ALU.subtract), waits=[rows_tok, tok["SCRF"]], inc=sV, ns=1250)
        tt1 = pr.op("vector", lambda e: e.tensor_tensor(
            ap(SCR_F, 0, [[1024, 128], [CH, PC], [1, CH]]),
            ap(SCR_F, 0, [[1024, 128], [CH, PC], [1, CH]]), rb_, ALU.mult),
            waits=[tt0, rows_tok], inc=sV, ns=1250)
        tok["SCRF"] = tt1
        tok["RB"] = tt1
        return tt1

    def tile_pass(src_bf, m, kcount, rhs_fn, xtra, nw=GR, split=False,
                  xtra_mid=None):
        slot = pass_i[0] % 4
        pass_i[0] += 1
        wide = kcount * 128
        nwin = wide // 1024
        if nwin == 2 and pool_pos[0] % 2 == 1:
            pool_pos[0] += 1
        w0_ = pool_pos[0] % 6
        pool_pos[0] += nwin
        sl = slabP.ap()[:, w0_ * 1024: w0_ * 1024 + wide]
        wt = [tok["slab"][(w0_ + j) % 6] for j in range(nwin)]
        wt = wt + [w_cast.get(src_bf.name)]
        if split and nwin == 2:
            lt = pr.dma("sync", sl[:, 0:1024],
                        src_bf[m * 128:(m + 1) * 128, 0:1024],
                        waits=wt, inc=sSlab[w0_ % 3])
            lt2 = pr.dma("scalar", sl[:, 1024:2048],
                         src_bf[m * 128:(m + 1) * 128, 1024:2048],
                         waits=wt, inc=sSlabS[(w0_ + 1) % 3])
        else:
            lt = pr.dma("sync" if w0_ % 2 == 0 else "scalar",
                        sl[:, 0:wide],
                        src_bf[m * 128:(m + 1) * 128, 0:wide],
                        waits=wt,
                        inc=(sSlab if w0_ % 2 == 0 else sSlabX)[w0_ % 3])
            lt2 = lt
        mmt = None
        rn = pr.new_run()
        for k in range(kcount):
            mmt = pr.op("tensor", lambda e, kk=k, s=sl, sl2=slot, w=nw:
                        e.matmul(PB4[sl2][:, 0:w],
                                 s[:, kk * 128:(kk + 1) * 128], rhs_fn(kk),
                                 start=(kk == 0), stop=(kk == kcount - 1)),
                        waits=[lt if k * 128 < 1024 else lt2,
                               pb_free[slot], mmt] +
                        (xtra if k == 0 else []) +
                        (xtra_mid if (xtra_mid is not None and
                                      k == kcount // 2) else []),
                        ns=(330 if nw == GR else 230), run=rn)
        for j in range(nwin):
            tok["slab"][(w0_ + j) % 6] = [mmt, lt2]
        return slot, mmt

    def v_pass_gen(c0, xh_tok, xb2):
        for m in range(PC):
            if pool_pos[0] % 2 == 1:
                pool_pos[0] += 1
            w0_ = pool_pos[0] % 6
            pool_pos[0] += 2
            sl = slabP.ap()[:, w0_ * 1024: w0_ * 1024 + 2048]
            wt = [tok["slab"][w0_], tok["slab"][(w0_ + 1) % 6],
                  w_cast[wv_bf.name]]
            lt = pr.dma("sync", sl[:, 0:1024],
                        wv_bf[m * 128:(m + 1) * 128, 0:1024],
                        waits=wt, inc=sSlab[w0_ % 3])
            lt2 = pr.dma("scalar", sl[:, 1024:2048],
                         wv_bf[m * 128:(m + 1) * 128, 1024:2048],
                         waits=wt, inc=sSlabS[(w0_ + 1) % 3])
            mm_both = []
            for ci in range(2):
                cc = c0 + ci
                voff = (cc % 2) * DVT
                slot = pass_i[0] % 4
                pass_i[0] += 1
                mmt = None
                rn = pr.new_run()
                for k in range(PC):
                    mmt = pr.op("tensor",
                                lambda e, kk=k, cci=ci, s=sl, sl2=slot,
                                x2_=xb2:
                                e.matmul(PB4[sl2][:, 0:256],
                                         x2_[:, kk * GR + cci * CH:
                                             kk * GR + cci * CH + CH],
                                         s[:, kk * 256:(kk + 1) * 256],
                                         start=(kk == 0),
                                         stop=(kk == PC - 1)),
                                waits=[lt if k < 4 else lt2, xh_tok,
                                       pb_free[slot], mmt],
                                ns=330, run=rn)
                ev = pr.op("scalar", lambda e, mm=m, sl2=slot, vo=voff:
                           e.activation(
                               v_tm[:, vo + mm * 256: vo + (mm + 1) * 256],
                               PB4[sl2][:, 0:256], AF.Identity),
                           waits=[mmt, tok["v_r"][cc % 2]], inc=sA, ns=480)
                pb_free[slot] = ev
                tok["v_w"][cc % 2] = ev
                mm_both.append(mmt)
            tok["slab"][w0_] = [mm_both, lt2]
            tok["slab"][(w0_ + 1) % 6] = [mm_both, lt2]
            yield

    def g_pass(m, xh_tok, xb2):
        slot, mmt = tile_pass(wg_bf, m, PC,
                              lambda k, x2_=xb2: x2_[:, k * GR:(k + 1) * GR],
                              [xh_tok])
        gl = pr.op("scalar", lambda e, mm=m, sl2=slot: e.activation(
            gsil[:, mm * GR:(mm + 1) * GR], PB4[sl2][:, 0:GR], AF.Silu),
            waits=[mmt, tok["gsil_r"]], inc=sA, ns=520)
        pb_free[slot] = gl
        return gl, mmt

    def wo_pass_gen(ci, cc):
        wo_mms = []
        xms = []
        for m in range(PC):
            slot, mmt = tile_pass(wo_m, m, DPC,
                                  lambda k: og2[:, k * 128:(k + 1) * 128],
                                  [tok["og2_w"][0:2]], nw=128, split=True,
                                  xtra_mid=[tok["og2_w"][2:4]])
            xm = pr.op("vector",
                       lambda e, mm=m, sl2=slot, c2=cc:
                       e.scalar_tensor_tensor(
                           x_res[:, mm * T + c2 * CH:
                                 mm * T + (c2 + 1) * CH],
                           PB4[sl2][:, 0:CH],
                           ga2[:, mm:mm + 1],
                           x_res[:, mm * T + c2 * CH:
                                 mm * T + (c2 + 1) * CH],
                           ALU.mult, ALU.add),
                       waits=[mmt, t_ga], inc=sV, ns=200)
            pb_free[slot] = xm
            wo_mms.append(mmt)
            xms.append(xm)
            tok["og2_r"] = wo_mms
            tok["xmid"][cc] = xms
            yield

    def f1_stats(cc):
        xap = ap(x_res, cc * CH, [[PC * T, 128], [T, PC], [1, CH]])
        rows_tok = ln_stats(xap, [tok["xmid"][cc]])
        st = pr.dma("scalar", stats_dram[:, cc * 256:(cc + 1) * 256],
                    RB[:, 0:256], waits=[rows_tok], inc=sStatO)
        tok["RBst"] = st
        stat_st[cc] = st
        return xap, rows_tok

    def f1_chain(cc, xap, rows_tok):
        tt1 = ln_norm(xap, rows_tok)
        hts = []
        for p in range(PC):
            hts.append(pr.op("vector", lambda e, pp=p: e.tensor_scalar(
                SCR_F[:, pp * CH:(pp + 1) * CH],
                SCR_F[:, pp * CH:(pp + 1) * CH],
                one_cb[:, pp:pp + 1], modv[:, 24 + pp:25 + pp],
                ALU.mult, ALU.add), waits=[tt1, t_ocb], inc=sV))
        rd = pr.op("vector", lambda e: e.reduce_max(
            red_c[:, :], SCR_F[:, :], axis=AX.X,
            apply_absolute_value=True), waits=hts + [tok["redc"]], inc=sV)
        am = pr.op("vector", lambda e: e.tensor_max(
            amax_c[:, :], amax_c[:, :], red_c[:, :]),
            waits=[rd, z_am2], inc=sV)
        tok["redc"] = am
        f1_done.append(am)
        tok["SCRF"] = rd

    def retention_chunk(ci, cc, fillers):
        voff = (cc % 2) * DVT
        og_last = None
        og_alls = []
        sco_readers = []
        head_info = []
        sbf_prev = tok["Sbf_w"]
        for hh in range(H):
            sl_s = pass_i[0] % 4
            pass_i[0] += 1
            mmt = None
            rn = pr.new_run()
            for b in range(2):
                pcq = 2 * hh + b
                mmt = pr.op("tensor",
                            lambda e, pq=pcq, bb=b, cci=ci, s2=sl_s:
                            e.matmul(PB4[s2][:, 0:128],
                                     kt[:, pq * GR + cci * CH:
                                        pq * GR + cci * CH + CH],
                                     qt[:, pq * GR + cci * CH:
                                        pq * GR + cci * CH + CH],
                                     start=(bb == 0), stop=(bb == 1)),
                            waits=[tok["qt_w"], tok["kt_w"], pb_free[sl_s],
                                   mmt], ns=230, run=rn)
            sct = pr.op("vector", lambda e, h2=hh, s2=sl_s:
                        e.scalar_tensor_tensor(
                            sco[:, h2 * 128:(h2 + 1) * 128],
                            PB4[s2][:, 0:128],
                            lamsig_s[:, h2:h2 + 1], mask_s[:, :],
                            ALU.mult, ALU.mult),
                        waits=[mmt, misc_tok, tok["sco_r"]], inc=sV, ns=220)
            pb_free[sl_s] = sct
            tok["sco_w"] = sct
            if fillers:
                fillers[min(hh, len(fillers) - 1)](hh)

            if cc > 0:
                sb_w = [tok["S"], sbf_prev, tok["Sbf_r"], tok["ropeB"]]
                sb1 = pr.op("vector", lambda e, h2=hh: e.tensor_copy(
                    S_bf[:, 0:512], S[:, h2 * 1024:h2 * 1024 + 512]),
                    waits=sb_w, inc=sV, ns=650)
                sb2 = pr.op("scalar", lambda e, h2=hh: e.activation(
                    S_bf[:, 512:1024], S[:, h2 * 1024 + 512:(h2 + 1) * 1024],
                    AF.Identity), waits=sb_w, inc=sA, ns=550)
                sbf_prev = [sb1, sb2]
                tok["Sbf_w"] = sbf_prev
                tok["ropeB"] = sbf_prev

            posl = hh % 2
            otok = None
            rn_o = pr.new_run()
            for b in range(4):
                w0 = [sct, tok["v_w"][cc % 2], po_free[posl], otok]
                if cc == 0:
                    otok = pr.op("tensor",
                                 lambda e, h2=hh, bb=b, pp=posl, vo=voff:
                                 e.matmul(
                                     PO[pp][:, bb * 128:(bb + 1) * 128],
                                     v_tm[:, vo + h2 * DV + bb * 128:
                                          vo + h2 * DV + (bb + 1) * 128],
                                     sco[:, h2 * 128:(h2 + 1) * 128],
                                     start=True, stop=True),
                                 waits=w0, ns=230, run=rn_o)
                else:
                    t_a = pr.op("tensor",
                                lambda e, h2=hh, bb=b, pp=posl, vo=voff:
                                e.matmul(PO[pp][:, bb * 128:(bb + 1) * 128],
                                         v_tm[:, vo + h2 * DV + bb * 128:
                                              vo + h2 * DV + (bb + 1) * 128],
                                         sco[:, h2 * 128:(h2 + 1) * 128],
                                         start=True, stop=False),
                                waits=w0 + [sbf_prev], ns=230, run=rn_o)
                    t_b = pr.op("tensor", lambda e, h2=hh, bb=b, pp=posl,
                                cci=ci:
                                e.matmul(PO[pp][:, bb * 128:(bb + 1) * 128],
                                         S_bf[:, bb * 128:(bb + 1) * 128],
                                         qt[:, (2 * h2) * GR + cci * CH:
                                            (2 * h2) * GR + cci * CH + CH],
                                         start=False, stop=False),
                                waits=[sbf_prev, tok["qt_w"], t_a], ns=230,
                                run=rn_o)
                    otok = pr.op("tensor", lambda e, h2=hh, bb=b,
                                 pp=posl, cci=ci:
                                 e.matmul(
                                     PO[pp][:, bb * 128:(bb + 1) * 128],
                                     S_bf[:, 512 + bb * 128:
                                          512 + (bb + 1) * 128],
                                     qt[:, (2 * h2 + 1) * GR + cci * CH:
                                        (2 * h2 + 1) * GR + cci * CH + CH],
                                     start=False, stop=True),
                                 waits=[t_b], ns=230, run=rn_o)
            tok["Sbf_r"] = otok
            sco_readers.append(otok)

            par = hh % 2
            osoff = 1024 + par * 512
            o2off = par * 512
            ssoff = 128 + par * 256
            oev = pr.op("scalar", lambda e, pp=posl, oo=osoff: e.activation(
                SCR_B[:, oo:oo + 512], PO[pp][:, 0:512], AF.Identity),
                waits=[otok, tok["SB"][2 + par]], inc=sA, ns=600)
            po_free[posl] = oev
            o2_ = pr.op("vector", lambda e, oo=osoff, so=o2off: e.tensor_mul(
                SCR_B[:, so:so + 512], SCR_B[:, oo:oo + 512],
                SCR_B[:, oo:oo + 512]), waits=[oev, tok["SB"][par]],
                inc=sV, ns=380)
            ssm = None
            rn_s = pr.new_run()
            for b in range(4):
                ssm = pr.op("tensor", lambda e, bb=b, so=o2off, sso=ssoff:
                            e.matmul(
                    PX[:, sso:sso + 128], ones_s[:, :],
                    SCR_B[:, so + bb * 128:so + (bb + 1) * 128],
                    start=(bb == 0), stop=(bb == 3)),
                    waits=[o2_, px_chain[0], ssm], ns=230, run=rn_s)
            px_chain[0] = ssm
            tok["SB"][par] = ssm
            head_info.append((hh, ssm, oev))
            if par == 1:
                (h0, ssm0, oev0), (h1, ssm1, oev1) = head_info[-2:]
                ssg = ap(PX, 128, [[512, 128], [256, 2], [1, 128]])
                rbg = ap(RB, 256, [[512, 128], [128, 2], [1, 128]])
                sdt2 = pr.op("scalar", lambda e, i=ssg, o=rbg: e.activation(
                    o, i, AF.Sqrt, bias=consts[:, 0:1],
                    scale=consts[:, 2:3]),
                    waits=[ssm0, ssm1, tok["RBv"], tok["RBs"]], inc=sA,
                    ns=560)
                rst2 = pr.op("vector", lambda e: e.reciprocal(
                    RB[:, 256:512], RB[:, 256:512]), waits=[sdt2], inc=sV,
                    ns=900)
                px_chain[0] = rst2
                for h3, oevx in ((h0, oev0), (h1, oev1)):
                    rboff = 256 + (h3 % 2) * 128
                    osx = 1024 + (h3 % 2) * 512
                    rstb = ap(RB, rboff, [[512, 128], [0, 4], [1, 128]])
                    gslice = ap(gsil, h3 * 4 * GR + ci * CH,
                                [[DPC * GR, 128], [GR, 4], [1, CH]])
                    oslice = ap(og2, h3 * 4 * 128,
                                [[DPC * 128, 128], [128, 4], [1, CH]])
                    g1t = pr.op("vector",
                                lambda e, o=oslice, gs=gslice, rb2=rstb:
                                e.tensor_tensor(o, gs, rb2, ALU.mult),
                                waits=[rst2, [tok["gsil_w"][4 * h3 + j]
                                              for j in range(4)],
                                       tok["og2_r"]], inc=sV, ns=620)
                    og_last = pr.op("vector", lambda e, o=oslice, oo=osx:
                                    e.tensor_tensor(
                                        o, o,
                                        ap(SCR_B, oo,
                                           [[2048, 128], [128, 4], [1, 128]]),
                                        ALU.mult),
                                    waits=[g1t, oevx], inc=sV, ns=620)
                    tok["SB"][2 + h3 % 2] = og_last
                    if h3 % 2 == 0:
                        tok["RBv"] = og_last
                    else:
                        tok["RBs"] = og_last
                    tok["gsil_r"] = og_last
                    og_alls.append(og_last)
        tok["og2_w"] = og_alls
        tok["sco_r"] = sco_readers

    def u_fold_chunk(ci, cc, ktm_last, wo_gen):
        voff = (cc % 2) * DVT
        last_umt = None
        umts = []
        for hh in range(H):
            for b in range(2):
                posl = (hh * 2 + b) % 2
                umt = pr.op("tensor",
                            lambda e, h2=hh, bb=b, pp=posl, vo=voff:
                            e.matmul(PO[pp][:, 0:512],
                                     k_tm[:, h2 * DK + bb * 128:
                                          h2 * DK + (bb + 1) * 128],
                                     v_tm[:, vo + h2 * DV:
                                          vo + (h2 + 1) * DV],
                                     start=True, stop=True),
                            waits=[ktm_last, tok["v_w"][cc % 2],
                                   po_free[posl]], ns=590)
                lc = float(lamh(hh) ** CH)
                fld = pr.op("vector", lambda e, h2=hh, bb=b, lcv=lc,
                            pp=posl: e.scalar_tensor_tensor(
                                S[:, (2 * h2 + bb) * 512:
                                  (2 * h2 + bb + 1) * 512],
                                S[:, (2 * h2 + bb) * 512:
                                  (2 * h2 + bb + 1) * 512],
                                lcv, PO[pp][:, 0:512],
                                ALU.mult, ALU.add),
                            waits=[umt, tok["S"], tok["Sbf_w"]], inc=sV,
                            ns=680)
                po_free[posl] = fld
                tok["S"] = fld
                last_umt = umt
                umts.append(umt)
                if wo_gen is not None:
                    next(wo_gen, None)
        tok["v_r"][cc % 2] = umts
        tok["ktm_r"] = umts
        tok["ropeB"] = tok["S"]
        return last_umt

    def transposes(ci, cc):
        tts = []
        tt_last = None
        rn_t = pr.new_run()
        rn_k = pr.new_run()
        for s8 in range(8):
            tt_last = pr.op("tensor", lambda e, pq=s8, cci=ci:
                            e.transpose(PTR[:, pq * 128:(pq + 1) * 128],
                                        kt[:, pq * GR + cci * CH:
                                           pq * GR + cci * CH + CH],
                                        ident_s[:, :]),
                            waits=[tok["kt_w"], misc_tok, ptr_chain[0],
                                   tt_last], ns=300, run=rn_t)
            tts.append(tt_last)
        ktm_last = None
        for s8 in range(8):
            hh2, bb2 = s8 // 2, s8 % 2
            ktm_last = pr.op("scalar", lambda e, pq=s8, h2=hh2, bb=bb2:
                             e.activation(
                                 k_tm[:, h2 * DK + bb * 128:
                                      h2 * DK + (bb + 1) * 128],
                                 PTR[:, pq * 128:(pq + 1) * 128],
                                 AF.Identity, bias=ZC(),
                                 scale=lamk_s[:, h2:h2 + 1]),
                             waits=[tts[7], misc_tok, tok["ktm_r"],
                                    ktm_last], ns=420, run=rn_k)
        ptr_chain[0] = ktm_last
        return ktm_last

    ln1_xh = {}
    vtn_tok = [z_S]
    pxs_free = {}
    for _b in range(2):
        for _ci in range(2):
            pxs_free[(_b, _ci)] = [bank_chain[_b]]

    def ln1_group(gn):
        bank = gn % 2
        xh_b = xh if gn % 2 == 0 else xh2
        xh_all = []
        for ci in range(2):
            cc = 2 * gn + ci
            ld = pr.dma("sync", RB[:, 0:256],
                        ln1_dram[:, cc * 256:(cc + 1) * 256],
                        waits=[prol_store[cc], tok["RB"], tok["RBst"]],
                        inc=sPI)
            rows_tok = [ld]
            mb = ap(RB, 0, [[512, 128], [0, PC], [1, CH]])
            rb_ = ap(RB, 128, [[512, 128], [0, PC], [1, CH]])
            xapc = ap(x_res, cc * CH, [[PC * T, 128], [T, PC], [1, CH]])
            w_extra = [tok["v_r"][0], tok["og2_w"]] if gn >= 1 else []
            tt0 = pr.op("vector", lambda e, i=xapc, m=mb: e.tensor_tensor(
                VT[:, :].rearrange("p (a b) -> p a b", a=PC), i, m,
                ALU.subtract),
                waits=rows_tok + vtn_tok + w_extra + xloads, inc=sV,
                ns=1250)
            tt1 = pr.op("vector", lambda e, r=rb_: e.tensor_tensor(
                VT[:, :].rearrange("p (a b) -> p a b", a=PC),
                VT[:, :].rearrange("p (a b) -> p a b", a=PC), r, ALU.mult),
                waits=[tt0] + rows_tok, inc=sV, ns=1250)
            tok["RB"] = [tt0, tt1]
            rn_x = pr.new_run()
            xh_ci = []
            for p in range(PC):
                xh_ci.append(pr.op("scalar",
                             lambda e, pp=p, cci=ci, xb2=xh_b:
                             e.activation(
                    xb2[:, pp * GR + cci * CH: pp * GR + cci * CH + CH],
                    VT[:, pp * CH:(pp + 1) * CH], AF.Identity,
                    bias=modv[:, pp:pp + 1], scale=one_ca[:, pp:pp + 1]),
                    waits=[tt1, t_oca, tok["xh2" if gn % 2 else "xh"]],
                    inc=sA, ns=420, run=rn_x))
            xh_all.extend(xh_ci)
            vtn_tok[:] = [xh_ci]
        ln1_xh[gn] = xh_all

    ln1_group(0)

    for g in range(NG):
        t0g = g * GR
        c0 = 2 * g
        tb1 = pr.dma("sync", cos_sl[:, :], cosT[:, t0g:t0g + GR],
                     waits=[tok["tab"]], inc=sTab)
        tb2 = pr.dma("sync", sin_sl[:, :], sinT[:, t0g:t0g + GR],
                     waits=[tok["tab"]], inc=sTab)
        ttab = [tb1, tb2]

        xh_b = xh if g % 2 == 0 else xh2
        xh_tok = ln1_xh[g]

        def proj_rope(wsrc, dst, dst_free, do_lam):
            outs = []
            pend = {}
            for m in range(PC):
                slot, mmt = tile_pass(wsrc, m, PC,
                                      lambda k, x2_=xh_b:
                                      x2_[:, k * GR:(k + 1) * GR],
                                      [xh_tok])
                pend[m] = (slot, mmt)
                if m % 2 == 0:
                    continue
                hh = m // 2
                sA_, mmA = pend[m - 1]
                sB_, mmB = pend[m]
                o1 = dst[:, (2 * hh) * GR:(2 * hh + 1) * GR]
                o2 = dst[:, (2 * hh + 1) * GR:(2 * hh + 2) * GR]
                rB1 = ropeB[:, 0:GR]
                rB2 = ropeB[:, GR:2 * GR]
                p1 = pr.op("vector", lambda e, pa=sA_, o=o1: e.tensor_tensor(
                    o, PB4[pa][:, 0:GR], cos_sl[:, :], ALU.mult),
                    waits=[mmA] + ttab + [dst_free], inc=sV, ns=300)
                p3 = pr.op("vector", lambda e, pa=sA_, o=o2: e.tensor_tensor(
                    o, PB4[pa][:, 0:GR], sin_sl[:, :], ALU.mult),
                    waits=[mmA] + ttab + [dst_free, p1], inc=sV, ns=300)
                q1 = pr.op("vector", lambda e, pb=sB_, r=rB1: e.tensor_tensor(
                    r, PB4[pb][:, 0:GR], sin_sl[:, :], ALU.mult),
                    waits=[mmB] + ttab + [tok["ropeB"]], inc=sV, ns=300)
                q2 = pr.op("vector", lambda e, pb=sB_, r=rB2: e.tensor_tensor(
                    r, PB4[pb][:, 0:GR], cos_sl[:, :], ALU.mult),
                    waits=[mmB] + ttab + [tok["ropeB"], q1], inc=sV, ns=300)
                p2 = pr.op("vector", lambda e, o=o1, r=rB1: e.tensor_tensor(
                    o, o, r, ALU.subtract), waits=[p1, q1], inc=sV, ns=300)
                p4 = pr.op("vector", lambda e, o=o2, r=rB2: e.tensor_tensor(
                    o, o, r, ALU.add), waits=[p3, q2], inc=sV, ns=300)
                pb_free[sA_] = [p1, p3]
                pb_free[sB_] = [q1, q2]
                tok["ropeB"] = [p2, p4]
                outs += [p2, p4]
            if do_lam:
                qall = ap(dst, 0, [[PC * GR, 128], [GR, PC],
                                   [CH, GR // CH], [1, CH]])
                lm = ap(lamq_s, 0, [[1024, 128], [128, PC],
                                    [0, GR // CH], [1, CH]])
                outs = [pr.op("vector", lambda e: e.tensor_tensor(
                    qall, qall, lm, ALU.mult),
                    waits=outs + [misc_tok], inc=sV, ns=800)]
            return outs

        tok["qt_w"] = proj_rope(wq_bf, qt, tok["qt_r"], True)
        tok["kt_w"] = proj_rope(wk_bf, kt, tok["kt_r"], False)

        tok["gsil_w"] = {}
        for m in range(4):
            gl, mmt = g_pass(m, xh_tok, xh_b)
            tok["gsil_w"][m] = gl
            xh_free = mmt
        g_next = [4]

        for _ in v_pass_gen(c0, xh_tok, xh_b):
            pass

        ktm_l = transposes(0, c0)

        def fill_c0(hh):
            while g_next[0] < min(4 * hh + 8, DPC):
                gl, _ = g_pass(g_next[0], xh_tok, xh_b)
                tok["gsil_w"][g_next[0]] = gl
                g_next[0] += 1

        retention_chunk(0, c0, [fill_c0] * 4)
        while g_next[0] < DPC:
            gl, _ = g_pass(g_next[0], xh_tok, xh_b)
            tok["gsil_w"][g_next[0]] = gl
            g_next[0] += 1
        wo0 = wo_pass_gen(0, c0)
        u_fold_chunk(0, c0, ktm_l, wo0)
        for _ in wo0:
            pass
        if g + 1 < NG:
            ln1_group(g + 1)
        xap0, rows0 = f1_stats(c0)

        ktm_l = transposes(1, c0 + 1)
        f1_emitted = [False]

        def fill_c1(hh):
            if not f1_emitted[0]:
                f1_emitted[0] = True
                f1_chain(c0, xap0, rows0)

        retention_chunk(1, c0 + 1, [fill_c1] * 4)
        wo1 = wo_pass_gen(1, c0 + 1)
        u_fold_chunk(1, c0 + 1, ktm_l, wo1)
        for _ in wo1:
            pass
        xap1, rows1 = f1_stats(c0 + 1)
        f1_chain(c0 + 1, xap1, rows1)

        tok["tab"] = tok["kt_w"]
        tok["qt_r"] = tok["v_r"][1]
        tok["kt_r"] = tok["v_r"][1]
        tok["xh"] = tok["v_r"][1]
        tok["xh2"] = tok["v_r"][1]

    e1 = pr.dma("scalar", cc_in[0, :], amax_c[:, 0], waits=f1_done, inc=sB1)
    cctok = pr.op("gpsimd", lambda e: e.collective_compute(
        "AllReduce", ALU.max, replica_groups=[list(range(n_cores))],
        ins=[cc_in.ap()], outs=[cc_out.ap()]), waits=[e1], inc=sCC,
        async_op=True)
    e2 = pr.dma("sync", RB[0:1, 256:384], cc_out[:, :],
                waits=[cctok, tok["RBv"], tok["RBs"]], inc=sB2)
    q1 = pr.op("vector", lambda e: e.reduce_max(
        small[0:1, 8:9], RB[0:1, 256:384], axis=AX.X), waits=[e2], inc=sV)
    q2 = pr.op("vector", lambda e: e.tensor_scalar_max(
        small[0:1, 9:10], small[0:1, 8:9], 1e-12), waits=[q1], inc=sV)
    q3 = pr.op("vector", lambda e: e.reciprocal(
        small[0:1, 10:11], small[0:1, 9:10]), waits=[q2], inc=sV)
    q4 = pr.op("vector", lambda e: e.tensor_scalar(
        small[0:1, 11:12], small[0:1, 10:11], FP8_MAX, FP8_MAX,
        ALU.mult, ALU.min), waits=[q3], inc=sV)
    q5 = pr.op("vector", lambda e: e.reciprocal(
        small[0:1, 12:13], small[0:1, 11:12]), waits=[q4], inc=sV)
    q6 = pr.op("vector", lambda e: e.tensor_mul(
        small[0:1, 13:14], small[0:1, 12:13], small[0:1, 4:5]),
        waits=[q5], inc=sV)
    e3 = pr.dma("scalar", srow[0:1, 2:3], small[0:1, 11:12], waits=[q4],
                inc=sB1)
    e4 = pr.dma("scalar", srow[0:1, 4:5], small[0:1, 13:14], waits=[q6],
                inc=sB1)
    e5 = pr.dma("sync", small[:, 14:15], ap(srow, 2, [[0, 128], [1, 1]]),
                waits=[e3], inc=sB2)
    e6 = pr.dma("sync", small[:, 15:16], ap(srow, 4, [[0, 128], [1, 1]]),
                waits=[e4], inc=sB2)
    b1 = pr.op("vector", lambda e: e.tensor_scalar_mul(
        ocbs[:, :], one_cb[:, :], small[:, 14:15]), waits=[e5, t_ocb], inc=sV)
    b2 = pr.op("vector", lambda e: e.tensor_scalar_mul(
        sbs[:, :], modv[:, 24:32], small[:, 14:15]), waits=[e5], inc=sV)

    tok["VT"] = tok["v_r"][0] + tok["v_r"][1]
    tok["VT2"] = tok["VT"]
    tok["scoF"] = None
    tok_os = [tok["qt_r"], tok["kt_r"]]
    og2h_r = [tok["og2_r"], tok["gsil_r"]]
    og2h_w = [tok["og2_w"], tok["gsil_r"]]
    h2q_buf = [og2, gsil]
    h2q_off = [0, 2048]

    def f2_chains(g):
        hbuf = h2q_buf[g % 2]
        hoff = h2q_off[g % 2]
        toks = []
        for ci in range(2):
            cc = 2 * g + ci
            xap = ap(x_res, cc * CH, [[PC * T, 128], [T, PC], [1, CH]])
            ld = pr.dma("sync", sco_f[:, 0:256],
                        stats_dram[:, cc * 256:(cc + 1) * 256],
                        waits=[stat_st[cc], tok["scoF"], tok["sco_r"]],
                        inc=sStatI)
            mb = ap(sco_f, 0, [[256, 128], [0, PC], [1, CH]])
            rb_ = ap(sco_f, 128, [[256, 128], [0, PC], [1, CH]])
            tt0 = pr.op("vector", lambda e, i=xap, m=mb: e.tensor_tensor(
                ap(SCR_F, 0, [[1024, 128], [CH, PC], [1, CH]]), i, m,
                ALU.subtract), waits=[ld, tok["SCRF"]], inc=sV,
                ns=1250)
            tt1 = pr.op("vector", lambda e, r=rb_: e.tensor_tensor(
                ap(SCR_F, 0, [[1024, 128], [CH, PC], [1, CH]]),
                ap(SCR_F, 0, [[1024, 128], [CH, PC], [1, CH]]), r, ALU.mult),
                waits=[tt0], inc=sV, ns=1250)
            tok["scoF"] = [tt0, tt1]
            hts = []
            rn_h = pr.new_run()
            for p in range(PC):
                hts.append(pr.op("scalar", lambda e, pp=p: e.activation(
                    VT[:, pp * CH:(pp + 1) * CH],
                    SCR_F[:, pp * CH:(pp + 1) * CH], AF.Identity,
                    bias=sbs[:, pp:pp + 1], scale=ocbs[:, pp:pp + 1]),
                    waits=[tt1, b1, b2, tok["VT"]], inc=sA, ns=420,
                    run=rn_h))
            aa = pr.op("scalar", lambda e: e.activation(
                SCR_F[:, :], VT[:, :], AF.Identity, bias=ZC(),
                scale=consts[:, 3:4]), waits=hts + [tt1], inc=sA, ns=900)
            bb_ = pr.op("vector", lambda e: e.tensor_sub(
                VT2[:, :], SCR_F[:, :], VT[:, :]),
                waits=[aa, tok["VT2"]], inc=sV, ns=1250)
            tok["VT"] = [hts, aa, bb_]
            outap = ap(hbuf, hoff + ci * CH,
                       [[hbuf.shape[1], 128], [GR, PC], [1, CH]])
            vlast = pr.op("vector", lambda e, o=outap: e.tensor_tensor(
                o, SCR_F[:, :].rearrange("p (a b) -> p a b", a=PC),
                VT2[:, :].rearrange("p (a b) -> p a b", a=PC),
                ALU.subtract),
                waits=[bb_, og2h_r[g % 2], og2h_w[g % 2]], inc=sV,
                ns=1250)
            tok["VT2"] = vlast
            tok["SCRF"] = [bb_, vlast]
            toks.append(vlast)
        og2h_w[g % 2] = toks
        return toks

    h2q_pend = f2_chains(0)
    for g in range(NG):
        c0 = 2 * g
        hbuf = h2q_buf[g % 2]
        hoff = h2q_off[g % 2]
        h2q_toks = h2q_pend
        ffn_mms = []
        evs = []
        for m in range(PC):
            slot, mmt = tile_pass(wf_q, m, PC,
                                  lambda k, hb=hbuf, bb=hoff:
                                  hb[:, bb + k * GR: bb + (k + 1) * GR],
                                  h2q_toks + [wfq_done])
            ffn_mms.append(mmt)
            ev = pr.op("scalar", lambda e, mm=m, sl2=slot: e.activation(
                gsil[:, mm * GR:(mm + 1) * GR], PB4[sl2][:, 0:GR],
                AF.Identity, bias=bf_s[:, mm:mm + 1], scale=small[:, 15:16]),
                waits=[mmt, e6, misc_tok, tok["gsil_r"]], inc=sA, ns=480)
            pb_free[slot] = ev
            evs.append(ev)
        og2h_r[g % 2] = ffn_mms
        if g + 1 < NG:
            h2q_pend = f2_chains(g + 1)
        rlast = [[], []]
        for m in range(PC):
            for ci in range(2):
                cc = c0 + ci
                rlast[ci].append(pr.op(
                    "vector", lambda e, mm=m, cci=ci, c2=cc, os=OS[ci]:
                    e.scalar_tensor_tensor(
                        os[:, mm * CH:(mm + 1) * CH],
                        gsil[:, mm * GR + cci * CH: mm * GR + cci * CH + CH],
                        gb2[:, mm:mm + 1],
                        x_res[:, mm * T + c2 * CH: mm * T + (c2 + 1) * CH],
                        ALU.mult, ALU.add),
                    waits=[evs[m], t_gb, tok_os[ci]], inc=sV, ns=200))
        tok["gsil_r"] = rlast[0] + rlast[1]
        for ci in range(2):
            cc = c0 + ci
            outap = ap(yT, cc * CH, [[T, 128], [128 * T, PC], [1, CH]])
            out_tok[0] = pr.op("scalar", lambda e, o=outap, i=OS[ci],
                               cci=ci: e.dma_start(
                                   out=o,
                                   in_=i[:, :].rearrange(
                                       "p (a b) -> p a b", a=PC)),
                               waits=rlast[ci], inc=sOout, amt=16)
            tok_os[ci] = out_tok[0]

    pr.op("scalar", lambda e: e.nop(), waits=[out_tok[0]])

    pr.emit()
    return nc, sbuf_left


def _host_constants(T):
    f = np.arange(128, dtype=np.float64)
    inv_freq = ROPE_BASE ** (-f / 128.0)
    t = np.arange(T, dtype=np.float64)
    ang = inv_freq[:, None] * t[None, :]
    lams = np.array([lamh(h) for h in range(H)], dtype=np.float64)
    tau = np.arange(CH, dtype=np.float64)
    lamq = np.zeros((128, 1024), dtype=np.float64)
    for pc in range(PC):
        lamq[:, pc * 128:(pc + 1) * 128] = (lams[pc // 2] ** (tau + 1.0)) / 16.0
    sig = np.arange(128, dtype=np.float64)
    consts = np.zeros((128, 8), dtype=np.float32)
    consts[:, 0] = EPS
    consts[:, 1] = 1.0 / D
    consts[:, 2] = 1.0 / DV
    consts[:, 3] = VELT_C
    consts[:, 4] = 1e-12
    consts[:, 5] = FP8_MAX
    consts[:, 7] = 1.0
    return dict(
        cosT=np.cos(ang).astype(BF), sinT=np.sin(ang).astype(BF),
        lamqI=lamq.astype(BF),
        lamsigI=np.stack([lams[h] ** (-(sig + 1.0)) for h in range(H)],
                         axis=1).astype(np.float32),
        lamkI=np.stack([lams[h] ** (127.0 - sig) for h in range(H)],
                       axis=1).astype(np.float32),
        maskI=np.triu(np.ones((128, 128))).astype(BF),
        identI=np.eye(128).astype(BF),
        onesI=np.ones((128, 128)).astype(BF),
        constsI=consts)


def _tile(WT, mwidth):
    IN, OUT = WT.shape
    nm, nk = OUT // mwidth, IN // 128
    return np.ascontiguousarray(
        WT.reshape(nk, 128, nm, mwidth).transpose(2, 1, 0, 3)
        .reshape(nm * 128, nk * mwidth))


def prep_inputs(inputs, T, n_cores):
    consts = _host_constants(T)
    f32 = lambda k: np.asarray(inputs[k], dtype=np.float32)
    x = f32("L0_attn_x")
    gnw_full = np.tile(f32("L0_attn_g_norm_w"), H)
    shared = dict(
        wq_t=_tile(np.ascontiguousarray(f32("L0_attn_Wq").T), 128),
        wk_t=_tile(np.ascontiguousarray(f32("L0_attn_Wk").T), 128),
        wv_t=_tile(np.ascontiguousarray(f32("L0_attn_Wv").T), 256),
        wg_t=_tile(np.ascontiguousarray(f32("L0_attn_Wg").T), 128),
        woT=np.ascontiguousarray(f32("L0_attn_Wo").T),
        wf_t=_tile(np.ascontiguousarray(f32("L0_ffn_weight").T), 128),
        wmod=np.ascontiguousarray(f32("L0_wmod")),
        gnw=np.ascontiguousarray(gnw_full.reshape(DPC, 128).T),
        bfc=np.ascontiguousarray(f32("L0_ffn_bias").reshape(PC, 128).T),
        **consts)
    cmat = f32("L0_c")
    in_maps = []
    for b in range(n_cores):
        m = dict(shared)
        m["xT"] = np.ascontiguousarray(x[b].T)
        m["cT"] = np.ascontiguousarray(cmat[b].reshape(PC, 128).T)
        in_maps.append(m)
    return in_maps


def kernel(**inputs):
    T = int(np.asarray(inputs["L0_attn_x"]).shape[1])
    n_cores = 8
    in_maps = prep_inputs(inputs, T, n_cores)
    nc, _ = build(T, n_cores)
    res = run_bass_kernel_spmd(nc, in_maps, core_ids=list(range(n_cores)))
    outs = [np.ascontiguousarray(np.asarray(res.results[b]["yT"]).T)
            for b in range(n_cores)]
    return np.stack(outs, axis=0).astype(np.float32)


# version 36
# speedup vs baseline: 1.0262x; 1.0262x over previous
import numpy as np
import ml_dtypes

import concourse.bass as bass
import concourse.mybir as mybir
from concourse.bass_isa import ReduceOp
from concourse import library_config
from concourse.bass_utils import run_bass_kernel_spmd

F32 = mybir.dt.float32
BF16 = mybir.dt.bfloat16
AF = mybir.ActivationFunctionType
ALU = mybir.AluOpType
AX = mybir.AxisListType
BF = ml_dtypes.bfloat16

ENGINES = ("sync", "scalar", "vector", "tensor", "gpsimd")

D = 1024
H = 4
DK = 256
DV = 512
DVT = H * DV
PC = D // 128
DPC = DVT // 128
CH = 128
GR = 256
ROPE_BASE = 10000.0
EPS = 1e-5
VELT_C = float(2 ** 20 + 1)
FP8_MAX = 448.0


def lamh(h):
    return 1.0 - 2.0 ** (-5.0 - h)


class Sem:
    def __init__(self, nc, name, dma=False):
        self.name = name
        self.dma = dma
        self.n = 0


class Prog:

    def __init__(self, nc):
        self.nc = nc
        self.n_ops = 0
        self.engine = []
        self.fn = []
        self.deps = []
        self.dur = []
        self.isdma = []
        self.isasync = []
        self.tag = []
        self.run = []
        self.boost = []
        self.n_runs = 0
        self.sched = None

    def new_run(self):
        self.n_runs += 1
        return self.n_runs

    def sem(self, name, dma=False):
        return Sem(self.nc, name, dma)

    def op(self, eng, fn, waits=(), inc=None, amt=1, ns=None, dma=False,
           async_op=False, run=None, boost=0.0):
        flat = set()

        def _fl(w):
            if w is None:
                return
            if isinstance(w, (list, tuple)):
                for x in w:
                    _fl(x)
            else:
                flat.add(w)
        _fl(list(waits))
        deps = sorted(flat)
        i = self.n_ops
        self.n_ops += 1
        isdma = dma or amt == 16
        self.isasync.append(isdma or async_op)
        self.run.append(None if isdma else run)
        self.boost.append(boost)
        self.engine.append(eng)
        self.fn.append(fn)
        self.deps.append(deps)
        self.dur.append(ns if ns is not None else (1500 if isdma else 350))
        self.isdma.append(isdma)
        self.tag.append(inc if isdma else None)
        return i

    def dma(self, eng, out, in_, waits=(), inc=None, boost=0.0):
        assert out.size() == in_.size(), (
            f"dma size mismatch {out.size()} vs {in_.size()} "
            f"out={out.tensor.name} in={in_.tensor.name}")
        assert inc is not None
        ns = 900 + in_.nbytes() // 300
        return self.op(eng, lambda e, o=out, i=in_: e.dma_start(out=o, in_=i),
                       waits=waits, inc=inc, ns=ns, dma=True, boost=boost)

    def emit(self, reorder=True):
        n = self.n_ops
        mid_of = [0] * n
        members = []
        runmap = {}
        for i in range(n):
            r = self.run[i]
            if r is not None and (self.engine[i], r) in runmap:
                m = runmap[(self.engine[i], r)]
                members[m].append(i)
                mid_of[i] = m
            else:
                m = len(members)
                members.append([i])
                mid_of[i] = m
                if r is not None:
                    runmap[(self.engine[i], r)] = m
        nm = len(members)
        mdeps = [set() for _ in range(nm)]
        for i in range(n):
            for d in self.deps[i]:
                if mid_of[d] != mid_of[i]:
                    mdeps[mid_of[i]].add(mid_of[d])
        mdur = [sum(self.dur[i] for i in mem) for mem in members]
        meng = [self.engine[mem[0]] for mem in members]
        misdma = [self.isdma[mem[0]] for mem in members]
        mchild = [[] for _ in range(nm)]
        for m in range(nm):
            for d in mdeps[m]:
                mchild[d].append(m)
        mprio = [0.0] * nm
        mboost = [max(self.boost[i] for i in mem) for mem in members]
        for m in range(nm - 1, -1, -1):
            mx = 0.0
            for c in mchild[m]:
                if mprio[c] > mx:
                    mx = mprio[c]
            mprio[m] = mdur[m] + mx + mboost[m]
        import heapq
        engs = sorted(set(self.engine))
        etime = {e: 0.0 for e in engs}
        done_t = [0.0] * nm
        ndep = [len(mdeps[m]) for m in range(nm)]
        ready = {e: [] for e in engs}
        for m in range(nm):
            if ndep[m] == 0:
                heapq.heappush(ready[meng[m]], (-mprio[m], m))
        morder = []
        scheduled = 0
        if not reorder:
            morder = list(range(nm))
        else:
            while scheduled < nm:
                best = None
                for e in engs:
                    if not ready[e]:
                        continue
                    pr_, m = ready[e][0]
                    st = max(etime[e],
                             max((done_t[d] + (420.0 if meng[d] != e else 0.0)
                                  for d in mdeps[m]), default=0.0))
                    key = (st, pr_)
                    if best is None or key < best[0]:
                        best = (key, e, m)
                assert best is not None, "scheduler stuck"
                _, e, m = best
                heapq.heappop(ready[e])
                st = max(etime[e],
                         max((done_t[d] + (420.0 if meng[d] != e else 0.0)
                              for d in mdeps[m]), default=0.0))
                if misdma[m]:
                    etime[e] = st + 150.0
                    done_t[m] = st + mdur[m]
                else:
                    etime[e] = st + mdur[m]
                    done_t[m] = etime[e]
                morder.append(m)
                scheduled += 1
                for c in mchild[m]:
                    ndep[c] -= 1
                    if ndep[c] == 0:
                        heapq.heappush(ready[meng[c]], (-mprio[c], c))
        order = []
        for m in morder:
            order.extend(members[m])
        streams = {e: [] for e in engs}
        for i in order:
            streams[self.engine[i]].append(i)

        def semkey(i):
            return (self.engine[i], self.tag[i]) if self.isdma[i] \
                else (self.engine[i], None)

        def sem_dep(d, i):
            return self.isasync[d] or self.isasync[i] or \
                not (self.engine[d] == self.engine[i] == "tensor")

        need_inc = [False] * n
        for i in range(n):
            for d in self.deps[i]:
                if sem_dep(d, i):
                    need_inc[d] = True
        for i in range(n):
            if self.isasync[i]:
                need_inc[i] = True

        sems = {}
        for i in order:
            if not need_inc[i]:
                continue
            k = semkey(i)
            if k not in sems:
                nm = f"s{len(sems)}_{k[0]}" + ("d" if self.isdma[i] else "")
                sems[k] = self.nc.alloc_semaphore(nm)
        cum = [0] * n
        scount = {}
        for e in engs:
            for i in streams[e]:
                k = semkey(i)
                if need_inc[i]:
                    scount[k] = scount.get(k, 0) + \
                        (16 if self.isdma[i] else 1)
                cum[i] = scount.get(k, 0)
        waited = {e: {} for e in engs}

        def wait_list(i):
            out = []
            for d in self.deps[i]:
                if not sem_dep(d, i):
                    continue
                out.append((semkey(d), cum[d]))
            if self.isdma[i] and cum[i] > 16:
                out.append((semkey(i), cum[i] - 16))
            return out

        import os
        if os.environ.get("PROG_DEBUG"):
            import pickle
            with open("/tmp/prog_dbg.pkl", "wb") as f:
                pickle.dump(dict(order=order, streams=streams, cum=cum,
                                 need_inc=need_inc, deps=self.deps,
                                 engine=self.engine, isdma=self.isdma,
                                 tag={i: str(self.tag[i]) for i in range(n)
                                      if self.isdma[i]}), f)
        with self.nc.Block() as blk:
            for eng_name in engs:
                ops_ = streams[eng_name]
                if not ops_:
                    continue

                def make(ops_=ops_, eng_name=eng_name):
                    def body(eng):
                        w8 = waited[eng_name]
                        for i in ops_:
                            for key, v in wait_list(i):
                                if w8.get(key, 0) < v:
                                    eng.wait_ge(sems[key], v)
                                    w8[key] = v
                            ins = self.fn[i](eng)
                            if need_inc[i]:
                                ins.then_inc(sems[semkey(i)],
                                             16 if self.isdma[i] else 1)
                    return body

                getattr(blk, eng_name)(make())
def ap(t, off, dims):
    return bass.AP(t, off, [list(d) for d in dims])


def build(T, n_cores):
    NG = T // GR
    nc = bass.Bass("TRN2", target_bir_lowering=False)

    xT = nc.dram_tensor("xT", [D, T], F32, kind="ExternalInput")
    wq_t = nc.dram_tensor("wq_t", [D, D], F32, kind="ExternalInput")
    wk_t = nc.dram_tensor("wk_t", [D, D], F32, kind="ExternalInput")
    wv_t = nc.dram_tensor("wv_t", [D, DVT], F32, kind="ExternalInput")
    wg_t = nc.dram_tensor("wg_t", [DVT, D], F32, kind="ExternalInput")
    woT = nc.dram_tensor("woT", [DVT, D], F32, kind="ExternalInput")
    wf_t = nc.dram_tensor("wf_t", [D, D], F32, kind="ExternalInput")
    wmod = nc.dram_tensor("wmod", [D, 6 * D], F32, kind="ExternalInput")
    cT = nc.dram_tensor("cT", [128, PC], F32, kind="ExternalInput")
    gnw = nc.dram_tensor("gnw", [128, DPC], F32, kind="ExternalInput")
    bfc = nc.dram_tensor("bfc", [128, PC], F32, kind="ExternalInput")
    cosT = nc.dram_tensor("cosT", [128, T], BF16, kind="ExternalInput")
    sinT = nc.dram_tensor("sinT", [128, T], BF16, kind="ExternalInput")
    lamqI = nc.dram_tensor("lamqI", [128, 1024], BF16, kind="ExternalInput")
    lamsigI = nc.dram_tensor("lamsigI", [128, H], F32, kind="ExternalInput")
    lamkI = nc.dram_tensor("lamkI", [128, H], F32, kind="ExternalInput")
    maskI = nc.dram_tensor("maskI", [128, 128], BF16, kind="ExternalInput")
    identI = nc.dram_tensor("identI", [128, 128], BF16, kind="ExternalInput")
    onesI = nc.dram_tensor("onesI", [128, 128], BF16, kind="ExternalInput")
    constsI = nc.dram_tensor("constsI", [128, 8], F32, kind="ExternalInput")

    yT = nc.dram_tensor("yT", [D, T], F32, kind="ExternalOutput")

    wq_bf = nc.dram_tensor("wq_bf", [D, D], BF16)
    wk_bf = nc.dram_tensor("wk_bf", [D, D], BF16)
    wv_bf = nc.dram_tensor("wv_bf", [D, DVT], BF16)
    wg_bf = nc.dram_tensor("wg_bf", [DVT, D], BF16)
    wo_m = nc.dram_tensor("wo_m", [PC * 128, DVT], BF16)
    wf_q = nc.dram_tensor("wf_q", [D, D], BF16)
    mod_dram = nc.dram_tensor("mod_dram", [1, 6 * D], F32)
    wrow = nc.dram_tensor("wrow", [1, 128], F32)
    srow = nc.dram_tensor("srow", [1, 8], F32)
    cc_in = nc.dram_tensor("cc_in", [1, 128], F32)
    cc_out = nc.dram_tensor("cc_out", [1, 128], F32)
    stats_dram = nc.dram_tensor("stats_dram", [128, (T // CH) * 256], F32)
    ln1_dram = nc.dram_tensor("ln1_dram", [128, (T // CH) * 256], F32)

    A = nc.alloc_sbuf_tensor
    x_res = A("x_res", [128, PC * T], F32)
    slabP = A("slabP", [128, 6144], BF16)
    xh = A("xh", [128, PC * GR], BF16)
    xh2 = xh
    qt = A("qt", [128, PC * GR], BF16)
    kt = A("kt", [128, PC * GR], BF16)
    v_tm = A("v_tm", [128, 2 * DVT], BF16)
    gsil = A("gsil", [128, DPC * GR], BF16)
    og2 = A("og2", [128, DPC * 128], BF16)
    S = A("S", [128, H * 2 * DV], F32)
    k_tm = A("k_tm", [128, H * DK], BF16)
    sco_f = A("sco_f", [128, 256], F32)
    sco = sco_f.ap().bitcast(BF16)
    ropeB = A("ropeB", [128, H * GR], BF16)
    cos_sl = A("cos_sl", [128, GR], BF16)
    sin_sl = A("sin_sl", [128, GR], BF16)
    lamq_s = A("lamq_s", [128, 1024], BF16)
    mask_s = A("mask_s", [128, 128], BF16)
    ident_s = A("ident_s", [128, 128], BF16)
    ones_s = A("ones_s", [128, 128], BF16)
    lamsig_s = A("lamsig_s", [128, H], F32)
    lamk_s = A("lamk_s", [128, H], F32)
    consts = A("consts", [128, 8], F32)
    modv = A("modv", [128, 48], F32)
    one_ca = A("one_ca", [128, PC], F32)
    one_cb = A("one_cb", [128, PC], F32)
    ga2 = A("ga2", [128, PC], F32)
    gb2 = A("gb2", [128, PC], F32)
    ocbs = A("ocbs", [128, PC], F32)
    sbs = A("sbs", [128, PC], F32)
    gnw_s = A("gnw_s", [128, DPC], F32)
    bf_s = A("bf_s", [128, PC], F32)
    cT_s = A("cT_s", [128, PC], F32)
    silu_c = A("silu_c", [128, PC], F32)
    RB = A("RB", [128, 512], F32)
    SCR_F = A("SCR_F", [128, 1024], F32)
    SCR_B = A("SCR_B", [128, 2048], BF16)
    amax_c = A("amax_c", [128, 1], F32)
    red_c = A("red_c", [128, 1], F32)
    small = A("small", [128, 16], F32)
    S_bf = ropeB
    VT = v_tm.ap().bitcast(F32)[:, 0:1024]
    VT2 = v_tm.ap().bitcast(F32)[:, 1024:2048]
    OS = [qt.ap().bitcast(F32), kt.ap().bitcast(F32)]

    sbuf_left = nc.sbuf_bytes_remaining

    P = nc.alloc_psum_tensor
    PB = [P("PB0", [128, 512], F32), P("PB1", [128, 512], F32)]
    PO = [P("POA", [128, 512], F32), P("POB", [128, 512], F32)]
    PX = P("PX", [128, 512], F32)
    PTR = P("PTR", [128, 1024], BF16)
    PXS = [P("PX2", [128, 512], F32), P("PX3", [128, 512], F32)]

    pr = Prog(nc)
    sMisc = pr.sem("sMisc", dma=True)
    sXq = [pr.sem("sX0", dma=True), pr.sem("sX1", dma=True),
           pr.sem("sX2", dma=True)]
    sTab = pr.sem("sTab", dma=True)
    sSlab = [pr.sem("sSlab0", dma=True), pr.sem("sSlab1", dma=True),
             pr.sem("sSlab2", dma=True)]
    sSlabX = [pr.sem("sSlabX0", dma=True), pr.sem("sSlabX1", dma=True),
              pr.sem("sSlabX2", dma=True)]
    sCF = pr.sem("sCF", dma=True)
    sCG = pr.sem("sCG", dma=True)
    sOF = pr.sem("sOF", dma=True)
    sOG = pr.sem("sOG", dma=True)
    sOmod = pr.sem("sOmod", dma=True)
    sOout = pr.sem("sOout", dma=True)
    sG = pr.sem("sG", dma=True)
    sStatO = pr.sem("sStatO", dma=True)
    sStatI = pr.sem("sStatI", dma=True)
    sPO = pr.sem("sPO", dma=True)
    sPI = pr.sem("sPI", dma=True)
    sWM1 = pr.sem("sWM1", dma=True)
    sWM2 = pr.sem("sWM2", dma=True)
    sB1 = pr.sem("sB1", dma=True)
    sB2 = pr.sem("sB2", dma=True)
    sSlabS = [pr.sem("sSlabS0", dma=True), pr.sem("sSlabS1", dma=True),
              pr.sem("sSlabS2", dma=True)]
    sGP = pr.sem("sGP")
    sCC = pr.sem("sCC")
    sT = pr.sem("sT")
    sA = pr.sem("sA")
    sV = pr.sem("sV")

    ZC = lambda: consts[:, 6:7]
    OC = lambda: consts[:, 7:8]

    misc_tok = []
    for dst, src in [(consts, constsI), (mask_s, maskI), (ident_s, identI),
                     (ones_s, onesI), (lamq_s, lamqI), (lamsig_s, lamsigI),
                     (lamk_s, lamkI), (gnw_s, gnw), (bf_s, bfc), (cT_s, cT)]:
        misc_tok.append(pr.dma("sync", dst[:, :], src[:, :], inc=sMisc,
                                boost=1e7))

    xloads = []
    for p in range(PC):
        xq = ["sync", "scalar", "gpsimd"][p % 3]
        xloads.append(pr.dma(xq, x_res[:, p * T:(p + 1) * T],
                             xT[p * 128:(p + 1) * 128, :],
                             inc=sXq[p % 3], boost=1e7))

    z_am = pr.op("vector", lambda e: e.memset(amax_c[:, :], 0.0), inc=sV)

    w_cast = {}
    _bsts = [4e6, 3e6, 2e6, 1e6]
    for _j, (csrc, cdst) in enumerate([(wq_t, wq_bf), (wk_t, wk_bf),
                                       (wv_t, wv_bf), (wg_t, wg_bf)]):
        w_cast[cdst.name] = pr.dma("gpsimd", cdst[:, :], csrc[:, :],
                                   inc=sG, boost=_bsts[_j])
    cast_jobs = []
    for r in range(DVT // 128):
        cast_jobs.append((woT, None, r, 0, r))

    scr = [SCR_F, VT]
    sC = [sCF, sCG]
    sO2 = [sOF, sOG]
    scr_tok = [None, None]
    sb_tok = [None, None]
    cast_last = [None, None]
    for j, (src, dst, r, c0, gcol) in enumerate(cast_jobs):
        s2 = j % 2
        lt = pr.dma("sync", scr[s2][:, :],
                    src[r * 128:(r + 1) * 128, c0:c0 + 1024],
                    waits=[scr_tok[s2]], inc=sC[s2])
        dd = SCR_B[:, s2 * 1024:(s2 + 1) * 1024]
        if gcol is None:
            ct = pr.op("vector",
                       lambda e, o=dd, i=scr[s2][:, :]: e.tensor_copy(o, i),
                       waits=[lt, sb_tok[s2]], inc=sV)
        else:
            ct = pr.op("scalar", lambda e, o=dd, i=scr[s2][:, :],
                       g=gnw_s[:, gcol:gcol + 1]:
                       e.activation(o, i, AF.Identity, bias=ZC(), scale=g),
                       waits=[lt, sb_tok[s2], misc_tok], inc=sA)
        if dst is None:
            dap = ap(wo_m, r * 128, [[DVT, 128], [128 * DVT, 8], [1, 128]])
            st = pr.op("scalar", lambda e, o=dap, i=dd:
                       e.dma_start(out=o, in_=i.rearrange(
                           "p (a b) -> p a b", a=8)),
                       waits=[ct], inc=sO2[s2], amt=16)
        else:
            st = pr.dma("scalar", dst[r * 128:(r + 1) * 128, c0:c0 + 1024],
                        dd, waits=[ct], inc=sO2[s2])
        scr_tok[s2] = ct
        sb_tok[s2] = st
        cast_last[s2] = st

    gsf = gsil.ap().bitcast(F32)
    t_sc = pr.op("scalar", lambda e: e.activation(silu_c[:, :], cT_s[:, :],
                                                  AF.Silu),
                 waits=[misc_tok], inc=sA)
    banks4 = [PB[0], PB[1], PO[0], PO[1]]
    bank4_tok = [None] * 4
    mod_st = []
    mm_prev = [None] * 4
    for rnd in range(3):
        mmtoks = [None] * 4
        for k in range(PC):
            lt = pr.dma("scalar", SCR_F[:, :],
                        wmod[k * 128:(k + 1) * 128,
                             rnd * 2048:rnd * 2048 + 1024],
                        waits=[scr_tok[0], mm_prev[0], mm_prev[1]], inc=sWM1)
            lt2 = pr.dma("scalar", VT[:, :],
                         wmod[k * 128:(k + 1) * 128,
                              rnd * 2048 + 1024:rnd * 2048 + 2048],
                         waits=[scr_tok[1], mm_prev[2], mm_prev[3]],
                         inc=sWM2)
            for b4 in range(4):
                src34 = SCR_F if b4 < 2 else VT
                mmtoks[b4] = pr.op(
                    "tensor",
                    lambda e, bk=banks4[b4], kk=k, bb=b4, sc3=src34:
                    e.matmul(bk[0:1, 0:512], silu_c[:, kk:kk + 1],
                             sc3[:, (bb % 2) * 512:(bb % 2 + 1) * 512],
                             start=(kk == 0), stop=(kk == PC - 1)),
                    waits=[lt if b4 < 2 else lt2, t_sc, bank4_tok[b4],
                           mm_prev[b4]], ns=600)
            for _b in range(4):
                mm_prev[_b] = mmtoks[_b]
        for b4 in range(4):
            seg = rnd * 4 + b4
            et = pr.op("scalar", lambda e, bk=banks4[b4]: e.activation(
                RB[0:1, 0:512], bk[0:1, 0:512], AF.Identity,
                bias=consts[0:1, 6:7], scale=consts[0:1, 7:8]),
                waits=[mmtoks[b4], misc_tok,
                       mod_st[-1] if mod_st else None], inc=sA)
            st = pr.dma("scalar", mod_dram[0:1, seg * 512:(seg + 1) * 512],
                        RB[0:1, 0:512], waits=[et], inc=sOmod)
            bank4_tok[b4] = et
            mod_st.append(st)
    zgE = pr.op("vector", lambda e: e.memset(gsf[:, 1024:1152], 0.0),
                inc=sV)
    ltE = pr.dma("sync", gsf[0:16, 1024:1152],
                 ap(mod_dram, 0, [[128, 16], [1, 128]]),
                 waits=mod_st[0:4] + [zgE], inc=sCG)
    idfE = pr.op("scalar", lambda e: e.activation(gsf[:, 1152:1280],
                                                  ident_s[:, :],
                                                  AF.Identity),
                 waits=[misc_tok], inc=sA)
    ttrE = pr.op("tensor", lambda e: e.transpose(PX[:, 256:384],
                                                 gsf[:, 1024:1152],
                                                 gsf[:, 1152:1280]),
                 waits=[ltE, idfE, zgE], inc=sT)
    tmvE = pr.op("scalar", lambda e: e.activation(modv[:, 0:16],
                                                  PX[:, 256:272],
                                                  AF.Identity),
                 waits=[ttrE], inc=sA)
    zg = pr.op("vector", lambda e: e.memset(VT[:, 0:128], 0.0),
               waits=[scr_tok[1], mm_prev[2], mm_prev[3]], inc=sV)
    lt = pr.dma("sync", VT[0:32, 0:128],
                ap(mod_dram, 2048, [[128, 32], [1, 128]]),
                waits=mod_st + [zg], inc=sCG)
    idf = pr.op("scalar", lambda e: e.activation(SCR_F[:, 0:128],
                                                 ident_s[:, :], AF.Identity),
                waits=[misc_tok, scr_tok[0], mm_prev[0], mm_prev[1]], inc=sA)
    ttr = pr.op("tensor", lambda e: e.transpose(PX[:, 0:128], VT[:, 0:128],
                                                SCR_F[:, 0:128]),
                waits=[lt, idf, zg, tmvE], inc=sT)
    tmv = pr.op("scalar", lambda e: e.activation(modv[:, 16:48],
                                                 PX[:, 0:32],
                                                 AF.Identity),
                waits=[ttr], inc=sA)
    scr_tok = [tmv, tmv]

    t_oca = pr.op("vector", lambda e: e.tensor_scalar_add(one_ca[:, :],
                                                          modv[:, 8:16], 1.0),
                  waits=[tmvE], inc=sV)
    t_ocb = pr.op("vector", lambda e: e.tensor_scalar_add(one_cb[:, :],
                                                          modv[:, 32:40], 1.0),
                  waits=[tmv], inc=sV)

    def sign_clamp(dstv, col0, w0):
        a = pr.op("scalar", lambda e: e.activation(SCR_F[:, 0:PC],
                                                   modv[:, col0:col0 + PC],
                                                   AF.Abs),
                  waits=[tmv, w0], inc=sA)
        b = pr.op("vector", lambda e: e.tensor_scalar_max(
            SCR_F[:, 0:PC], SCR_F[:, 0:PC], 0.25), waits=[a], inc=sV)
        c1 = pr.op("vector", lambda e: e.tensor_single_scalar(
            VT[:, 0:PC], modv[:, col0:col0 + PC], 0.0, ALU.is_ge),
            waits=[tmv, w0], inc=sV)
        c2 = pr.op("vector", lambda e: e.tensor_scalar(
            VT[:, 0:PC], VT[:, 0:PC], 2.0, -1.0, ALU.mult, ALU.add),
            waits=[c1], inc=sV)
        return pr.op("vector", lambda e: e.tensor_mul(
            dstv[:, :], SCR_F[:, 0:PC], VT[:, 0:PC]),
            waits=[b, c2], inc=sV)

    t_ga = sign_clamp(ga2, 16, scr_tok[0])
    t_gb = sign_clamp(gb2, 40, t_ga)
    scr_tok = [t_gb, t_gb]

    wfa = z_am
    for k in range(PC):
        lt = pr.dma("sync", SCR_F[:, :], wf_t[k * 128:(k + 1) * 128, :],
                    waits=[scr_tok[0] if k == 0 else wfa], inc=sCF)
        rt = pr.op("vector", lambda e: e.reduce_max(
            red_c[:, :], SCR_F[:, :], axis=AX.X, apply_absolute_value=True),
            waits=[lt, z_am], inc=sV)
        wfa = pr.op("vector", lambda e: e.tensor_max(
            amax_c[:, :], amax_c[:, :], red_c[:, :]), waits=[rt], inc=sV)
    d1 = pr.dma("scalar", wrow[0, :], amax_c[:, 0], waits=[wfa], inc=sB1)
    d2 = pr.dma("sync", RB[0:1, 0:128], wrow[:, :],
                waits=[d1, mod_st[-1]], inc=sB2)
    r1 = pr.op("vector", lambda e: e.reduce_max(small[0:1, 0:1],
                                                RB[0:1, 0:128], axis=AX.X),
               waits=[d2], inc=sV)
    r2 = pr.op("vector", lambda e: e.tensor_scalar_max(
        small[0:1, 1:2], small[0:1, 0:1], 1e-12), waits=[r1], inc=sV)
    r3 = pr.op("vector", lambda e: e.reciprocal(small[0:1, 2:3],
                                                small[0:1, 1:2]),
               waits=[r2], inc=sV)
    r4 = pr.op("vector", lambda e: e.tensor_scalar(
        small[0:1, 3:4], small[0:1, 2:3], FP8_MAX, FP8_MAX, ALU.mult,
        ALU.min), waits=[r3], inc=sV)
    r5 = pr.op("vector", lambda e: e.reciprocal(small[0:1, 4:5],
                                                small[0:1, 3:4]),
               waits=[r4], inc=sV)
    d3 = pr.dma("scalar", srow[0:1, 0:2], small[0:1, 3:5], waits=[r5],
                inc=sB1)
    d4 = pr.dma("sync", small[:, 5:6], ap(srow, 0, [[0, 128], [1, 1]]),
                waits=[d3], inc=sB2)
    wqp = None
    rbtok = d2
    for k in range(PC):
        lt = pr.dma("sync", SCR_F[:, :], wf_t[k * 128:(k + 1) * 128, :],
                    waits=[wqp, d4], inc=sCF)
        y = pr.op("vector", lambda e: e.tensor_scalar_mul(
            VT[:, :], SCR_F[:, :], small[:, 5:6]),
            waits=[lt, scr_tok[1] if k == 0 else None, d4], inc=sV)
        tt = pr.op("vector", lambda e: e.tensor_scalar_mul(
            SCR_F[:, :], VT[:, :], VELT_C), waits=[y], inc=sV)
        v1 = pr.op("vector", lambda e: e.tensor_sub(
            RB[:, 0:512], SCR_F[:, 0:512], VT[:, 0:512]),
            waits=[tt, rbtok], inc=sV)
        h1 = pr.op("vector", lambda e: e.tensor_sub(
            SCR_B[:, 0:512], SCR_F[:, 0:512], RB[:, 0:512]),
            waits=[v1, sb_tok[0]], inc=sV)
        v2 = pr.op("vector", lambda e: e.tensor_sub(
            RB[:, 0:512], SCR_F[:, 512:1024], VT[:, 512:1024]),
            waits=[h1], inc=sV)
        h2_ = pr.op("vector", lambda e: e.tensor_sub(
            SCR_B[:, 512:1024], SCR_F[:, 512:1024], RB[:, 0:512]),
            waits=[v2], inc=sV)
        st = pr.dma("scalar", wf_q[k * 128:(k + 1) * 128, :],
                    SCR_B[:, 0:1024], waits=[h2_], inc=sOF)
        wqp = st
        rbtok = h2_
        sb_tok[0] = st
    wfq_done = wqp

    z_S = pr.op("vector", lambda e: e.memset(S[:, :], 0.0),
                waits=[sb_tok[0], sb_tok[1], cast_last[1]], inc=sV)
    z_am2 = pr.op("vector", lambda e: e.memset(amax_c[:, :], 0.0),
                  waits=[d1], inc=sV)

    NCH = T // CH
    og2f = og2.ap().bitcast(F32)
    prol_rows = {}
    prol_store = {}
    stage_tok = [None, None]
    bank_chain = [None, None]
    for cc in range(NCH):
        sb2 = cc % 2
        stb = og2 if sb2 == 0 else gsil
        stf = og2f if sb2 == 0 else gsf
        bank = (cc // 2) % 2
        PXB = PXS[bank]
        s0 = (cc % 2) * 256
        xapc = ap(x_res, cc * CH, [[PC * T, 128], [T, PC], [1, CH]])
        cst = pr.op("scalar", lambda e, i=xapc, o=stb: e.activation(
            ap(o, 0, [[o.shape[1], 128], [CH, PC], [1, CH]]), i,
            AF.Identity), waits=xloads + [stage_tok[sb2]], inc=sA, ns=900)
        sq = pr.op("vector", lambda e, i=xapc, o=stb: e.tensor_tensor(
            ap(o, 1024, [[o.shape[1], 128], [CH, PC], [1, CH]]), i, i,
            ALU.mult), waits=xloads + [stage_tok[sb2]], inc=sV, ns=1250)
        mm1 = mm2 = None
        rn1 = pr.new_run()
        for k in range(PC):
            mm1 = pr.op("tensor", lambda e, kk=k, o=stb, b=PXB, s=s0:
                        e.matmul(b[:, s:s + 128], ones_s[:, :],
                                 o[:, kk * 128:(kk + 1) * 128],
                                 start=(kk == 0), stop=(kk == PC - 1)),
                        waits=[cst, misc_tok, bank_chain[bank], mm1],
                        ns=230, run=rn1)
        rn2 = pr.new_run()
        for k in range(PC):
            mm2 = pr.op("tensor", lambda e, kk=k, o=stb, b=PXB, s=s0:
                        e.matmul(b[:, s + 128:s + 256], ones_s[:, :],
                                 o[:, 1024 + kk * 128:1024 + (kk + 1) * 128],
                                 start=(kk == 0), stop=(kk == PC - 1)),
                        waits=[sq, mm1, mm2], ns=230, run=rn2)
        rm = pr.op("scalar", lambda e, b=PXB, s=s0: e.activation(
            b[:, s:s + 128], b[:, s:s + 128], AF.Identity, bias=ZC(),
            scale=consts[:, 1:2]), waits=[mm2], inc=sA, ns=420)
        rq = pr.op("scalar", lambda e, b=PXB, s=s0: e.activation(
            b[:, s + 128:s + 256], b[:, s + 128:s + 256], AF.Identity,
            bias=ZC(), scale=consts[:, 1:2]), waits=[mm2, rm], inc=sA,
            ns=420)
        t1 = pr.op("scalar", lambda e, b=PXB, s=s0, f=stf: e.activation(
            f[:, 0:128], b[:, s:s + 128], AF.Square),
            waits=[rm, mm2], inc=sA, ns=420)
        a2 = pr.op("vector", lambda e, b=PXB, s=s0, f=stf: e.tensor_sub(
            f[:, 0:128], b[:, s + 128:s + 256], f[:, 0:128]),
            waits=[t1, rq], inc=sV, ns=200)
        a3 = pr.op("scalar", lambda e, f=stf: e.activation(
            f[:, 0:128], f[:, 0:128], AF.Sqrt, bias=consts[:, 0:1],
            scale=OC()), waits=[a2], inc=sA, ns=520)
        a4 = pr.op("vector", lambda e, b=PXB, s=s0, f=stf: e.reciprocal(
            b[:, s + 128:s + 256], f[:, 0:128]), waits=[a3], inc=sV,
            ns=900)
        cp = pr.op("scalar", lambda e, b=PXB, s=s0, f=stf: e.activation(
            f[:, 128:384], b[:, s:s + 256], AF.Identity),
            waits=[a4, mm2], inc=sA, ns=450)
        st = pr.dma("scalar", ln1_dram[:, cc * 256:(cc + 1) * 256],
                    stf[:, 128:384], waits=[cp], inc=sPO)
        prol_rows[cc] = [rm, a4]
        prol_store[cc] = st
        stage_tok[sb2] = [mm2, a4, st]
        bank_chain[bank] = cp

    PB4 = PB + PXS
    pb_free = [bank4_tok[0], bank4_tok[1],
               [bank_chain[0]], [bank_chain[1]]]
    po_free = [bank4_tok[2], bank4_tok[3]]
    px_chain = [tmv, tmvE]
    ptr_chain = [None]
    pass_i = [0]
    pool_pos = [0]

    tok = {
        "xh": None, "xh2": None, "qt_w": None, "kt_w": None,
        "qt_r": None, "kt_r": None,
        "v_w": [None, None], "v_r": [None, None], "RBr": rbtok,
        "gsil_w": None, "gsil_r": [stage_tok[1], ttrE],
        "og2_w": None, "og2_r": [stage_tok[0]],
        "S": z_S, "Sbf_w": None, "Sbf_r": None,
        "ktm_w": None, "ktm_r": None, "sco_w": None, "sco_r": None,
        "ropeB": None, "tab": None, "RB": rbtok,
        "RBv": rbtok, "RBs": rbtok,
        "SCRF": z_S, "SB": [z_S, z_S, z_S, z_S], "VT": z_S, "VT2": z_S,
        "RBst": rbtok, "redc": d1,
        "slab": [cast_last[0], cast_last[1]] * 3, "xmid": {},
    }
    f1_done = []
    stat_st = {}
    res_toks = {}
    out_tok = [None]

    def ln_stats(xap, waits_read):
        xb = pr.op("scalar", lambda e, i=xap: e.activation(
            ap(SCR_B, 0, [[2048, 128], [CH, PC], [1, CH]]), i, AF.Identity),
            waits=waits_read + [tok["SB"][0], tok["SB"][1]], inc=sA, ns=900)
        x2 = pr.op("vector", lambda e, i=xap: e.tensor_tensor(
            ap(SCR_B, 1024, [[2048, 128], [CH, PC], [1, CH]]), i, i,
            ALU.mult),
            waits=waits_read + [tok["SB"][2], tok["SB"][3], xb], inc=sV,
            ns=1250)
        slA = pass_i[0] % 4
        pass_i[0] += 1
        slB = pass_i[0] % 4
        pass_i[0] += 1
        mm1 = mm2 = None
        rn1 = pr.new_run()
        for k in range(PC):
            mm1 = pr.op("tensor", lambda e, kk=k, s2=slA: e.matmul(
                PB4[s2][:, 0:128], ones_s[:, :],
                SCR_B[:, kk * 128:(kk + 1) * 128],
                start=(kk == 0), stop=(kk == PC - 1)),
                waits=[xb, misc_tok, pb_free[slA], mm1], ns=230, run=rn1)
        rn2 = pr.new_run()
        for k in range(PC):
            mm2 = pr.op("tensor", lambda e, kk=k, s2=slB: e.matmul(
                PB4[s2][:, 0:128], ones_s[:, :],
                SCR_B[:, 1024 + kk * 128:1024 + (kk + 1) * 128],
                start=(kk == 0), stop=(kk == PC - 1)),
                waits=[x2, pb_free[slB], mm2], ns=230, run=rn2)
        rm = pr.op("scalar", lambda e, s2=slA: e.activation(
            RB[:, 0:128], PB4[s2][:, 0:128], AF.Identity, bias=ZC(),
            scale=consts[:, 1:2]), waits=[mm1, tok["RB"], tok["RBst"]],
            inc=sA, ns=420)
        rq = pr.op("scalar", lambda e, s2=slB: e.activation(
            RB[:, 128:256], PB4[s2][:, 0:128], AF.Identity, bias=ZC(),
            scale=consts[:, 1:2]), waits=[mm2, rm], inc=sA, ns=420)
        pb_free[slA] = rm
        pb_free[slB] = rq
        a1 = pr.op("vector", lambda e: e.tensor_mul(
            RB[:, 256:384], RB[:, 0:128], RB[:, 0:128]),
            waits=[rm, tok["RBv"]], inc=sV, ns=200)
        a2 = pr.op("vector", lambda e: e.tensor_sub(
            RB[:, 256:384], RB[:, 128:256], RB[:, 256:384]),
            waits=[a1, rq], inc=sV, ns=200)
        a3 = pr.op("scalar", lambda e: e.activation(
            RB[:, 256:384], RB[:, 256:384], AF.Sqrt, bias=consts[:, 0:1],
            scale=OC()), waits=[a2], inc=sA, ns=520)
        a4 = pr.op("vector", lambda e: e.reciprocal(
            RB[:, 128:256], RB[:, 256:384]), waits=[a3], inc=sV, ns=900)
        tok["RB"] = a4
        tok["RBv"] = a4
        tok["SB"][0] = mm1
        tok["SB"][1] = mm1
        tok["SB"][2] = mm2
        tok["SB"][3] = mm2
        return a4

    def ln_norm(xap, rows_tok):
        mb = ap(RB, 0, [[512, 128], [0, PC], [1, CH]])
        rb_ = ap(RB, 128, [[512, 128], [0, PC], [1, CH]])
        tt0 = pr.op("vector", lambda e, i=xap: e.tensor_tensor(
            ap(SCR_F, 0, [[1024, 128], [CH, PC], [1, CH]]), i, mb,
            ALU.subtract), waits=[rows_tok, tok["SCRF"]], inc=sV, ns=1250)
        tt1 = pr.op("vector", lambda e: e.tensor_tensor(
            ap(SCR_F, 0, [[1024, 128], [CH, PC], [1, CH]]),
            ap(SCR_F, 0, [[1024, 128], [CH, PC], [1, CH]]), rb_, ALU.mult),
            waits=[tt0, rows_tok], inc=sV, ns=1250)
        tok["SCRF"] = tt1
        tok["RB"] = tt1
        return tt1

    def tile_pass(src_bf, m, kcount, rhs_fn, xtra, nw=GR, split=False,
                  xtra_mid=None):
        slot = pass_i[0] % 4
        pass_i[0] += 1
        wide = kcount * 128
        nwin = wide // 1024
        if nwin == 2 and pool_pos[0] % 2 == 1:
            pool_pos[0] += 1
        w0_ = pool_pos[0] % 6
        pool_pos[0] += nwin
        sl = slabP.ap()[:, w0_ * 1024: w0_ * 1024 + wide]
        wt = [tok["slab"][(w0_ + j) % 6] for j in range(nwin)]
        wt = wt + [w_cast.get(src_bf.name)]
        if split and nwin == 2:
            lt = pr.dma("sync", sl[:, 0:1024],
                        src_bf[m * 128:(m + 1) * 128, 0:1024],
                        waits=wt, inc=sSlab[w0_ % 3])
            lt2 = pr.dma("scalar", sl[:, 1024:2048],
                         src_bf[m * 128:(m + 1) * 128, 1024:2048],
                         waits=wt, inc=sSlabS[(w0_ + 1) % 3])
        else:
            lt = pr.dma("sync" if w0_ % 2 == 0 else "scalar",
                        sl[:, 0:wide],
                        src_bf[m * 128:(m + 1) * 128, 0:wide],
                        waits=wt,
                        inc=(sSlab if w0_ % 2 == 0 else sSlabX)[w0_ % 3])
            lt2 = lt
        mmt = None
        rn = pr.new_run()
        for k in range(kcount):
            mmt = pr.op("tensor", lambda e, kk=k, s=sl, sl2=slot, w=nw:
                        e.matmul(PB4[sl2][:, 0:w],
                                 s[:, kk * 128:(kk + 1) * 128], rhs_fn(kk),
                                 start=(kk == 0), stop=(kk == kcount - 1)),
                        waits=[lt if k * 128 < 1024 else lt2,
                               pb_free[slot], mmt] +
                        (xtra if k == 0 else []) +
                        (xtra_mid if (xtra_mid is not None and
                                      k == kcount // 2) else []),
                        ns=(330 if nw == GR else 230), run=rn)
        for j in range(nwin):
            tok["slab"][(w0_ + j) % 6] = [mmt, lt2]
        return slot, mmt

    def v_pass_gen(c0, xh_tok, xb2):
        for m in range(PC):
            if pool_pos[0] % 2 == 1:
                pool_pos[0] += 1
            w0_ = pool_pos[0] % 6
            pool_pos[0] += 2
            sl = slabP.ap()[:, w0_ * 1024: w0_ * 1024 + 2048]
            wt = [tok["slab"][w0_], tok["slab"][(w0_ + 1) % 6],
                  w_cast[wv_bf.name]]
            lt = pr.dma("sync", sl[:, 0:1024],
                        wv_bf[m * 128:(m + 1) * 128, 0:1024],
                        waits=wt, inc=sSlab[w0_ % 3])
            lt2 = pr.dma("scalar", sl[:, 1024:2048],
                         wv_bf[m * 128:(m + 1) * 128, 1024:2048],
                         waits=wt, inc=sSlabS[(w0_ + 1) % 3])
            mm_both = []
            for ci in range(2):
                cc = c0 + ci
                voff = (cc % 2) * DVT
                slot = pass_i[0] % 4
                pass_i[0] += 1
                mmt = None
                rn = pr.new_run()
                for k in range(PC):
                    mmt = pr.op("tensor",
                                lambda e, kk=k, cci=ci, s=sl, sl2=slot,
                                x2_=xb2:
                                e.matmul(PB4[sl2][:, 0:256],
                                         x2_[:, kk * GR + cci * CH:
                                             kk * GR + cci * CH + CH],
                                         s[:, kk * 256:(kk + 1) * 256],
                                         start=(kk == 0),
                                         stop=(kk == PC - 1)),
                                waits=[lt if k < 4 else lt2, xh_tok,
                                       pb_free[slot], mmt],
                                ns=330, run=rn)
                ev = pr.op("scalar", lambda e, mm=m, sl2=slot, vo=voff:
                           e.activation(
                               v_tm[:, vo + mm * 256: vo + (mm + 1) * 256],
                               PB4[sl2][:, 0:256], AF.Identity),
                           waits=[mmt, tok["v_r"][cc % 2]], inc=sA, ns=480)
                pb_free[slot] = ev
                tok["v_w"][cc % 2] = ev
                mm_both.append(mmt)
            tok["slab"][w0_] = [mm_both, lt2]
            tok["slab"][(w0_ + 1) % 6] = [mm_both, lt2]
            yield

    def g_pass(m, xh_tok, xb2):
        slot, mmt = tile_pass(wg_bf, m, PC,
                              lambda k, x2_=xb2: x2_[:, k * GR:(k + 1) * GR],
                              [xh_tok])
        gl = pr.op("scalar", lambda e, mm=m, sl2=slot: e.activation(
            gsil[:, mm * GR:(mm + 1) * GR], PB4[sl2][:, 0:GR], AF.Silu),
            waits=[mmt, tok["gsil_r"]], inc=sA, ns=520)
        pb_free[slot] = gl
        return gl, mmt

    def wo_pass_gen(ci, cc):
        wo_mms = []
        xms = []
        for m in range(PC):
            slot, mmt = tile_pass(wo_m, m, DPC,
                                  lambda k: og2[:, k * 128:(k + 1) * 128],
                                  [tok["og2_w"][0:2]], nw=128, split=True,
                                  xtra_mid=[tok["og2_w"][2:4]])
            xm = pr.op("vector",
                       lambda e, mm=m, sl2=slot, c2=cc:
                       e.scalar_tensor_tensor(
                           x_res[:, mm * T + c2 * CH:
                                 mm * T + (c2 + 1) * CH],
                           PB4[sl2][:, 0:CH],
                           ga2[:, mm:mm + 1],
                           x_res[:, mm * T + c2 * CH:
                                 mm * T + (c2 + 1) * CH],
                           ALU.mult, ALU.add),
                       waits=[mmt, t_ga], inc=sV, ns=200)
            pb_free[slot] = xm
            wo_mms.append(mmt)
            xms.append(xm)
            tok["og2_r"] = wo_mms
            tok["xmid"][cc] = xms
            yield

    def f1_stats(cc):
        xap = ap(x_res, cc * CH, [[PC * T, 128], [T, PC], [1, CH]])
        rows_tok = ln_stats(xap, [tok["xmid"][cc]])
        st = pr.dma("scalar", stats_dram[:, cc * 256:(cc + 1) * 256],
                    RB[:, 0:256], waits=[rows_tok], inc=sStatO)
        tok["RBst"] = st
        stat_st[cc] = st
        return xap, rows_tok

    def f1_chain(cc, xap, rows_tok):
        tt1 = ln_norm(xap, rows_tok)
        hts = []
        for p in range(PC):
            hts.append(pr.op("vector", lambda e, pp=p: e.tensor_scalar(
                SCR_F[:, pp * CH:(pp + 1) * CH],
                SCR_F[:, pp * CH:(pp + 1) * CH],
                one_cb[:, pp:pp + 1], modv[:, 24 + pp:25 + pp],
                ALU.mult, ALU.add), waits=[tt1, t_ocb], inc=sV))
        rd = pr.op("vector", lambda e: e.reduce_max(
            red_c[:, :], SCR_F[:, :], axis=AX.X,
            apply_absolute_value=True), waits=hts + [tok["redc"]], inc=sV)
        am = pr.op("vector", lambda e: e.tensor_max(
            amax_c[:, :], amax_c[:, :], red_c[:, :]),
            waits=[rd, z_am2], inc=sV)
        tok["redc"] = am
        f1_done.append(am)
        tok["SCRF"] = rd

    def retention_chunk(ci, cc, fillers):
        voff = (cc % 2) * DVT
        og_last = None
        og_alls = []
        sco_readers = []
        head_info = []
        sbf_prev = tok["Sbf_w"]
        for hh in range(H):
            sl_s = pass_i[0] % 4
            pass_i[0] += 1
            mmt = None
            rn = pr.new_run()
            for b in range(2):
                pcq = 2 * hh + b
                mmt = pr.op("tensor",
                            lambda e, pq=pcq, bb=b, cci=ci, s2=sl_s:
                            e.matmul(PB4[s2][:, 0:128],
                                     kt[:, pq * GR + cci * CH:
                                        pq * GR + cci * CH + CH],
                                     qt[:, pq * GR + cci * CH:
                                        pq * GR + cci * CH + CH],
                                     start=(bb == 0), stop=(bb == 1)),
                            waits=[tok["qt_w"], tok["kt_w"], pb_free[sl_s],
                                   mmt], ns=230, run=rn)
            sct = pr.op("vector", lambda e, h2=hh, s2=sl_s:
                        e.scalar_tensor_tensor(
                            sco[:, h2 * 128:(h2 + 1) * 128],
                            PB4[s2][:, 0:128],
                            lamsig_s[:, h2:h2 + 1], mask_s[:, :],
                            ALU.mult, ALU.mult),
                        waits=[mmt, misc_tok, tok["sco_r"]], inc=sV, ns=220)
            pb_free[sl_s] = sct
            tok["sco_w"] = sct
            if fillers:
                fillers[min(hh, len(fillers) - 1)](hh)

            if cc > 0:
                sb_w = [tok["S"], sbf_prev, tok["Sbf_r"], tok["ropeB"]]
                sb1 = pr.op("vector", lambda e, h2=hh: e.tensor_copy(
                    S_bf[:, 0:512], S[:, h2 * 1024:h2 * 1024 + 512]),
                    waits=sb_w, inc=sV, ns=650)
                sb2 = pr.op("scalar", lambda e, h2=hh: e.activation(
                    S_bf[:, 512:1024], S[:, h2 * 1024 + 512:(h2 + 1) * 1024],
                    AF.Identity), waits=sb_w, inc=sA, ns=550)
                sbf_prev = [sb1, sb2]
                tok["Sbf_w"] = sbf_prev
                tok["ropeB"] = sbf_prev

            posl = hh % 2
            otok = None
            rn_o = pr.new_run()
            for b in range(4):
                w0 = [sct, tok["v_w"][cc % 2], po_free[posl], otok]
                if cc == 0:
                    otok = pr.op("tensor",
                                 lambda e, h2=hh, bb=b, pp=posl, vo=voff:
                                 e.matmul(
                                     PO[pp][:, bb * 128:(bb + 1) * 128],
                                     v_tm[:, vo + h2 * DV + bb * 128:
                                          vo + h2 * DV + (bb + 1) * 128],
                                     sco[:, h2 * 128:(h2 + 1) * 128],
                                     start=True, stop=True),
                                 waits=w0, ns=230, run=rn_o)
                else:
                    t_a = pr.op("tensor",
                                lambda e, h2=hh, bb=b, pp=posl, vo=voff:
                                e.matmul(PO[pp][:, bb * 128:(bb + 1) * 128],
                                         v_tm[:, vo + h2 * DV + bb * 128:
                                              vo + h2 * DV + (bb + 1) * 128],
                                         sco[:, h2 * 128:(h2 + 1) * 128],
                                         start=True, stop=False),
                                waits=w0 + [sbf_prev], ns=230, run=rn_o)
                    t_b = pr.op("tensor", lambda e, h2=hh, bb=b, pp=posl,
                                cci=ci:
                                e.matmul(PO[pp][:, bb * 128:(bb + 1) * 128],
                                         S_bf[:, bb * 128:(bb + 1) * 128],
                                         qt[:, (2 * h2) * GR + cci * CH:
                                            (2 * h2) * GR + cci * CH + CH],
                                         start=False, stop=False),
                                waits=[sbf_prev, tok["qt_w"], t_a], ns=230,
                                run=rn_o)
                    otok = pr.op("tensor", lambda e, h2=hh, bb=b,
                                 pp=posl, cci=ci:
                                 e.matmul(
                                     PO[pp][:, bb * 128:(bb + 1) * 128],
                                     S_bf[:, 512 + bb * 128:
                                          512 + (bb + 1) * 128],
                                     qt[:, (2 * h2 + 1) * GR + cci * CH:
                                        (2 * h2 + 1) * GR + cci * CH + CH],
                                     start=False, stop=True),
                                 waits=[t_b], ns=230, run=rn_o)
            tok["Sbf_r"] = otok
            sco_readers.append(otok)

            par = hh % 2
            osoff = 1024 + par * 512
            o2off = par * 512
            ssoff = 128 + par * 256
            oev = pr.op("scalar", lambda e, pp=posl, oo=osoff: e.activation(
                SCR_B[:, oo:oo + 512], PO[pp][:, 0:512], AF.Identity),
                waits=[otok, tok["SB"][2 + par]], inc=sA, ns=600)
            po_free[posl] = oev
            o2_ = pr.op("vector", lambda e, oo=osoff, so=o2off: e.tensor_mul(
                SCR_B[:, so:so + 512], SCR_B[:, oo:oo + 512],
                SCR_B[:, oo:oo + 512]), waits=[oev, tok["SB"][par]],
                inc=sV, ns=380)
            ssm = None
            rn_s = pr.new_run()
            for b in range(4):
                ssm = pr.op("tensor", lambda e, bb=b, so=o2off, sso=ssoff:
                            e.matmul(
                    PX[:, sso:sso + 128], ones_s[:, :],
                    SCR_B[:, so + bb * 128:so + (bb + 1) * 128],
                    start=(bb == 0), stop=(bb == 3)),
                    waits=[o2_, px_chain[0], ssm], ns=230, run=rn_s)
            px_chain[0] = ssm
            tok["SB"][par] = ssm
            head_info.append((hh, ssm, oev))
            if par == 1:
                (h0, ssm0, oev0), (h1, ssm1, oev1) = head_info[-2:]
                ssg = ap(PX, 128, [[512, 128], [256, 2], [1, 128]])
                rbg = ap(RB, 256, [[512, 128], [128, 2], [1, 128]])
                sdt2 = pr.op("scalar", lambda e, i=ssg, o=rbg: e.activation(
                    o, i, AF.Sqrt, bias=consts[:, 0:1],
                    scale=consts[:, 2:3]),
                    waits=[ssm0, ssm1, tok["RBv"], tok["RBs"]], inc=sA,
                    ns=560)
                rst2 = pr.op("vector", lambda e: e.reciprocal(
                    RB[:, 256:512], RB[:, 256:512]), waits=[sdt2], inc=sV,
                    ns=900)
                px_chain[0] = rst2
                for h3, oevx in ((h0, oev0), (h1, oev1)):
                    rboff = 256 + (h3 % 2) * 128
                    osx = 1024 + (h3 % 2) * 512
                    rstb = ap(RB, rboff, [[512, 128], [0, 4], [1, 128]])
                    gslice = ap(gsil, h3 * 4 * GR + ci * CH,
                                [[DPC * GR, 128], [GR, 4], [1, CH]])
                    oslice = ap(og2, h3 * 4 * 128,
                                [[DPC * 128, 128], [128, 4], [1, CH]])
                    g1t = pr.op("vector",
                                lambda e, o=oslice, gs=gslice, rb2=rstb:
                                e.tensor_tensor(o, gs, rb2, ALU.mult),
                                waits=[rst2, [tok["gsil_w"][4 * h3 + j]
                                              for j in range(4)],
                                       tok["og2_r"]], inc=sV, ns=620)
                    og_last = pr.op("vector", lambda e, o=oslice, oo=osx:
                                    e.tensor_tensor(
                                        o, o,
                                        ap(SCR_B, oo,
                                           [[2048, 128], [128, 4], [1, 128]]),
                                        ALU.mult),
                                    waits=[g1t, oevx], inc=sV, ns=620)
                    tok["SB"][2 + h3 % 2] = og_last
                    if h3 % 2 == 0:
                        tok["RBv"] = og_last
                    else:
                        tok["RBs"] = og_last
                    tok["gsil_r"] = og_last
                    og_alls.append(og_last)
        tok["og2_w"] = og_alls
        tok["sco_r"] = sco_readers

    def u_fold_chunk(ci, cc, ktm_last, wo_gen):
        voff = (cc % 2) * DVT
        last_umt = None
        umts = []
        for hh in range(H):
            for b in range(2):
                posl = (hh * 2 + b) % 2
                umt = pr.op("tensor",
                            lambda e, h2=hh, bb=b, pp=posl, vo=voff:
                            e.matmul(PO[pp][:, 0:512],
                                     k_tm[:, h2 * DK + bb * 128:
                                          h2 * DK + (bb + 1) * 128],
                                     v_tm[:, vo + h2 * DV:
                                          vo + (h2 + 1) * DV],
                                     start=True, stop=True),
                            waits=[ktm_last, tok["v_w"][cc % 2],
                                   po_free[posl]], ns=590)
                lc = float(lamh(hh) ** CH)
                fld = pr.op("vector", lambda e, h2=hh, bb=b, lcv=lc,
                            pp=posl: e.scalar_tensor_tensor(
                                S[:, (2 * h2 + bb) * 512:
                                  (2 * h2 + bb + 1) * 512],
                                S[:, (2 * h2 + bb) * 512:
                                  (2 * h2 + bb + 1) * 512],
                                lcv, PO[pp][:, 0:512],
                                ALU.mult, ALU.add),
                            waits=[umt, tok["S"], tok["Sbf_w"]], inc=sV,
                            ns=680)
                po_free[posl] = fld
                tok["S"] = fld
                last_umt = umt
                umts.append(umt)
                if wo_gen is not None:
                    next(wo_gen, None)
        tok["v_r"][cc % 2] = umts
        tok["ktm_r"] = umts
        tok["ropeB"] = tok["S"]
        return last_umt

    def transposes(ci, cc):
        tts = []
        tt_last = None
        rn_t = pr.new_run()
        rn_k = pr.new_run()
        for s8 in range(8):
            tt_last = pr.op("tensor", lambda e, pq=s8, cci=ci:
                            e.transpose(PTR[:, pq * 128:(pq + 1) * 128],
                                        kt[:, pq * GR + cci * CH:
                                           pq * GR + cci * CH + CH],
                                        ident_s[:, :]),
                            waits=[tok["kt_w"], misc_tok, ptr_chain[0],
                                   tt_last], ns=300, run=rn_t)
            tts.append(tt_last)
        ktm_last = None
        for s8 in range(8):
            hh2, bb2 = s8 // 2, s8 % 2
            ktm_last = pr.op("scalar", lambda e, pq=s8, h2=hh2, bb=bb2:
                             e.activation(
                                 k_tm[:, h2 * DK + bb * 128:
                                      h2 * DK + (bb + 1) * 128],
                                 PTR[:, pq * 128:(pq + 1) * 128],
                                 AF.Identity, bias=ZC(),
                                 scale=lamk_s[:, h2:h2 + 1]),
                             waits=[tts[7], misc_tok, tok["ktm_r"],
                                    ktm_last], ns=420, run=rn_k)
        ptr_chain[0] = ktm_last
        return ktm_last

    ln1_xh = {}
    vtn_tok = [z_S]
    pxs_free = {}
    for _b in range(2):
        for _ci in range(2):
            pxs_free[(_b, _ci)] = [bank_chain[_b]]

    def ln1_group(gn):
        bank = gn % 2
        xh_b = xh if gn % 2 == 0 else xh2
        xh_all = []
        for ci in range(2):
            cc = 2 * gn + ci
            ld = pr.dma("sync", RB[:, 0:256],
                        ln1_dram[:, cc * 256:(cc + 1) * 256],
                        waits=[prol_store[cc], tok["RB"], tok["RBst"]],
                        inc=sPI)
            rows_tok = [ld]
            mb = ap(RB, 0, [[512, 128], [0, PC], [1, CH]])
            rb_ = ap(RB, 128, [[512, 128], [0, PC], [1, CH]])
            xapc = ap(x_res, cc * CH, [[PC * T, 128], [T, PC], [1, CH]])
            w_extra = [tok["v_r"][0], tok["og2_w"]] if gn >= 1 else []
            tt0 = pr.op("vector", lambda e, i=xapc, m=mb: e.tensor_tensor(
                VT[:, :].rearrange("p (a b) -> p a b", a=PC), i, m,
                ALU.subtract),
                waits=rows_tok + vtn_tok + w_extra + xloads, inc=sV,
                ns=1250)
            tt1 = pr.op("vector", lambda e, r=rb_: e.tensor_tensor(
                VT[:, :].rearrange("p (a b) -> p a b", a=PC),
                VT[:, :].rearrange("p (a b) -> p a b", a=PC), r, ALU.mult),
                waits=[tt0] + rows_tok, inc=sV, ns=1250)
            tok["RB"] = [tt0, tt1]
            rn_x = pr.new_run()
            xh_ci = []
            for p in range(PC):
                xh_ci.append(pr.op("scalar",
                             lambda e, pp=p, cci=ci, xb2=xh_b:
                             e.activation(
                    xb2[:, pp * GR + cci * CH: pp * GR + cci * CH + CH],
                    VT[:, pp * CH:(pp + 1) * CH], AF.Identity,
                    bias=modv[:, pp:pp + 1], scale=one_ca[:, pp:pp + 1]),
                    waits=[tt1, t_oca, tok["xh2" if gn % 2 else "xh"]],
                    inc=sA, ns=420, run=rn_x))
            xh_all.extend(xh_ci)
            vtn_tok[:] = [xh_ci]
        ln1_xh[gn] = xh_all

    ln1_group(0)

    for g in range(NG):
        t0g = g * GR
        c0 = 2 * g
        tb1 = pr.dma("sync", cos_sl[:, :], cosT[:, t0g:t0g + GR],
                     waits=[tok["tab"]], inc=sTab)
        tb2 = pr.dma("sync", sin_sl[:, :], sinT[:, t0g:t0g + GR],
                     waits=[tok["tab"]], inc=sTab)
        ttab = [tb1, tb2]

        xh_b = xh if g % 2 == 0 else xh2
        xh_tok = ln1_xh[g]

        def proj_rope(wsrc, dst, dst_free, do_lam):
            outs = []
            pend = {}
            for m in range(PC):
                slot, mmt = tile_pass(wsrc, m, PC,
                                      lambda k, x2_=xh_b:
                                      x2_[:, k * GR:(k + 1) * GR],
                                      [xh_tok])
                pend[m] = (slot, mmt)
                if m % 2 == 0:
                    continue
                hh = m // 2
                sA_, mmA = pend[m - 1]
                sB_, mmB = pend[m]
                o1 = dst[:, (2 * hh) * GR:(2 * hh + 1) * GR]
                o2 = dst[:, (2 * hh + 1) * GR:(2 * hh + 2) * GR]
                rB1 = ropeB[:, 0:GR]
                rB2 = ropeB[:, GR:2 * GR]
                p1 = pr.op("vector", lambda e, pa=sA_, o=o1: e.tensor_tensor(
                    o, PB4[pa][:, 0:GR], cos_sl[:, :], ALU.mult),
                    waits=[mmA] + ttab + [dst_free], inc=sV, ns=300)
                p3 = pr.op("vector", lambda e, pa=sA_, o=o2: e.tensor_tensor(
                    o, PB4[pa][:, 0:GR], sin_sl[:, :], ALU.mult),
                    waits=[mmA] + ttab + [dst_free, p1], inc=sV, ns=300)
                q1 = pr.op("vector", lambda e, pb=sB_, r=rB1: e.tensor_tensor(
                    r, PB4[pb][:, 0:GR], sin_sl[:, :], ALU.mult),
                    waits=[mmB] + ttab + [tok["ropeB"]], inc=sV, ns=300)
                q2 = pr.op("vector", lambda e, pb=sB_, r=rB2: e.tensor_tensor(
                    r, PB4[pb][:, 0:GR], cos_sl[:, :], ALU.mult),
                    waits=[mmB] + ttab + [tok["ropeB"], q1], inc=sV, ns=300)
                p2 = pr.op("vector", lambda e, o=o1, r=rB1: e.tensor_tensor(
                    o, o, r, ALU.subtract), waits=[p1, q1], inc=sV, ns=300)
                p4 = pr.op("vector", lambda e, o=o2, r=rB2: e.tensor_tensor(
                    o, o, r, ALU.add), waits=[p3, q2], inc=sV, ns=300)
                pb_free[sA_] = [p1, p3]
                pb_free[sB_] = [q1, q2]
                tok["ropeB"] = [p2, p4]
                outs += [p2, p4]
            if do_lam:
                qall = ap(dst, 0, [[PC * GR, 128], [GR, PC],
                                   [CH, GR // CH], [1, CH]])
                lm = ap(lamq_s, 0, [[1024, 128], [128, PC],
                                    [0, GR // CH], [1, CH]])
                outs = [pr.op("vector", lambda e: e.tensor_tensor(
                    qall, qall, lm, ALU.mult),
                    waits=outs + [misc_tok], inc=sV, ns=800)]
            return outs

        tok["qt_w"] = proj_rope(wq_bf, qt, tok["qt_r"], True)
        tok["kt_w"] = proj_rope(wk_bf, kt, tok["kt_r"], False)

        tok["gsil_w"] = {}
        for m in range(4):
            gl, mmt = g_pass(m, xh_tok, xh_b)
            tok["gsil_w"][m] = gl
            xh_free = mmt
        g_next = [4]

        for _ in v_pass_gen(c0, xh_tok, xh_b):
            pass

        ktm_l = transposes(0, c0)

        def fill_c0(hh):
            while g_next[0] < min(4 * hh + 8, DPC):
                gl, _ = g_pass(g_next[0], xh_tok, xh_b)
                tok["gsil_w"][g_next[0]] = gl
                g_next[0] += 1

        retention_chunk(0, c0, [fill_c0] * 4)
        while g_next[0] < DPC:
            gl, _ = g_pass(g_next[0], xh_tok, xh_b)
            tok["gsil_w"][g_next[0]] = gl
            g_next[0] += 1
        wo0 = wo_pass_gen(0, c0)
        u_fold_chunk(0, c0, ktm_l, wo0)
        for _ in wo0:
            pass
        if g + 1 < NG:
            ln1_group(g + 1)
        xap0, rows0 = f1_stats(c0)

        ktm_l = transposes(1, c0 + 1)
        f1_emitted = [False]

        def fill_c1(hh):
            if not f1_emitted[0]:
                f1_emitted[0] = True
                f1_chain(c0, xap0, rows0)

        retention_chunk(1, c0 + 1, [fill_c1] * 4)
        wo1 = wo_pass_gen(1, c0 + 1)
        u_fold_chunk(1, c0 + 1, ktm_l, wo1)
        for _ in wo1:
            pass
        xap1, rows1 = f1_stats(c0 + 1)
        f1_chain(c0 + 1, xap1, rows1)

        tok["tab"] = tok["kt_w"]
        tok["qt_r"] = tok["v_r"][1]
        tok["kt_r"] = tok["v_r"][1]
        tok["xh"] = tok["v_r"][1]
        tok["xh2"] = tok["v_r"][1]

    e1 = pr.dma("scalar", cc_in[0, :], amax_c[:, 0], waits=f1_done, inc=sB1)
    cctok = pr.op("gpsimd", lambda e: e.collective_compute(
        "AllReduce", ALU.max, replica_groups=[list(range(n_cores))],
        ins=[cc_in.ap()], outs=[cc_out.ap()]), waits=[e1], inc=sCC,
        async_op=True)
    e2 = pr.dma("sync", RB[0:1, 256:384], cc_out[:, :],
                waits=[cctok, tok["RBv"], tok["RBs"]], inc=sB2)
    q1 = pr.op("vector", lambda e: e.reduce_max(
        small[0:1, 8:9], RB[0:1, 256:384], axis=AX.X), waits=[e2], inc=sV)
    q2 = pr.op("vector", lambda e: e.tensor_scalar_max(
        small[0:1, 9:10], small[0:1, 8:9], 1e-12), waits=[q1], inc=sV)
    q3 = pr.op("vector", lambda e: e.reciprocal(
        small[0:1, 10:11], small[0:1, 9:10]), waits=[q2], inc=sV)
    q4 = pr.op("vector", lambda e: e.tensor_scalar(
        small[0:1, 11:12], small[0:1, 10:11], FP8_MAX, FP8_MAX,
        ALU.mult, ALU.min), waits=[q3], inc=sV)
    q5 = pr.op("vector", lambda e: e.reciprocal(
        small[0:1, 12:13], small[0:1, 11:12]), waits=[q4], inc=sV)
    q6 = pr.op("vector", lambda e: e.tensor_mul(
        small[0:1, 13:14], small[0:1, 12:13], small[0:1, 4:5]),
        waits=[q5], inc=sV)
    e3 = pr.dma("scalar", srow[0:1, 2:3], small[0:1, 11:12], waits=[q4],
                inc=sB1)
    e4 = pr.dma("scalar", srow[0:1, 4:5], small[0:1, 13:14], waits=[q6],
                inc=sB1)
    e5 = pr.dma("sync", small[:, 14:15], ap(srow, 2, [[0, 128], [1, 1]]),
                waits=[e3], inc=sB2)
    e6 = pr.dma("sync", small[:, 15:16], ap(srow, 4, [[0, 128], [1, 1]]),
                waits=[e4], inc=sB2)
    b1 = pr.op("vector", lambda e: e.tensor_scalar_mul(
        ocbs[:, :], one_cb[:, :], small[:, 14:15]), waits=[e5, t_ocb], inc=sV)
    b2 = pr.op("vector", lambda e: e.tensor_scalar_mul(
        sbs[:, :], modv[:, 24:32], small[:, 14:15]), waits=[e5], inc=sV)

    tok["VT"] = tok["v_r"][0] + tok["v_r"][1]
    tok["VT2"] = tok["VT"]
    tok["scoF"] = None
    tok_os = [tok["qt_r"], tok["kt_r"]]
    og2h_r = [tok["og2_r"], tok["gsil_r"]]
    og2h_w = [tok["og2_w"], tok["gsil_r"]]
    h2q_buf = [og2, gsil]
    h2q_off = [0, 2048]

    def f2_chains(g):
        hbuf = h2q_buf[g % 2]
        hoff = h2q_off[g % 2]
        toks = []
        for ci in range(2):
            cc = 2 * g + ci
            xap = ap(x_res, cc * CH, [[PC * T, 128], [T, PC], [1, CH]])
            ld = pr.dma("sync", sco_f[:, 0:256],
                        stats_dram[:, cc * 256:(cc + 1) * 256],
                        waits=[stat_st[cc], tok["scoF"], tok["sco_r"]],
                        inc=sStatI)
            mb = ap(sco_f, 0, [[256, 128], [0, PC], [1, CH]])
            rb_ = ap(sco_f, 128, [[256, 128], [0, PC], [1, CH]])
            tt0 = pr.op("vector", lambda e, i=xap, m=mb: e.tensor_tensor(
                ap(SCR_F, 0, [[1024, 128], [CH, PC], [1, CH]]), i, m,
                ALU.subtract), waits=[ld, tok["SCRF"]], inc=sV,
                ns=1250)
            tt1 = pr.op("vector", lambda e, r=rb_: e.tensor_tensor(
                ap(SCR_F, 0, [[1024, 128], [CH, PC], [1, CH]]),
                ap(SCR_F, 0, [[1024, 128], [CH, PC], [1, CH]]), r, ALU.mult),
                waits=[tt0], inc=sV, ns=1250)
            tok["scoF"] = [tt0, tt1]
            hts = []
            rn_h = pr.new_run()
            for p in range(PC):
                hts.append(pr.op("scalar", lambda e, pp=p: e.activation(
                    VT[:, pp * CH:(pp + 1) * CH],
                    SCR_F[:, pp * CH:(pp + 1) * CH], AF.Identity,
                    bias=sbs[:, pp:pp + 1], scale=ocbs[:, pp:pp + 1]),
                    waits=[tt1, b1, b2, tok["VT"]], inc=sA, ns=420,
                    run=rn_h))
            aa = pr.op("scalar", lambda e: e.activation(
                SCR_F[:, :], VT[:, :], AF.Identity, bias=ZC(),
                scale=consts[:, 3:4]), waits=hts + [tt1], inc=sA, ns=900)
            bb_ = pr.op("vector", lambda e: e.tensor_sub(
                VT2[:, :], SCR_F[:, :], VT[:, :]),
                waits=[aa, tok["VT2"]], inc=sV, ns=1250)
            tok["VT"] = [hts, aa, bb_]
            outap = ap(hbuf, hoff + ci * CH,
                       [[hbuf.shape[1], 128], [GR, PC], [1, CH]])
            vlast = pr.op("vector", lambda e, o=outap: e.tensor_tensor(
                o, SCR_F[:, :].rearrange("p (a b) -> p a b", a=PC),
                VT2[:, :].rearrange("p (a b) -> p a b", a=PC),
                ALU.subtract),
                waits=[bb_, og2h_r[g % 2], og2h_w[g % 2]], inc=sV,
                ns=1250)
            tok["VT2"] = vlast
            tok["SCRF"] = [bb_, vlast]
            toks.append(vlast)
        og2h_w[g % 2] = toks
        return toks

    h2q_pend = f2_chains(0)
    for g in range(NG):
        c0 = 2 * g
        hbuf = h2q_buf[g % 2]
        hoff = h2q_off[g % 2]
        h2q_toks = h2q_pend
        ffn_mms = []
        evs = []
        for m in range(PC):
            slot, mmt = tile_pass(wf_q, m, PC,
                                  lambda k, hb=hbuf, bb=hoff:
                                  hb[:, bb + k * GR: bb + (k + 1) * GR],
                                  h2q_toks + [wfq_done])
            ffn_mms.append(mmt)
            ev = pr.op("scalar", lambda e, mm=m, sl2=slot: e.activation(
                gsil[:, mm * GR:(mm + 1) * GR], PB4[sl2][:, 0:GR],
                AF.Identity, bias=bf_s[:, mm:mm + 1], scale=small[:, 15:16]),
                waits=[mmt, e6, misc_tok, tok["gsil_r"]], inc=sA, ns=480)
            pb_free[slot] = ev
            evs.append(ev)
        og2h_r[g % 2] = ffn_mms
        if g + 1 < NG:
            h2q_pend = f2_chains(g + 1)
        rlast = [[], []]
        for m in range(PC):
            for ci in range(2):
                cc = c0 + ci
                rlast[ci].append(pr.op(
                    "vector", lambda e, mm=m, cci=ci, c2=cc, os=OS[ci]:
                    e.scalar_tensor_tensor(
                        os[:, mm * CH:(mm + 1) * CH],
                        gsil[:, mm * GR + cci * CH: mm * GR + cci * CH + CH],
                        gb2[:, mm:mm + 1],
                        x_res[:, mm * T + c2 * CH: mm * T + (c2 + 1) * CH],
                        ALU.mult, ALU.add),
                    waits=[evs[m], t_gb, tok_os[ci]], inc=sV, ns=200))
        tok["gsil_r"] = rlast[0] + rlast[1]
        for ci in range(2):
            cc = c0 + ci
            outap = ap(yT, cc * CH, [[T, 128], [128 * T, PC], [1, CH]])
            out_tok[0] = pr.op("scalar", lambda e, o=outap, i=OS[ci],
                               cci=ci: e.dma_start(
                                   out=o,
                                   in_=i[:, :].rearrange(
                                       "p (a b) -> p a b", a=PC)),
                               waits=rlast[ci], inc=sOout, amt=16)
            tok_os[ci] = out_tok[0]

    pr.op("scalar", lambda e: e.nop(), waits=[out_tok[0]])

    pr.emit()
    return nc, sbuf_left


def _host_constants(T):
    f = np.arange(128, dtype=np.float64)
    inv_freq = ROPE_BASE ** (-f / 128.0)
    t = np.arange(T, dtype=np.float64)
    ang = inv_freq[:, None] * t[None, :]
    lams = np.array([lamh(h) for h in range(H)], dtype=np.float64)
    tau = np.arange(CH, dtype=np.float64)
    lamq = np.zeros((128, 1024), dtype=np.float64)
    for pc in range(PC):
        lamq[:, pc * 128:(pc + 1) * 128] = (lams[pc // 2] ** (tau + 1.0)) / 16.0
    sig = np.arange(128, dtype=np.float64)
    consts = np.zeros((128, 8), dtype=np.float32)
    consts[:, 0] = EPS
    consts[:, 1] = 1.0 / D
    consts[:, 2] = 1.0 / DV
    consts[:, 3] = VELT_C
    consts[:, 4] = 1e-12
    consts[:, 5] = FP8_MAX
    consts[:, 7] = 1.0
    return dict(
        cosT=np.cos(ang).astype(BF), sinT=np.sin(ang).astype(BF),
        lamqI=lamq.astype(BF),
        lamsigI=np.stack([lams[h] ** (-(sig + 1.0)) for h in range(H)],
                         axis=1).astype(np.float32),
        lamkI=np.stack([lams[h] ** (127.0 - sig) for h in range(H)],
                       axis=1).astype(np.float32),
        maskI=np.triu(np.ones((128, 128))).astype(BF),
        identI=np.eye(128).astype(BF),
        onesI=np.ones((128, 128)).astype(BF),
        constsI=consts)


def _tile(WT, mwidth):
    IN, OUT = WT.shape
    nm, nk = OUT // mwidth, IN // 128
    return np.ascontiguousarray(
        WT.reshape(nk, 128, nm, mwidth).transpose(2, 1, 0, 3)
        .reshape(nm * 128, nk * mwidth))


def prep_inputs(inputs, T, n_cores):
    consts = _host_constants(T)
    f32 = lambda k: np.asarray(inputs[k], dtype=np.float32)
    x = f32("L0_attn_x")
    gnw_full = np.tile(f32("L0_attn_g_norm_w"), H)
    shared = dict(
        wq_t=_tile(np.ascontiguousarray(f32("L0_attn_Wq").T), 128),
        wk_t=_tile(np.ascontiguousarray(f32("L0_attn_Wk").T), 128),
        wv_t=_tile(np.ascontiguousarray(f32("L0_attn_Wv").T), 256),
        wg_t=_tile(np.ascontiguousarray(f32("L0_attn_Wg").T), 128),
        woT=np.ascontiguousarray(f32("L0_attn_Wo").T),
        wf_t=_tile(np.ascontiguousarray(f32("L0_ffn_weight").T), 128),
        wmod=np.ascontiguousarray(f32("L0_wmod")),
        gnw=np.ascontiguousarray(gnw_full.reshape(DPC, 128).T),
        bfc=np.ascontiguousarray(f32("L0_ffn_bias").reshape(PC, 128).T),
        **consts)
    cmat = f32("L0_c")
    in_maps = []
    for b in range(n_cores):
        m = dict(shared)
        m["xT"] = np.ascontiguousarray(x[b].T)
        m["cT"] = np.ascontiguousarray(cmat[b].reshape(PC, 128).T)
        in_maps.append(m)
    return in_maps


def kernel(**inputs):
    T = int(np.asarray(inputs["L0_attn_x"]).shape[1])
    n_cores = 8
    in_maps = prep_inputs(inputs, T, n_cores)
    nc, _ = build(T, n_cores)
    res = run_bass_kernel_spmd(nc, in_maps, core_ids=list(range(n_cores)))
    outs = [np.ascontiguousarray(np.asarray(res.results[b]["yT"]).T)
            for b in range(n_cores)]
    return np.stack(outs, axis=0).astype(np.float32)


# version 37
# speedup vs baseline: 1.0292x; 1.0029x over previous
import numpy as np
import ml_dtypes

import concourse.bass as bass
import concourse.mybir as mybir
from concourse.bass_isa import ReduceOp
from concourse import library_config
from concourse.bass_utils import run_bass_kernel_spmd

F32 = mybir.dt.float32
BF16 = mybir.dt.bfloat16
AF = mybir.ActivationFunctionType
ALU = mybir.AluOpType
AX = mybir.AxisListType
BF = ml_dtypes.bfloat16

ENGINES = ("sync", "scalar", "vector", "tensor", "gpsimd")

D = 1024
H = 4
DK = 256
DV = 512
DVT = H * DV
PC = D // 128
DPC = DVT // 128
CH = 128
GR = 256
ROPE_BASE = 10000.0
EPS = 1e-5
VELT_C = float(2 ** 20 + 1)
FP8_MAX = 448.0


def lamh(h):
    return 1.0 - 2.0 ** (-5.0 - h)


class Sem:
    def __init__(self, nc, name, dma=False):
        self.name = name
        self.dma = dma
        self.n = 0


class Prog:

    def __init__(self, nc):
        self.nc = nc
        self.n_ops = 0
        self.engine = []
        self.fn = []
        self.deps = []
        self.dur = []
        self.isdma = []
        self.isasync = []
        self.tag = []
        self.run = []
        self.boost = []
        self.n_runs = 0
        self.sched = None

    def new_run(self):
        self.n_runs += 1
        return self.n_runs

    def sem(self, name, dma=False):
        return Sem(self.nc, name, dma)

    def op(self, eng, fn, waits=(), inc=None, amt=1, ns=None, dma=False,
           async_op=False, run=None, boost=0.0):
        flat = set()

        def _fl(w):
            if w is None:
                return
            if isinstance(w, (list, tuple)):
                for x in w:
                    _fl(x)
            else:
                flat.add(w)
        _fl(list(waits))
        deps = sorted(flat)
        i = self.n_ops
        self.n_ops += 1
        isdma = dma or amt == 16
        self.isasync.append(isdma or async_op)
        self.run.append(None if isdma else run)
        self.boost.append(boost)
        self.engine.append(eng)
        self.fn.append(fn)
        self.deps.append(deps)
        self.dur.append(ns if ns is not None else (1500 if isdma else 350))
        self.isdma.append(isdma)
        self.tag.append(inc if isdma else None)
        return i

    def dma(self, eng, out, in_, waits=(), inc=None, boost=0.0):
        assert out.size() == in_.size(), (
            f"dma size mismatch {out.size()} vs {in_.size()} "
            f"out={out.tensor.name} in={in_.tensor.name}")
        assert inc is not None
        ns = 900 + in_.nbytes() // 300
        return self.op(eng, lambda e, o=out, i=in_: e.dma_start(out=o, in_=i),
                       waits=waits, inc=inc, ns=ns, dma=True, boost=boost)

    def emit(self, reorder=True):
        n = self.n_ops
        mid_of = [0] * n
        members = []
        runmap = {}
        for i in range(n):
            r = self.run[i]
            if r is not None and (self.engine[i], r) in runmap:
                m = runmap[(self.engine[i], r)]
                members[m].append(i)
                mid_of[i] = m
            else:
                m = len(members)
                members.append([i])
                mid_of[i] = m
                if r is not None:
                    runmap[(self.engine[i], r)] = m
        nm = len(members)
        mdeps = [set() for _ in range(nm)]
        for i in range(n):
            for d in self.deps[i]:
                if mid_of[d] != mid_of[i]:
                    mdeps[mid_of[i]].add(mid_of[d])
        mdur = [sum(self.dur[i] for i in mem) for mem in members]
        meng = [self.engine[mem[0]] for mem in members]
        misdma = [self.isdma[mem[0]] for mem in members]
        mchild = [[] for _ in range(nm)]
        for m in range(nm):
            for d in mdeps[m]:
                mchild[d].append(m)
        mprio = [0.0] * nm
        mboost = [max(self.boost[i] for i in mem) for mem in members]
        for m in range(nm - 1, -1, -1):
            mx = 0.0
            for c in mchild[m]:
                if mprio[c] > mx:
                    mx = mprio[c]
            mprio[m] = mdur[m] + mx + mboost[m]
        import heapq
        engs = sorted(set(self.engine))
        etime = {e: 0.0 for e in engs}
        done_t = [0.0] * nm
        ndep = [len(mdeps[m]) for m in range(nm)]
        ready = {e: [] for e in engs}
        for m in range(nm):
            if ndep[m] == 0:
                heapq.heappush(ready[meng[m]], (-mprio[m], m))
        morder = []
        scheduled = 0
        if not reorder:
            morder = list(range(nm))
        else:
            while scheduled < nm:
                best = None
                for e in engs:
                    if not ready[e]:
                        continue
                    pr_, m = ready[e][0]
                    st = max(etime[e],
                             max((done_t[d] + (250.0 if meng[d] != e else 0.0)
                                  for d in mdeps[m]), default=0.0))
                    key = (st, pr_)
                    if best is None or key < best[0]:
                        best = (key, e, m)
                assert best is not None, "scheduler stuck"
                _, e, m = best
                heapq.heappop(ready[e])
                st = max(etime[e],
                         max((done_t[d] + (250.0 if meng[d] != e else 0.0)
                              for d in mdeps[m]), default=0.0))
                if misdma[m]:
                    etime[e] = st + 150.0
                    done_t[m] = st + mdur[m]
                else:
                    etime[e] = st + mdur[m]
                    done_t[m] = etime[e]
                morder.append(m)
                scheduled += 1
                for c in mchild[m]:
                    ndep[c] -= 1
                    if ndep[c] == 0:
                        heapq.heappush(ready[meng[c]], (-mprio[c], c))
        order = []
        for m in morder:
            order.extend(members[m])
        streams = {e: [] for e in engs}
        for i in order:
            streams[self.engine[i]].append(i)

        def semkey(i):
            return (self.engine[i], self.tag[i]) if self.isdma[i] \
                else (self.engine[i], None)

        def sem_dep(d, i):
            return self.isasync[d] or self.isasync[i] or \
                not (self.engine[d] == self.engine[i] == "tensor")

        need_inc = [False] * n
        for i in range(n):
            for d in self.deps[i]:
                if sem_dep(d, i):
                    need_inc[d] = True
        for i in range(n):
            if self.isasync[i]:
                need_inc[i] = True

        sems = {}
        for i in order:
            if not need_inc[i]:
                continue
            k = semkey(i)
            if k not in sems:
                nm = f"s{len(sems)}_{k[0]}" + ("d" if self.isdma[i] else "")
                sems[k] = self.nc.alloc_semaphore(nm)
        cum = [0] * n
        scount = {}
        for e in engs:
            for i in streams[e]:
                k = semkey(i)
                if need_inc[i]:
                    scount[k] = scount.get(k, 0) + \
                        (16 if self.isdma[i] else 1)
                cum[i] = scount.get(k, 0)
        waited = {e: {} for e in engs}

        def wait_list(i):
            out = []
            for d in self.deps[i]:
                if not sem_dep(d, i):
                    continue
                out.append((semkey(d), cum[d]))
            if self.isdma[i] and cum[i] > 16:
                out.append((semkey(i), cum[i] - 16))
            return out

        import os
        if os.environ.get("PROG_DEBUG"):
            import pickle
            with open("/tmp/prog_dbg.pkl", "wb") as f:
                pickle.dump(dict(order=order, streams=streams, cum=cum,
                                 need_inc=need_inc, deps=self.deps,
                                 engine=self.engine, isdma=self.isdma,
                                 tag={i: str(self.tag[i]) for i in range(n)
                                      if self.isdma[i]}), f)
        with self.nc.Block() as blk:
            for eng_name in engs:
                ops_ = streams[eng_name]
                if not ops_:
                    continue

                def make(ops_=ops_, eng_name=eng_name):
                    def body(eng):
                        w8 = waited[eng_name]
                        for i in ops_:
                            for key, v in wait_list(i):
                                if w8.get(key, 0) < v:
                                    eng.wait_ge(sems[key], v)
                                    w8[key] = v
                            ins = self.fn[i](eng)
                            if need_inc[i]:
                                ins.then_inc(sems[semkey(i)],
                                             16 if self.isdma[i] else 1)
                    return body

                getattr(blk, eng_name)(make())
def ap(t, off, dims):
    return bass.AP(t, off, [list(d) for d in dims])


def build(T, n_cores):
    NG = T // GR
    nc = bass.Bass("TRN2", target_bir_lowering=False)

    xT = nc.dram_tensor("xT", [D, T], F32, kind="ExternalInput")
    wq_t = nc.dram_tensor("wq_t", [D, D], F32, kind="ExternalInput")
    wk_t = nc.dram_tensor("wk_t", [D, D], F32, kind="ExternalInput")
    wv_t = nc.dram_tensor("wv_t", [D, DVT], F32, kind="ExternalInput")
    wg_t = nc.dram_tensor("wg_t", [DVT, D], F32, kind="ExternalInput")
    woT = nc.dram_tensor("woT", [DVT, D], F32, kind="ExternalInput")
    wf_t = nc.dram_tensor("wf_t", [D, D], F32, kind="ExternalInput")
    wmod = nc.dram_tensor("wmod", [D, 6 * D], F32, kind="ExternalInput")
    cT = nc.dram_tensor("cT", [128, PC], F32, kind="ExternalInput")
    gnw = nc.dram_tensor("gnw", [128, DPC], F32, kind="ExternalInput")
    bfc = nc.dram_tensor("bfc", [128, PC], F32, kind="ExternalInput")
    cosT = nc.dram_tensor("cosT", [128, T], BF16, kind="ExternalInput")
    sinT = nc.dram_tensor("sinT", [128, T], BF16, kind="ExternalInput")
    lamqI = nc.dram_tensor("lamqI", [128, 1024], BF16, kind="ExternalInput")
    lamsigI = nc.dram_tensor("lamsigI", [128, H], F32, kind="ExternalInput")
    lamkI = nc.dram_tensor("lamkI", [128, H], F32, kind="ExternalInput")
    maskI = nc.dram_tensor("maskI", [128, 128], BF16, kind="ExternalInput")
    identI = nc.dram_tensor("identI", [128, 128], BF16, kind="ExternalInput")
    onesI = nc.dram_tensor("onesI", [128, 128], BF16, kind="ExternalInput")
    constsI = nc.dram_tensor("constsI", [128, 8], F32, kind="ExternalInput")

    yT = nc.dram_tensor("yT", [D, T], F32, kind="ExternalOutput")

    wq_bf = nc.dram_tensor("wq_bf", [D, D], BF16)
    wk_bf = nc.dram_tensor("wk_bf", [D, D], BF16)
    wv_bf = nc.dram_tensor("wv_bf", [D, DVT], BF16)
    wg_bf = nc.dram_tensor("wg_bf", [DVT, D], BF16)
    wo_m = nc.dram_tensor("wo_m", [PC * 128, DVT], BF16)
    wf_q = nc.dram_tensor("wf_q", [D, D], BF16)
    mod_dram = nc.dram_tensor("mod_dram", [1, 6 * D], F32)
    wrow = nc.dram_tensor("wrow", [1, 128], F32)
    srow = nc.dram_tensor("srow", [1, 8], F32)
    cc_in = nc.dram_tensor("cc_in", [1, 128], F32)
    cc_out = nc.dram_tensor("cc_out", [1, 128], F32)
    stats_dram = nc.dram_tensor("stats_dram", [128, (T // CH) * 256], F32)
    ln1_dram = nc.dram_tensor("ln1_dram", [128, (T // CH) * 256], F32)

    A = nc.alloc_sbuf_tensor
    x_res = A("x_res", [128, PC * T], F32)
    slabP = A("slabP", [128, 6144], BF16)
    xh = A("xh", [128, PC * GR], BF16)
    xh2 = xh
    qt = A("qt", [128, PC * GR], BF16)
    kt = A("kt", [128, PC * GR], BF16)
    v_tm = A("v_tm", [128, 2 * DVT], BF16)
    gsil = A("gsil", [128, DPC * GR], BF16)
    og2 = A("og2", [128, DPC * 128], BF16)
    S = A("S", [128, H * 2 * DV], F32)
    k_tm = A("k_tm", [128, H * DK], BF16)
    sco_f = A("sco_f", [128, 256], F32)
    sco = sco_f.ap().bitcast(BF16)
    ropeB = A("ropeB", [128, H * GR], BF16)
    cos_sl = A("cos_sl", [128, GR], BF16)
    sin_sl = A("sin_sl", [128, GR], BF16)
    lamq_s = A("lamq_s", [128, 1024], BF16)
    mask_s = A("mask_s", [128, 128], BF16)
    ident_s = A("ident_s", [128, 128], BF16)
    ones_s = A("ones_s", [128, 128], BF16)
    lamsig_s = A("lamsig_s", [128, H], F32)
    lamk_s = A("lamk_s", [128, H], F32)
    consts = A("consts", [128, 8], F32)
    modv = A("modv", [128, 48], F32)
    one_ca = A("one_ca", [128, PC], F32)
    one_cb = A("one_cb", [128, PC], F32)
    ga2 = A("ga2", [128, PC], F32)
    gb2 = A("gb2", [128, PC], F32)
    ocbs = A("ocbs", [128, PC], F32)
    sbs = A("sbs", [128, PC], F32)
    gnw_s = A("gnw_s", [128, DPC], F32)
    bf_s = A("bf_s", [128, PC], F32)
    cT_s = A("cT_s", [128, PC], F32)
    silu_c = A("silu_c", [128, PC], F32)
    RB = A("RB", [128, 512], F32)
    SCR_F = A("SCR_F", [128, 1024], F32)
    SCR_B = A("SCR_B", [128, 2048], BF16)
    amax_c = A("amax_c", [128, 1], F32)
    red_c = A("red_c", [128, 1], F32)
    small = A("small", [128, 16], F32)
    S_bf = ropeB
    VT = v_tm.ap().bitcast(F32)[:, 0:1024]
    VT2 = v_tm.ap().bitcast(F32)[:, 1024:2048]
    OS = [qt.ap().bitcast(F32), kt.ap().bitcast(F32)]

    sbuf_left = nc.sbuf_bytes_remaining

    P = nc.alloc_psum_tensor
    PB = [P("PB0", [128, 512], F32), P("PB1", [128, 512], F32)]
    PO = [P("POA", [128, 512], F32), P("POB", [128, 512], F32)]
    PX = P("PX", [128, 512], F32)
    PTR = P("PTR", [128, 1024], BF16)
    PXS = [P("PX2", [128, 512], F32), P("PX3", [128, 512], F32)]

    pr = Prog(nc)
    sMisc = pr.sem("sMisc", dma=True)
    sXq = [pr.sem("sX0", dma=True), pr.sem("sX1", dma=True),
           pr.sem("sX2", dma=True)]
    sTab = pr.sem("sTab", dma=True)
    sSlab = [pr.sem("sSlab0", dma=True), pr.sem("sSlab1", dma=True),
             pr.sem("sSlab2", dma=True)]
    sSlabX = [pr.sem("sSlabX0", dma=True), pr.sem("sSlabX1", dma=True),
              pr.sem("sSlabX2", dma=True)]
    sCF = pr.sem("sCF", dma=True)
    sCG = pr.sem("sCG", dma=True)
    sOF = pr.sem("sOF", dma=True)
    sOG = pr.sem("sOG", dma=True)
    sOmod = pr.sem("sOmod", dma=True)
    sOout = pr.sem("sOout", dma=True)
    sG = pr.sem("sG", dma=True)
    sStatO = pr.sem("sStatO", dma=True)
    sStatI = pr.sem("sStatI", dma=True)
    sPO = pr.sem("sPO", dma=True)
    sPI = pr.sem("sPI", dma=True)
    sWM1 = pr.sem("sWM1", dma=True)
    sWM2 = pr.sem("sWM2", dma=True)
    sB1 = pr.sem("sB1", dma=True)
    sB2 = pr.sem("sB2", dma=True)
    sSlabS = [pr.sem("sSlabS0", dma=True), pr.sem("sSlabS1", dma=True),
              pr.sem("sSlabS2", dma=True)]
    sGP = pr.sem("sGP")
    sCC = pr.sem("sCC")
    sT = pr.sem("sT")
    sA = pr.sem("sA")
    sV = pr.sem("sV")

    ZC = lambda: consts[:, 6:7]
    OC = lambda: consts[:, 7:8]

    misc_tok = []
    for dst, src in [(consts, constsI), (mask_s, maskI), (ident_s, identI),
                     (ones_s, onesI), (lamq_s, lamqI), (lamsig_s, lamsigI),
                     (lamk_s, lamkI), (gnw_s, gnw), (bf_s, bfc), (cT_s, cT)]:
        misc_tok.append(pr.dma("sync", dst[:, :], src[:, :], inc=sMisc,
                                boost=1e7))

    xloads = []
    for p in range(PC):
        xq = ["sync", "scalar", "gpsimd"][p % 3]
        xloads.append(pr.dma(xq, x_res[:, p * T:(p + 1) * T],
                             xT[p * 128:(p + 1) * 128, :],
                             inc=sXq[p % 3], boost=1e7))

    z_am = pr.op("vector", lambda e: e.memset(amax_c[:, :], 0.0), inc=sV)

    w_cast = {}
    _bsts = [4e6, 3e6, 2e6, 1e6]
    for _j, (csrc, cdst) in enumerate([(wq_t, wq_bf), (wk_t, wk_bf),
                                       (wv_t, wv_bf), (wg_t, wg_bf)]):
        w_cast[cdst.name] = pr.dma("gpsimd", cdst[:, :], csrc[:, :],
                                   inc=sG, boost=_bsts[_j])
    cast_jobs = []
    for r in range(DVT // 128):
        cast_jobs.append((woT, None, r, 0, r))

    scr = [SCR_F, VT]
    sC = [sCF, sCG]
    sO2 = [sOF, sOG]
    scr_tok = [None, None]
    sb_tok = [None, None]
    cast_last = [None, None]
    for j, (src, dst, r, c0, gcol) in enumerate(cast_jobs):
        s2 = j % 2
        lt = pr.dma("sync", scr[s2][:, :],
                    src[r * 128:(r + 1) * 128, c0:c0 + 1024],
                    waits=[scr_tok[s2]], inc=sC[s2])
        dd = SCR_B[:, s2 * 1024:(s2 + 1) * 1024]
        if gcol is None:
            ct = pr.op("vector",
                       lambda e, o=dd, i=scr[s2][:, :]: e.tensor_copy(o, i),
                       waits=[lt, sb_tok[s2]], inc=sV)
        else:
            ct = pr.op("scalar", lambda e, o=dd, i=scr[s2][:, :],
                       g=gnw_s[:, gcol:gcol + 1]:
                       e.activation(o, i, AF.Identity, bias=ZC(), scale=g),
                       waits=[lt, sb_tok[s2], misc_tok], inc=sA)
        if dst is None:
            dap = ap(wo_m, r * 128, [[DVT, 128], [128 * DVT, 8], [1, 128]])
            st = pr.op("scalar", lambda e, o=dap, i=dd:
                       e.dma_start(out=o, in_=i.rearrange(
                           "p (a b) -> p a b", a=8)),
                       waits=[ct], inc=sO2[s2], amt=16)
        else:
            st = pr.dma("scalar", dst[r * 128:(r + 1) * 128, c0:c0 + 1024],
                        dd, waits=[ct], inc=sO2[s2])
        scr_tok[s2] = ct
        sb_tok[s2] = st
        cast_last[s2] = st

    wm_prev = None
    for _w in range(700):
        if _w % 50 == 0:
            rn_w = pr.new_run()
        wm_prev = pr.op("tensor", lambda e: e.matmul(
            PX[:, 0:128], ones_s[:, :], ones_s[:, :],
            start=True, stop=True),
            waits=[misc_tok, wm_prev], ns=110, run=rn_w)
    warm_prep = wm_prev

    gsf = gsil.ap().bitcast(F32)
    t_sc = pr.op("scalar", lambda e: e.activation(silu_c[:, :], cT_s[:, :],
                                                  AF.Silu),
                 waits=[misc_tok], inc=sA)
    banks4 = [PB[0], PB[1], PO[0], PO[1]]
    bank4_tok = [None] * 4
    mod_st = []
    mm_prev = [None] * 4
    for rnd in range(3):
        mmtoks = [None] * 4
        for k in range(PC):
            lt = pr.dma("scalar", SCR_F[:, :],
                        wmod[k * 128:(k + 1) * 128,
                             rnd * 2048:rnd * 2048 + 1024],
                        waits=[scr_tok[0], mm_prev[0], mm_prev[1]], inc=sWM1)
            lt2 = pr.dma("scalar", VT[:, :],
                         wmod[k * 128:(k + 1) * 128,
                              rnd * 2048 + 1024:rnd * 2048 + 2048],
                         waits=[scr_tok[1], mm_prev[2], mm_prev[3]],
                         inc=sWM2)
            for b4 in range(4):
                src34 = SCR_F if b4 < 2 else VT
                mmtoks[b4] = pr.op(
                    "tensor",
                    lambda e, bk=banks4[b4], kk=k, bb=b4, sc3=src34:
                    e.matmul(bk[0:1, 0:512], silu_c[:, kk:kk + 1],
                             sc3[:, (bb % 2) * 512:(bb % 2 + 1) * 512],
                             start=(kk == 0), stop=(kk == PC - 1)),
                    waits=[lt if b4 < 2 else lt2, t_sc, bank4_tok[b4],
                           mm_prev[b4]], ns=600)
            for _b in range(4):
                mm_prev[_b] = mmtoks[_b]
        for b4 in range(4):
            seg = rnd * 4 + b4
            et = pr.op("scalar", lambda e, bk=banks4[b4]: e.activation(
                RB[0:1, 0:512], bk[0:1, 0:512], AF.Identity,
                bias=consts[0:1, 6:7], scale=consts[0:1, 7:8]),
                waits=[mmtoks[b4], misc_tok,
                       mod_st[-1] if mod_st else None], inc=sA)
            st = pr.dma("scalar", mod_dram[0:1, seg * 512:(seg + 1) * 512],
                        RB[0:1, 0:512], waits=[et], inc=sOmod)
            bank4_tok[b4] = et
            mod_st.append(st)
    zgE = pr.op("vector", lambda e: e.memset(gsf[:, 1024:1152], 0.0),
                inc=sV)
    ltE = pr.dma("sync", gsf[0:16, 1024:1152],
                 ap(mod_dram, 0, [[128, 16], [1, 128]]),
                 waits=mod_st[0:4] + [zgE], inc=sCG)
    idfE = pr.op("scalar", lambda e: e.activation(gsf[:, 1152:1280],
                                                  ident_s[:, :],
                                                  AF.Identity),
                 waits=[misc_tok], inc=sA)
    ttrE = pr.op("tensor", lambda e: e.transpose(PX[:, 256:384],
                                                 gsf[:, 1024:1152],
                                                 gsf[:, 1152:1280]),
                 waits=[ltE, idfE, zgE, warm_prep], inc=sT)
    tmvE = pr.op("scalar", lambda e: e.activation(modv[:, 0:16],
                                                  PX[:, 256:272],
                                                  AF.Identity),
                 waits=[ttrE], inc=sA)
    zg = pr.op("vector", lambda e: e.memset(VT[:, 0:128], 0.0),
               waits=[scr_tok[1], mm_prev[2], mm_prev[3]], inc=sV)
    lt = pr.dma("sync", VT[0:32, 0:128],
                ap(mod_dram, 2048, [[128, 32], [1, 128]]),
                waits=mod_st + [zg], inc=sCG)
    idf = pr.op("scalar", lambda e: e.activation(SCR_F[:, 0:128],
                                                 ident_s[:, :], AF.Identity),
                waits=[misc_tok, scr_tok[0], mm_prev[0], mm_prev[1]], inc=sA)
    ttr = pr.op("tensor", lambda e: e.transpose(PX[:, 0:128], VT[:, 0:128],
                                                SCR_F[:, 0:128]),
                waits=[lt, idf, zg, tmvE], inc=sT)
    tmv = pr.op("scalar", lambda e: e.activation(modv[:, 16:48],
                                                 PX[:, 0:32],
                                                 AF.Identity),
                waits=[ttr], inc=sA)
    scr_tok = [tmv, tmv]

    t_oca = pr.op("vector", lambda e: e.tensor_scalar_add(one_ca[:, :],
                                                          modv[:, 8:16], 1.0),
                  waits=[tmvE], inc=sV)
    t_ocb = pr.op("vector", lambda e: e.tensor_scalar_add(one_cb[:, :],
                                                          modv[:, 32:40], 1.0),
                  waits=[tmv], inc=sV)

    def sign_clamp(dstv, col0, w0):
        a = pr.op("scalar", lambda e: e.activation(SCR_F[:, 0:PC],
                                                   modv[:, col0:col0 + PC],
                                                   AF.Abs),
                  waits=[tmv, w0], inc=sA)
        b = pr.op("vector", lambda e: e.tensor_scalar_max(
            SCR_F[:, 0:PC], SCR_F[:, 0:PC], 0.25), waits=[a], inc=sV)
        c1 = pr.op("vector", lambda e: e.tensor_single_scalar(
            VT[:, 0:PC], modv[:, col0:col0 + PC], 0.0, ALU.is_ge),
            waits=[tmv, w0], inc=sV)
        c2 = pr.op("vector", lambda e: e.tensor_scalar(
            VT[:, 0:PC], VT[:, 0:PC], 2.0, -1.0, ALU.mult, ALU.add),
            waits=[c1], inc=sV)
        return pr.op("vector", lambda e: e.tensor_mul(
            dstv[:, :], SCR_F[:, 0:PC], VT[:, 0:PC]),
            waits=[b, c2], inc=sV)

    t_ga = sign_clamp(ga2, 16, scr_tok[0])
    t_gb = sign_clamp(gb2, 40, t_ga)
    scr_tok = [t_gb, t_gb]

    wfa = z_am
    for k in range(PC):
        lt = pr.dma("sync", SCR_F[:, :], wf_t[k * 128:(k + 1) * 128, :],
                    waits=[scr_tok[0] if k == 0 else wfa], inc=sCF)
        rt = pr.op("vector", lambda e: e.reduce_max(
            red_c[:, :], SCR_F[:, :], axis=AX.X, apply_absolute_value=True),
            waits=[lt, z_am], inc=sV)
        wfa = pr.op("vector", lambda e: e.tensor_max(
            amax_c[:, :], amax_c[:, :], red_c[:, :]), waits=[rt], inc=sV)
    d1 = pr.dma("scalar", wrow[0, :], amax_c[:, 0], waits=[wfa], inc=sB1)
    d2 = pr.dma("sync", RB[0:1, 0:128], wrow[:, :],
                waits=[d1, mod_st[-1]], inc=sB2)
    r1 = pr.op("vector", lambda e: e.reduce_max(small[0:1, 0:1],
                                                RB[0:1, 0:128], axis=AX.X),
               waits=[d2], inc=sV)
    r2 = pr.op("vector", lambda e: e.tensor_scalar_max(
        small[0:1, 1:2], small[0:1, 0:1], 1e-12), waits=[r1], inc=sV)
    r3 = pr.op("vector", lambda e: e.reciprocal(small[0:1, 2:3],
                                                small[0:1, 1:2]),
               waits=[r2], inc=sV)
    r4 = pr.op("vector", lambda e: e.tensor_scalar(
        small[0:1, 3:4], small[0:1, 2:3], FP8_MAX, FP8_MAX, ALU.mult,
        ALU.min), waits=[r3], inc=sV)
    r5 = pr.op("vector", lambda e: e.reciprocal(small[0:1, 4:5],
                                                small[0:1, 3:4]),
               waits=[r4], inc=sV)
    d3 = pr.dma("scalar", srow[0:1, 0:2], small[0:1, 3:5], waits=[r5],
                inc=sB1)
    d4 = pr.dma("sync", small[:, 5:6], ap(srow, 0, [[0, 128], [1, 1]]),
                waits=[d3], inc=sB2)
    wqp = None
    rbtok = d2
    for k in range(PC):
        lt = pr.dma("sync", SCR_F[:, :], wf_t[k * 128:(k + 1) * 128, :],
                    waits=[wqp, d4], inc=sCF)
        y = pr.op("vector", lambda e: e.tensor_scalar_mul(
            VT[:, :], SCR_F[:, :], small[:, 5:6]),
            waits=[lt, scr_tok[1] if k == 0 else None, d4], inc=sV)
        tt = pr.op("vector", lambda e: e.tensor_scalar_mul(
            SCR_F[:, :], VT[:, :], VELT_C), waits=[y], inc=sV)
        v1 = pr.op("vector", lambda e: e.tensor_sub(
            RB[:, 0:512], SCR_F[:, 0:512], VT[:, 0:512]),
            waits=[tt, rbtok], inc=sV)
        h1 = pr.op("vector", lambda e: e.tensor_sub(
            SCR_B[:, 0:512], SCR_F[:, 0:512], RB[:, 0:512]),
            waits=[v1, sb_tok[0]], inc=sV)
        v2 = pr.op("vector", lambda e: e.tensor_sub(
            RB[:, 0:512], SCR_F[:, 512:1024], VT[:, 512:1024]),
            waits=[h1], inc=sV)
        h2_ = pr.op("vector", lambda e: e.tensor_sub(
            SCR_B[:, 512:1024], SCR_F[:, 512:1024], RB[:, 0:512]),
            waits=[v2], inc=sV)
        st = pr.dma("scalar", wf_q[k * 128:(k + 1) * 128, :],
                    SCR_B[:, 0:1024], waits=[h2_], inc=sOF)
        wqp = st
        rbtok = h2_
        sb_tok[0] = st
    wfq_done = wqp

    z_S = pr.op("vector", lambda e: e.memset(S[:, :], 0.0),
                waits=[sb_tok[0], sb_tok[1], cast_last[1]], inc=sV)
    z_am2 = pr.op("vector", lambda e: e.memset(amax_c[:, :], 0.0),
                  waits=[d1], inc=sV)

    NCH = T // CH
    og2f = og2.ap().bitcast(F32)
    prol_rows = {}
    prol_store = {}
    stage_tok = [None, None]
    bank_chain = [None, None]
    for cc in range(NCH):
        sb2 = cc % 2
        stb = og2 if sb2 == 0 else gsil
        stf = og2f if sb2 == 0 else gsf
        bank = (cc // 2) % 2
        PXB = PXS[bank]
        s0 = (cc % 2) * 256
        xapc = ap(x_res, cc * CH, [[PC * T, 128], [T, PC], [1, CH]])
        cst = pr.op("scalar", lambda e, i=xapc, o=stb: e.activation(
            ap(o, 0, [[o.shape[1], 128], [CH, PC], [1, CH]]), i,
            AF.Identity), waits=xloads + [stage_tok[sb2]], inc=sA, ns=900)
        sq = pr.op("vector", lambda e, i=xapc, o=stb: e.tensor_tensor(
            ap(o, 1024, [[o.shape[1], 128], [CH, PC], [1, CH]]), i, i,
            ALU.mult), waits=xloads + [stage_tok[sb2]], inc=sV, ns=1250)
        mm1 = mm2 = None
        rn1 = pr.new_run()
        for k in range(PC):
            mm1 = pr.op("tensor", lambda e, kk=k, o=stb, b=PXB, s=s0:
                        e.matmul(b[:, s:s + 128], ones_s[:, :],
                                 o[:, kk * 128:(kk + 1) * 128],
                                 start=(kk == 0), stop=(kk == PC - 1)),
                        waits=[cst, misc_tok, bank_chain[bank], mm1],
                        ns=230, run=rn1)
        rn2 = pr.new_run()
        for k in range(PC):
            mm2 = pr.op("tensor", lambda e, kk=k, o=stb, b=PXB, s=s0:
                        e.matmul(b[:, s + 128:s + 256], ones_s[:, :],
                                 o[:, 1024 + kk * 128:1024 + (kk + 1) * 128],
                                 start=(kk == 0), stop=(kk == PC - 1)),
                        waits=[sq, mm1, mm2], ns=230, run=rn2)
        rm = pr.op("scalar", lambda e, b=PXB, s=s0: e.activation(
            b[:, s:s + 128], b[:, s:s + 128], AF.Identity, bias=ZC(),
            scale=consts[:, 1:2]), waits=[mm2], inc=sA, ns=420)
        rq = pr.op("scalar", lambda e, b=PXB, s=s0: e.activation(
            b[:, s + 128:s + 256], b[:, s + 128:s + 256], AF.Identity,
            bias=ZC(), scale=consts[:, 1:2]), waits=[mm2, rm], inc=sA,
            ns=420)
        t1 = pr.op("scalar", lambda e, b=PXB, s=s0, f=stf: e.activation(
            f[:, 0:128], b[:, s:s + 128], AF.Square),
            waits=[rm, mm2], inc=sA, ns=420)
        a2 = pr.op("vector", lambda e, b=PXB, s=s0, f=stf: e.tensor_sub(
            f[:, 0:128], b[:, s + 128:s + 256], f[:, 0:128]),
            waits=[t1, rq], inc=sV, ns=200)
        a3 = pr.op("scalar", lambda e, f=stf: e.activation(
            f[:, 0:128], f[:, 0:128], AF.Sqrt, bias=consts[:, 0:1],
            scale=OC()), waits=[a2], inc=sA, ns=520)
        a4 = pr.op("vector", lambda e, b=PXB, s=s0, f=stf: e.reciprocal(
            b[:, s + 128:s + 256], f[:, 0:128]), waits=[a3], inc=sV,
            ns=900)
        cp = pr.op("scalar", lambda e, b=PXB, s=s0, f=stf: e.activation(
            f[:, 128:384], b[:, s:s + 256], AF.Identity),
            waits=[a4, mm2], inc=sA, ns=450)
        st = pr.dma("scalar", ln1_dram[:, cc * 256:(cc + 1) * 256],
                    stf[:, 128:384], waits=[cp], inc=sPO)
        prol_rows[cc] = [rm, a4]
        prol_store[cc] = st
        stage_tok[sb2] = [mm2, a4, st]
        bank_chain[bank] = cp

    PB4 = PB + PXS
    pb_free = [bank4_tok[0], bank4_tok[1],
               [bank_chain[0]], [bank_chain[1]]]
    po_free = [bank4_tok[2], bank4_tok[3]]
    px_chain = [tmv, tmvE]
    ptr_chain = [None]
    pass_i = [0]
    pool_pos = [0]

    tok = {
        "xh": None, "xh2": None, "qt_w": None, "kt_w": None,
        "qt_r": None, "kt_r": None,
        "v_w": [None, None], "v_r": [None, None], "RBr": rbtok,
        "gsil_w": None, "gsil_r": [stage_tok[1], ttrE],
        "og2_w": None, "og2_r": [stage_tok[0]],
        "S": z_S, "Sbf_w": None, "Sbf_r": None,
        "ktm_w": None, "ktm_r": None, "sco_w": None, "sco_r": None,
        "ropeB": None, "tab": None, "RB": rbtok,
        "RBv": rbtok, "RBs": rbtok,
        "SCRF": z_S, "SB": [z_S, z_S, z_S, z_S], "VT": z_S, "VT2": z_S,
        "RBst": rbtok, "redc": d1,
        "slab": [cast_last[0], cast_last[1]] * 3, "xmid": {},
    }
    f1_done = []
    stat_st = {}
    res_toks = {}
    out_tok = [None]

    def ln_stats(xap, waits_read):
        xb = pr.op("scalar", lambda e, i=xap: e.activation(
            ap(SCR_B, 0, [[2048, 128], [CH, PC], [1, CH]]), i, AF.Identity),
            waits=waits_read + [tok["SB"][0], tok["SB"][1]], inc=sA, ns=900)
        x2 = pr.op("vector", lambda e, i=xap: e.tensor_tensor(
            ap(SCR_B, 1024, [[2048, 128], [CH, PC], [1, CH]]), i, i,
            ALU.mult),
            waits=waits_read + [tok["SB"][2], tok["SB"][3], xb], inc=sV,
            ns=1250)
        slA = pass_i[0] % 4
        pass_i[0] += 1
        slB = pass_i[0] % 4
        pass_i[0] += 1
        mm1 = mm2 = None
        rn1 = pr.new_run()
        for k in range(PC):
            mm1 = pr.op("tensor", lambda e, kk=k, s2=slA: e.matmul(
                PB4[s2][:, 0:128], ones_s[:, :],
                SCR_B[:, kk * 128:(kk + 1) * 128],
                start=(kk == 0), stop=(kk == PC - 1)),
                waits=[xb, misc_tok, pb_free[slA], mm1], ns=230, run=rn1)
        rn2 = pr.new_run()
        for k in range(PC):
            mm2 = pr.op("tensor", lambda e, kk=k, s2=slB: e.matmul(
                PB4[s2][:, 0:128], ones_s[:, :],
                SCR_B[:, 1024 + kk * 128:1024 + (kk + 1) * 128],
                start=(kk == 0), stop=(kk == PC - 1)),
                waits=[x2, pb_free[slB], mm2], ns=230, run=rn2)
        rm = pr.op("scalar", lambda e, s2=slA: e.activation(
            RB[:, 0:128], PB4[s2][:, 0:128], AF.Identity, bias=ZC(),
            scale=consts[:, 1:2]), waits=[mm1, tok["RB"], tok["RBst"]],
            inc=sA, ns=420)
        rq = pr.op("scalar", lambda e, s2=slB: e.activation(
            RB[:, 128:256], PB4[s2][:, 0:128], AF.Identity, bias=ZC(),
            scale=consts[:, 1:2]), waits=[mm2, rm], inc=sA, ns=420)
        pb_free[slA] = rm
        pb_free[slB] = rq
        a1 = pr.op("vector", lambda e: e.tensor_mul(
            RB[:, 256:384], RB[:, 0:128], RB[:, 0:128]),
            waits=[rm, tok["RBv"]], inc=sV, ns=200)
        a2 = pr.op("vector", lambda e: e.tensor_sub(
            RB[:, 256:384], RB[:, 128:256], RB[:, 256:384]),
            waits=[a1, rq], inc=sV, ns=200)
        a3 = pr.op("scalar", lambda e: e.activation(
            RB[:, 256:384], RB[:, 256:384], AF.Sqrt, bias=consts[:, 0:1],
            scale=OC()), waits=[a2], inc=sA, ns=520)
        a4 = pr.op("vector", lambda e: e.reciprocal(
            RB[:, 128:256], RB[:, 256:384]), waits=[a3], inc=sV, ns=900)
        tok["RB"] = a4
        tok["RBv"] = a4
        tok["SB"][0] = mm1
        tok["SB"][1] = mm1
        tok["SB"][2] = mm2
        tok["SB"][3] = mm2
        return a4

    def ln_norm(xap, rows_tok):
        mb = ap(RB, 0, [[512, 128], [0, PC], [1, CH]])
        rb_ = ap(RB, 128, [[512, 128], [0, PC], [1, CH]])
        tt0 = pr.op("vector", lambda e, i=xap: e.tensor_tensor(
            ap(SCR_F, 0, [[1024, 128], [CH, PC], [1, CH]]), i, mb,
            ALU.subtract), waits=[rows_tok, tok["SCRF"]], inc=sV, ns=1250)
        tt1 = pr.op("vector", lambda e: e.tensor_tensor(
            ap(SCR_F, 0, [[1024, 128], [CH, PC], [1, CH]]),
            ap(SCR_F, 0, [[1024, 128], [CH, PC], [1, CH]]), rb_, ALU.mult),
            waits=[tt0, rows_tok], inc=sV, ns=1250)
        tok["SCRF"] = tt1
        tok["RB"] = tt1
        return tt1

    def tile_pass(src_bf, m, kcount, rhs_fn, xtra, nw=GR, split=False,
                  xtra_mid=None):
        slot = pass_i[0] % 4
        pass_i[0] += 1
        wide = kcount * 128
        nwin = wide // 1024
        if nwin == 2 and pool_pos[0] % 2 == 1:
            pool_pos[0] += 1
        w0_ = pool_pos[0] % 6
        pool_pos[0] += nwin
        sl = slabP.ap()[:, w0_ * 1024: w0_ * 1024 + wide]
        wt = [tok["slab"][(w0_ + j) % 6] for j in range(nwin)]
        wt = wt + [w_cast.get(src_bf.name)]
        if split and nwin == 2:
            lt = pr.dma("sync", sl[:, 0:1024],
                        src_bf[m * 128:(m + 1) * 128, 0:1024],
                        waits=wt, inc=sSlab[w0_ % 3])
            lt2 = pr.dma("scalar", sl[:, 1024:2048],
                         src_bf[m * 128:(m + 1) * 128, 1024:2048],
                         waits=wt, inc=sSlabS[(w0_ + 1) % 3])
        else:
            lt = pr.dma("sync" if w0_ % 2 == 0 else "scalar",
                        sl[:, 0:wide],
                        src_bf[m * 128:(m + 1) * 128, 0:wide],
                        waits=wt,
                        inc=(sSlab if w0_ % 2 == 0 else sSlabX)[w0_ % 3])
            lt2 = lt
        mmt = None
        rn = pr.new_run()
        for k in range(kcount):
            mmt = pr.op("tensor", lambda e, kk=k, s=sl, sl2=slot, w=nw:
                        e.matmul(PB4[sl2][:, 0:w],
                                 s[:, kk * 128:(kk + 1) * 128], rhs_fn(kk),
                                 start=(kk == 0), stop=(kk == kcount - 1)),
                        waits=[lt if k * 128 < 1024 else lt2,
                               pb_free[slot], mmt] +
                        (xtra if k == 0 else []) +
                        (xtra_mid if (xtra_mid is not None and
                                      k == kcount // 2) else []),
                        ns=(330 if nw == GR else 230), run=rn)
        for j in range(nwin):
            tok["slab"][(w0_ + j) % 6] = [mmt, lt2]
        return slot, mmt

    def v_pass_gen(c0, xh_tok, xb2):
        for m in range(PC):
            if pool_pos[0] % 2 == 1:
                pool_pos[0] += 1
            w0_ = pool_pos[0] % 6
            pool_pos[0] += 2
            sl = slabP.ap()[:, w0_ * 1024: w0_ * 1024 + 2048]
            wt = [tok["slab"][w0_], tok["slab"][(w0_ + 1) % 6],
                  w_cast[wv_bf.name]]
            lt = pr.dma("sync", sl[:, 0:1024],
                        wv_bf[m * 128:(m + 1) * 128, 0:1024],
                        waits=wt, inc=sSlab[w0_ % 3])
            lt2 = pr.dma("scalar", sl[:, 1024:2048],
                         wv_bf[m * 128:(m + 1) * 128, 1024:2048],
                         waits=wt, inc=sSlabS[(w0_ + 1) % 3])
            mm_both = []
            for ci in range(2):
                cc = c0 + ci
                voff = (cc % 2) * DVT
                slot = pass_i[0] % 4
                pass_i[0] += 1
                mmt = None
                rn = pr.new_run()
                for k in range(PC):
                    mmt = pr.op("tensor",
                                lambda e, kk=k, cci=ci, s=sl, sl2=slot,
                                x2_=xb2:
                                e.matmul(PB4[sl2][:, 0:256],
                                         x2_[:, kk * GR + cci * CH:
                                             kk * GR + cci * CH + CH],
                                         s[:, kk * 256:(kk + 1) * 256],
                                         start=(kk == 0),
                                         stop=(kk == PC - 1)),
                                waits=[lt if k < 4 else lt2, xh_tok,
                                       pb_free[slot], mmt],
                                ns=330, run=rn)
                ev = pr.op("scalar", lambda e, mm=m, sl2=slot, vo=voff:
                           e.activation(
                               v_tm[:, vo + mm * 256: vo + (mm + 1) * 256],
                               PB4[sl2][:, 0:256], AF.Identity),
                           waits=[mmt, tok["v_r"][cc % 2]], inc=sA, ns=480)
                pb_free[slot] = ev
                tok["v_w"][cc % 2] = ev
                mm_both.append(mmt)
            tok["slab"][w0_] = [mm_both, lt2]
            tok["slab"][(w0_ + 1) % 6] = [mm_both, lt2]
            yield

    def g_pass(m, xh_tok, xb2):
        slot, mmt = tile_pass(wg_bf, m, PC,
                              lambda k, x2_=xb2: x2_[:, k * GR:(k + 1) * GR],
                              [xh_tok])
        gl = pr.op("scalar", lambda e, mm=m, sl2=slot: e.activation(
            gsil[:, mm * GR:(mm + 1) * GR], PB4[sl2][:, 0:GR], AF.Silu),
            waits=[mmt, tok["gsil_r"]], inc=sA, ns=520)
        pb_free[slot] = gl
        return gl, mmt

    def wo_pass_gen(ci, cc):
        wo_mms = []
        xms = []
        for m in range(PC):
            slot, mmt = tile_pass(wo_m, m, DPC,
                                  lambda k: og2[:, k * 128:(k + 1) * 128],
                                  [tok["og2_w"][0:2]], nw=128, split=True,
                                  xtra_mid=[tok["og2_w"][2:4]])
            xm = pr.op("vector",
                       lambda e, mm=m, sl2=slot, c2=cc:
                       e.scalar_tensor_tensor(
                           x_res[:, mm * T + c2 * CH:
                                 mm * T + (c2 + 1) * CH],
                           PB4[sl2][:, 0:CH],
                           ga2[:, mm:mm + 1],
                           x_res[:, mm * T + c2 * CH:
                                 mm * T + (c2 + 1) * CH],
                           ALU.mult, ALU.add),
                       waits=[mmt, t_ga], inc=sV, ns=200)
            pb_free[slot] = xm
            wo_mms.append(mmt)
            xms.append(xm)
            tok["og2_r"] = wo_mms
            tok["xmid"][cc] = xms
            yield

    def f1_stats(cc):
        xap = ap(x_res, cc * CH, [[PC * T, 128], [T, PC], [1, CH]])
        rows_tok = ln_stats(xap, [tok["xmid"][cc]])
        st = pr.dma("scalar", stats_dram[:, cc * 256:(cc + 1) * 256],
                    RB[:, 0:256], waits=[rows_tok], inc=sStatO)
        tok["RBst"] = st
        stat_st[cc] = st
        return xap, rows_tok

    def f1_chain(cc, xap, rows_tok):
        tt1 = ln_norm(xap, rows_tok)
        hts = []
        for p in range(PC):
            hts.append(pr.op("vector", lambda e, pp=p: e.tensor_scalar(
                SCR_F[:, pp * CH:(pp + 1) * CH],
                SCR_F[:, pp * CH:(pp + 1) * CH],
                one_cb[:, pp:pp + 1], modv[:, 24 + pp:25 + pp],
                ALU.mult, ALU.add), waits=[tt1, t_ocb], inc=sV))
        rd = pr.op("vector", lambda e: e.reduce_max(
            red_c[:, :], SCR_F[:, :], axis=AX.X,
            apply_absolute_value=True), waits=hts + [tok["redc"]], inc=sV)
        am = pr.op("vector", lambda e: e.tensor_max(
            amax_c[:, :], amax_c[:, :], red_c[:, :]),
            waits=[rd, z_am2], inc=sV)
        tok["redc"] = am
        f1_done.append(am)
        tok["SCRF"] = rd

    def retention_chunk(ci, cc, fillers):
        voff = (cc % 2) * DVT
        og_last = None
        og_alls = []
        sco_readers = []
        head_info = []
        sbf_prev = tok["Sbf_w"]
        for hh in range(H):
            sl_s = pass_i[0] % 4
            pass_i[0] += 1
            mmt = None
            rn = pr.new_run()
            for b in range(2):
                pcq = 2 * hh + b
                mmt = pr.op("tensor",
                            lambda e, pq=pcq, bb=b, cci=ci, s2=sl_s:
                            e.matmul(PB4[s2][:, 0:128],
                                     kt[:, pq * GR + cci * CH:
                                        pq * GR + cci * CH + CH],
                                     qt[:, pq * GR + cci * CH:
                                        pq * GR + cci * CH + CH],
                                     start=(bb == 0), stop=(bb == 1)),
                            waits=[tok["qt_w"], tok["kt_w"], pb_free[sl_s],
                                   mmt], ns=230, run=rn)
            sct = pr.op("vector", lambda e, h2=hh, s2=sl_s:
                        e.scalar_tensor_tensor(
                            sco[:, h2 * 128:(h2 + 1) * 128],
                            PB4[s2][:, 0:128],
                            lamsig_s[:, h2:h2 + 1], mask_s[:, :],
                            ALU.mult, ALU.mult),
                        waits=[mmt, misc_tok, tok["sco_r"]], inc=sV, ns=220)
            pb_free[sl_s] = sct
            tok["sco_w"] = sct
            if fillers:
                fillers[min(hh, len(fillers) - 1)](hh)

            if cc > 0:
                sb_w = [tok["S"], sbf_prev, tok["Sbf_r"], tok["ropeB"]]
                sb1 = pr.op("vector", lambda e, h2=hh: e.tensor_copy(
                    S_bf[:, 0:512], S[:, h2 * 1024:h2 * 1024 + 512]),
                    waits=sb_w, inc=sV, ns=650)
                sb2 = pr.op("scalar", lambda e, h2=hh: e.activation(
                    S_bf[:, 512:1024], S[:, h2 * 1024 + 512:(h2 + 1) * 1024],
                    AF.Identity), waits=sb_w, inc=sA, ns=550)
                sbf_prev = [sb1, sb2]
                tok["Sbf_w"] = sbf_prev
                tok["ropeB"] = sbf_prev

            posl = hh % 2
            otok = None
            rn_o = pr.new_run()
            for b in range(4):
                w0 = [sct, tok["v_w"][cc % 2], po_free[posl], otok]
                if cc == 0:
                    otok = pr.op("tensor",
                                 lambda e, h2=hh, bb=b, pp=posl, vo=voff:
                                 e.matmul(
                                     PO[pp][:, bb * 128:(bb + 1) * 128],
                                     v_tm[:, vo + h2 * DV + bb * 128:
                                          vo + h2 * DV + (bb + 1) * 128],
                                     sco[:, h2 * 128:(h2 + 1) * 128],
                                     start=True, stop=True),
                                 waits=w0, ns=230, run=rn_o)
                else:
                    t_a = pr.op("tensor",
                                lambda e, h2=hh, bb=b, pp=posl, vo=voff:
                                e.matmul(PO[pp][:, bb * 128:(bb + 1) * 128],
                                         v_tm[:, vo + h2 * DV + bb * 128:
                                              vo + h2 * DV + (bb + 1) * 128],
                                         sco[:, h2 * 128:(h2 + 1) * 128],
                                         start=True, stop=False),
                                waits=w0 + [sbf_prev], ns=230, run=rn_o)
                    t_b = pr.op("tensor", lambda e, h2=hh, bb=b, pp=posl,
                                cci=ci:
                                e.matmul(PO[pp][:, bb * 128:(bb + 1) * 128],
                                         S_bf[:, bb * 128:(bb + 1) * 128],
                                         qt[:, (2 * h2) * GR + cci * CH:
                                            (2 * h2) * GR + cci * CH + CH],
                                         start=False, stop=False),
                                waits=[sbf_prev, tok["qt_w"], t_a], ns=230,
                                run=rn_o)
                    otok = pr.op("tensor", lambda e, h2=hh, bb=b,
                                 pp=posl, cci=ci:
                                 e.matmul(
                                     PO[pp][:, bb * 128:(bb + 1) * 128],
                                     S_bf[:, 512 + bb * 128:
                                          512 + (bb + 1) * 128],
                                     qt[:, (2 * h2 + 1) * GR + cci * CH:
                                        (2 * h2 + 1) * GR + cci * CH + CH],
                                     start=False, stop=True),
                                 waits=[t_b], ns=230, run=rn_o)
            tok["Sbf_r"] = otok
            sco_readers.append(otok)

            par = hh % 2
            osoff = 1024 + par * 512
            o2off = par * 512
            ssoff = 128 + par * 256
            oev = pr.op("scalar", lambda e, pp=posl, oo=osoff: e.activation(
                SCR_B[:, oo:oo + 512], PO[pp][:, 0:512], AF.Identity),
                waits=[otok, tok["SB"][2 + par]], inc=sA, ns=600)
            po_free[posl] = oev
            o2_ = pr.op("vector", lambda e, oo=osoff, so=o2off: e.tensor_mul(
                SCR_B[:, so:so + 512], SCR_B[:, oo:oo + 512],
                SCR_B[:, oo:oo + 512]), waits=[oev, tok["SB"][par]],
                inc=sV, ns=380)
            ssm = None
            rn_s = pr.new_run()
            for b in range(4):
                ssm = pr.op("tensor", lambda e, bb=b, so=o2off, sso=ssoff:
                            e.matmul(
                    PX[:, sso:sso + 128], ones_s[:, :],
                    SCR_B[:, so + bb * 128:so + (bb + 1) * 128],
                    start=(bb == 0), stop=(bb == 3)),
                    waits=[o2_, px_chain[0], ssm], ns=230, run=rn_s)
            px_chain[0] = ssm
            tok["SB"][par] = ssm
            head_info.append((hh, ssm, oev))
            if par == 1:
                (h0, ssm0, oev0), (h1, ssm1, oev1) = head_info[-2:]
                ssg = ap(PX, 128, [[512, 128], [256, 2], [1, 128]])
                rbg = ap(RB, 256, [[512, 128], [128, 2], [1, 128]])
                sdt2 = pr.op("scalar", lambda e, i=ssg, o=rbg: e.activation(
                    o, i, AF.Sqrt, bias=consts[:, 0:1],
                    scale=consts[:, 2:3]),
                    waits=[ssm0, ssm1, tok["RBv"], tok["RBs"]], inc=sA,
                    ns=560)
                rst2 = pr.op("vector", lambda e: e.reciprocal(
                    RB[:, 256:512], RB[:, 256:512]), waits=[sdt2], inc=sV,
                    ns=900)
                px_chain[0] = rst2
                for h3, oevx in ((h0, oev0), (h1, oev1)):
                    rboff = 256 + (h3 % 2) * 128
                    osx = 1024 + (h3 % 2) * 512
                    rstb = ap(RB, rboff, [[512, 128], [0, 4], [1, 128]])
                    gslice = ap(gsil, h3 * 4 * GR + ci * CH,
                                [[DPC * GR, 128], [GR, 4], [1, CH]])
                    oslice = ap(og2, h3 * 4 * 128,
                                [[DPC * 128, 128], [128, 4], [1, CH]])
                    g1t = pr.op("vector",
                                lambda e, o=oslice, gs=gslice, rb2=rstb:
                                e.tensor_tensor(o, gs, rb2, ALU.mult),
                                waits=[rst2, [tok["gsil_w"][4 * h3 + j]
                                              for j in range(4)],
                                       tok["og2_r"]], inc=sV, ns=620)
                    og_last = pr.op("vector", lambda e, o=oslice, oo=osx:
                                    e.tensor_tensor(
                                        o, o,
                                        ap(SCR_B, oo,
                                           [[2048, 128], [128, 4], [1, 128]]),
                                        ALU.mult),
                                    waits=[g1t, oevx], inc=sV, ns=620)
                    tok["SB"][2 + h3 % 2] = og_last
                    if h3 % 2 == 0:
                        tok["RBv"] = og_last
                    else:
                        tok["RBs"] = og_last
                    tok["gsil_r"] = og_last
                    og_alls.append(og_last)
        tok["og2_w"] = og_alls
        tok["sco_r"] = sco_readers

    def u_fold_chunk(ci, cc, ktm_last, wo_gen):
        voff = (cc % 2) * DVT
        last_umt = None
        umts = []
        for hh in range(H):
            for b in range(2):
                posl = (hh * 2 + b) % 2
                umt = pr.op("tensor",
                            lambda e, h2=hh, bb=b, pp=posl, vo=voff:
                            e.matmul(PO[pp][:, 0:512],
                                     k_tm[:, h2 * DK + bb * 128:
                                          h2 * DK + (bb + 1) * 128],
                                     v_tm[:, vo + h2 * DV:
                                          vo + (h2 + 1) * DV],
                                     start=True, stop=True),
                            waits=[ktm_last, tok["v_w"][cc % 2],
                                   po_free[posl]], ns=590)
                lc = float(lamh(hh) ** CH)
                fld = pr.op("vector", lambda e, h2=hh, bb=b, lcv=lc,
                            pp=posl: e.scalar_tensor_tensor(
                                S[:, (2 * h2 + bb) * 512:
                                  (2 * h2 + bb + 1) * 512],
                                S[:, (2 * h2 + bb) * 512:
                                  (2 * h2 + bb + 1) * 512],
                                lcv, PO[pp][:, 0:512],
                                ALU.mult, ALU.add),
                            waits=[umt, tok["S"], tok["Sbf_w"]], inc=sV,
                            ns=680)
                po_free[posl] = fld
                tok["S"] = fld
                last_umt = umt
                umts.append(umt)
                if wo_gen is not None:
                    next(wo_gen, None)
        tok["v_r"][cc % 2] = umts
        tok["ktm_r"] = umts
        tok["ropeB"] = tok["S"]
        return last_umt

    def transposes(ci, cc):
        tts = []
        tt_last = None
        rn_t = pr.new_run()
        rn_k = pr.new_run()
        for s8 in range(8):
            tt_last = pr.op("tensor", lambda e, pq=s8, cci=ci:
                            e.transpose(PTR[:, pq * 128:(pq + 1) * 128],
                                        kt[:, pq * GR + cci * CH:
                                           pq * GR + cci * CH + CH],
                                        ident_s[:, :]),
                            waits=[tok["kt_w"], misc_tok, ptr_chain[0],
                                   tt_last], ns=300, run=rn_t)
            tts.append(tt_last)
        ktm_last = None
        for s8 in range(8):
            hh2, bb2 = s8 // 2, s8 % 2
            ktm_last = pr.op("scalar", lambda e, pq=s8, h2=hh2, bb=bb2:
                             e.activation(
                                 k_tm[:, h2 * DK + bb * 128:
                                      h2 * DK + (bb + 1) * 128],
                                 PTR[:, pq * 128:(pq + 1) * 128],
                                 AF.Identity, bias=ZC(),
                                 scale=lamk_s[:, h2:h2 + 1]),
                             waits=[tts[7], misc_tok, tok["ktm_r"],
                                    ktm_last], ns=420, run=rn_k)
        ptr_chain[0] = ktm_last
        return ktm_last

    ln1_xh = {}
    vtn_tok = [z_S]
    pxs_free = {}
    for _b in range(2):
        for _ci in range(2):
            pxs_free[(_b, _ci)] = [bank_chain[_b]]

    def ln1_group(gn):
        bank = gn % 2
        xh_b = xh if gn % 2 == 0 else xh2
        xh_all = []
        for ci in range(2):
            cc = 2 * gn + ci
            ld = pr.dma("sync", RB[:, 0:256],
                        ln1_dram[:, cc * 256:(cc + 1) * 256],
                        waits=[prol_store[cc], tok["RB"], tok["RBst"]],
                        inc=sPI)
            rows_tok = [ld]
            mb = ap(RB, 0, [[512, 128], [0, PC], [1, CH]])
            rb_ = ap(RB, 128, [[512, 128], [0, PC], [1, CH]])
            xapc = ap(x_res, cc * CH, [[PC * T, 128], [T, PC], [1, CH]])
            w_extra = [tok["v_r"][0], tok["og2_w"]] if gn >= 1 else []
            tt0 = pr.op("vector", lambda e, i=xapc, m=mb: e.tensor_tensor(
                VT[:, :].rearrange("p (a b) -> p a b", a=PC), i, m,
                ALU.subtract),
                waits=rows_tok + vtn_tok + w_extra + xloads, inc=sV,
                ns=1250)
            tt1 = pr.op("vector", lambda e, r=rb_: e.tensor_tensor(
                VT[:, :].rearrange("p (a b) -> p a b", a=PC),
                VT[:, :].rearrange("p (a b) -> p a b", a=PC), r, ALU.mult),
                waits=[tt0] + rows_tok, inc=sV, ns=1250)
            tok["RB"] = [tt0, tt1]
            rn_x = pr.new_run()
            xh_ci = []
            for p in range(PC):
                xh_ci.append(pr.op("scalar",
                             lambda e, pp=p, cci=ci, xb2=xh_b:
                             e.activation(
                    xb2[:, pp * GR + cci * CH: pp * GR + cci * CH + CH],
                    VT[:, pp * CH:(pp + 1) * CH], AF.Identity,
                    bias=modv[:, pp:pp + 1], scale=one_ca[:, pp:pp + 1]),
                    waits=[tt1, t_oca, tok["xh2" if gn % 2 else "xh"]],
                    inc=sA, ns=420, run=rn_x))
            xh_all.extend(xh_ci)
            vtn_tok[:] = [xh_ci]
        ln1_xh[gn] = xh_all

    ln1_group(0)

    for g in range(NG):
        t0g = g * GR
        c0 = 2 * g
        tb1 = pr.dma("sync", cos_sl[:, :], cosT[:, t0g:t0g + GR],
                     waits=[tok["tab"]], inc=sTab)
        tb2 = pr.dma("sync", sin_sl[:, :], sinT[:, t0g:t0g + GR],
                     waits=[tok["tab"]], inc=sTab)
        ttab = [tb1, tb2]

        xh_b = xh if g % 2 == 0 else xh2
        xh_tok = ln1_xh[g]

        def proj_rope(wsrc, dst, dst_free, do_lam):
            outs = []
            pend = {}
            for m in range(PC):
                slot, mmt = tile_pass(wsrc, m, PC,
                                      lambda k, x2_=xh_b:
                                      x2_[:, k * GR:(k + 1) * GR],
                                      [xh_tok])
                pend[m] = (slot, mmt)
                if m % 2 == 0:
                    continue
                hh = m // 2
                sA_, mmA = pend[m - 1]
                sB_, mmB = pend[m]
                o1 = dst[:, (2 * hh) * GR:(2 * hh + 1) * GR]
                o2 = dst[:, (2 * hh + 1) * GR:(2 * hh + 2) * GR]
                rB1 = ropeB[:, 0:GR]
                rB2 = ropeB[:, GR:2 * GR]
                p1 = pr.op("vector", lambda e, pa=sA_, o=o1: e.tensor_tensor(
                    o, PB4[pa][:, 0:GR], cos_sl[:, :], ALU.mult),
                    waits=[mmA] + ttab + [dst_free], inc=sV, ns=300)
                p3 = pr.op("vector", lambda e, pa=sA_, o=o2: e.tensor_tensor(
                    o, PB4[pa][:, 0:GR], sin_sl[:, :], ALU.mult),
                    waits=[mmA] + ttab + [dst_free, p1], inc=sV, ns=300)
                q1 = pr.op("vector", lambda e, pb=sB_, r=rB1: e.tensor_tensor(
                    r, PB4[pb][:, 0:GR], sin_sl[:, :], ALU.mult),
                    waits=[mmB] + ttab + [tok["ropeB"]], inc=sV, ns=300)
                q2 = pr.op("vector", lambda e, pb=sB_, r=rB2: e.tensor_tensor(
                    r, PB4[pb][:, 0:GR], cos_sl[:, :], ALU.mult),
                    waits=[mmB] + ttab + [tok["ropeB"], q1], inc=sV, ns=300)
                p2 = pr.op("vector", lambda e, o=o1, r=rB1: e.tensor_tensor(
                    o, o, r, ALU.subtract), waits=[p1, q1], inc=sV, ns=300)
                p4 = pr.op("vector", lambda e, o=o2, r=rB2: e.tensor_tensor(
                    o, o, r, ALU.add), waits=[p3, q2], inc=sV, ns=300)
                pb_free[sA_] = [p1, p3]
                pb_free[sB_] = [q1, q2]
                tok["ropeB"] = [p2, p4]
                outs += [p2, p4]
            if do_lam:
                qall = ap(dst, 0, [[PC * GR, 128], [GR, PC],
                                   [CH, GR // CH], [1, CH]])
                lm = ap(lamq_s, 0, [[1024, 128], [128, PC],
                                    [0, GR // CH], [1, CH]])
                outs = [pr.op("vector", lambda e: e.tensor_tensor(
                    qall, qall, lm, ALU.mult),
                    waits=outs + [misc_tok], inc=sV, ns=800)]
            return outs

        tok["qt_w"] = proj_rope(wq_bf, qt, tok["qt_r"], True)
        tok["kt_w"] = proj_rope(wk_bf, kt, tok["kt_r"], False)

        tok["gsil_w"] = {}
        for m in range(4):
            gl, mmt = g_pass(m, xh_tok, xh_b)
            tok["gsil_w"][m] = gl
            xh_free = mmt
        g_next = [4]

        for _ in v_pass_gen(c0, xh_tok, xh_b):
            pass

        ktm_l = transposes(0, c0)

        def fill_c0(hh):
            while g_next[0] < min(4 * hh + 8, DPC):
                gl, _ = g_pass(g_next[0], xh_tok, xh_b)
                tok["gsil_w"][g_next[0]] = gl
                g_next[0] += 1

        retention_chunk(0, c0, [fill_c0] * 4)
        while g_next[0] < DPC:
            gl, _ = g_pass(g_next[0], xh_tok, xh_b)
            tok["gsil_w"][g_next[0]] = gl
            g_next[0] += 1
        wo0 = wo_pass_gen(0, c0)
        u_fold_chunk(0, c0, ktm_l, wo0)
        for _ in wo0:
            pass
        if g + 1 < NG:
            ln1_group(g + 1)
        xap0, rows0 = f1_stats(c0)

        ktm_l = transposes(1, c0 + 1)
        f1_emitted = [False]

        def fill_c1(hh):
            if not f1_emitted[0]:
                f1_emitted[0] = True
                f1_chain(c0, xap0, rows0)

        retention_chunk(1, c0 + 1, [fill_c1] * 4)
        wo1 = wo_pass_gen(1, c0 + 1)
        u_fold_chunk(1, c0 + 1, ktm_l, wo1)
        for _ in wo1:
            pass
        xap1, rows1 = f1_stats(c0 + 1)
        f1_chain(c0 + 1, xap1, rows1)

        tok["tab"] = tok["kt_w"]
        tok["qt_r"] = tok["v_r"][1]
        tok["kt_r"] = tok["v_r"][1]
        tok["xh"] = tok["v_r"][1]
        tok["xh2"] = tok["v_r"][1]

    e1 = pr.dma("scalar", cc_in[0, :], amax_c[:, 0], waits=f1_done, inc=sB1)
    wm2 = None
    for _w in range(500):
        if _w % 50 == 0:
            rn_w2 = pr.new_run()
        wm2 = pr.op("tensor", lambda e: e.matmul(
            PX[:, 0:128], ones_s[:, :], ones_s[:, :],
            start=True, stop=True),
            waits=[e1, px_chain[0], wm2], ns=110, run=rn_w2)
    cctok = pr.op("gpsimd", lambda e: e.collective_compute(
        "AllReduce", ALU.max, replica_groups=[list(range(n_cores))],
        ins=[cc_in.ap()], outs=[cc_out.ap()]), waits=[e1], inc=sCC,
        async_op=True)
    e2 = pr.dma("sync", RB[0:1, 256:384], cc_out[:, :],
                waits=[cctok, tok["RBv"], tok["RBs"]], inc=sB2)
    q1 = pr.op("vector", lambda e: e.reduce_max(
        small[0:1, 8:9], RB[0:1, 256:384], axis=AX.X), waits=[e2], inc=sV)
    q2 = pr.op("vector", lambda e: e.tensor_scalar_max(
        small[0:1, 9:10], small[0:1, 8:9], 1e-12), waits=[q1], inc=sV)
    q3 = pr.op("vector", lambda e: e.reciprocal(
        small[0:1, 10:11], small[0:1, 9:10]), waits=[q2], inc=sV)
    q4 = pr.op("vector", lambda e: e.tensor_scalar(
        small[0:1, 11:12], small[0:1, 10:11], FP8_MAX, FP8_MAX,
        ALU.mult, ALU.min), waits=[q3], inc=sV)
    q5 = pr.op("vector", lambda e: e.reciprocal(
        small[0:1, 12:13], small[0:1, 11:12]), waits=[q4], inc=sV)
    q6 = pr.op("vector", lambda e: e.tensor_mul(
        small[0:1, 13:14], small[0:1, 12:13], small[0:1, 4:5]),
        waits=[q5], inc=sV)
    e3 = pr.dma("scalar", srow[0:1, 2:3], small[0:1, 11:12], waits=[q4],
                inc=sB1)
    e4 = pr.dma("scalar", srow[0:1, 4:5], small[0:1, 13:14], waits=[q6],
                inc=sB1)
    e5 = pr.dma("sync", small[:, 14:15], ap(srow, 2, [[0, 128], [1, 1]]),
                waits=[e3], inc=sB2)
    e6 = pr.dma("sync", small[:, 15:16], ap(srow, 4, [[0, 128], [1, 1]]),
                waits=[e4], inc=sB2)
    b1 = pr.op("vector", lambda e: e.tensor_scalar_mul(
        ocbs[:, :], one_cb[:, :], small[:, 14:15]), waits=[e5, t_ocb], inc=sV)
    b2 = pr.op("vector", lambda e: e.tensor_scalar_mul(
        sbs[:, :], modv[:, 24:32], small[:, 14:15]), waits=[e5], inc=sV)

    tok["VT"] = tok["v_r"][0] + tok["v_r"][1]
    tok["VT2"] = tok["VT"]
    tok["scoF"] = None
    tok_os = [tok["qt_r"], tok["kt_r"]]
    og2h_r = [tok["og2_r"], tok["gsil_r"]]
    og2h_w = [tok["og2_w"], tok["gsil_r"]]
    h2q_buf = [og2, gsil]
    h2q_off = [0, 2048]

    def f2_chains(g):
        hbuf = h2q_buf[g % 2]
        hoff = h2q_off[g % 2]
        toks = []
        for ci in range(2):
            cc = 2 * g + ci
            xap = ap(x_res, cc * CH, [[PC * T, 128], [T, PC], [1, CH]])
            ld = pr.dma("sync", sco_f[:, 0:256],
                        stats_dram[:, cc * 256:(cc + 1) * 256],
                        waits=[stat_st[cc], tok["scoF"], tok["sco_r"]],
                        inc=sStatI)
            mb = ap(sco_f, 0, [[256, 128], [0, PC], [1, CH]])
            rb_ = ap(sco_f, 128, [[256, 128], [0, PC], [1, CH]])
            tt0 = pr.op("vector", lambda e, i=xap, m=mb: e.tensor_tensor(
                ap(SCR_F, 0, [[1024, 128], [CH, PC], [1, CH]]), i, m,
                ALU.subtract), waits=[ld, tok["SCRF"]], inc=sV,
                ns=1250)
            tt1 = pr.op("vector", lambda e, r=rb_: e.tensor_tensor(
                ap(SCR_F, 0, [[1024, 128], [CH, PC], [1, CH]]),
                ap(SCR_F, 0, [[1024, 128], [CH, PC], [1, CH]]), r, ALU.mult),
                waits=[tt0], inc=sV, ns=1250)
            tok["scoF"] = [tt0, tt1]
            hts = []
            rn_h = pr.new_run()
            for p in range(PC):
                hts.append(pr.op("scalar", lambda e, pp=p: e.activation(
                    VT[:, pp * CH:(pp + 1) * CH],
                    SCR_F[:, pp * CH:(pp + 1) * CH], AF.Identity,
                    bias=sbs[:, pp:pp + 1], scale=ocbs[:, pp:pp + 1]),
                    waits=[tt1, b1, b2, tok["VT"]], inc=sA, ns=420,
                    run=rn_h))
            aa = pr.op("scalar", lambda e: e.activation(
                SCR_F[:, :], VT[:, :], AF.Identity, bias=ZC(),
                scale=consts[:, 3:4]), waits=hts + [tt1], inc=sA, ns=900)
            bb_ = pr.op("vector", lambda e: e.tensor_sub(
                VT2[:, :], SCR_F[:, :], VT[:, :]),
                waits=[aa, tok["VT2"]], inc=sV, ns=1250)
            tok["VT"] = [hts, aa, bb_]
            outap = ap(hbuf, hoff + ci * CH,
                       [[hbuf.shape[1], 128], [GR, PC], [1, CH]])
            vlast = pr.op("vector", lambda e, o=outap: e.tensor_tensor(
                o, SCR_F[:, :].rearrange("p (a b) -> p a b", a=PC),
                VT2[:, :].rearrange("p (a b) -> p a b", a=PC),
                ALU.subtract),
                waits=[bb_, og2h_r[g % 2], og2h_w[g % 2]], inc=sV,
                ns=1250)
            tok["VT2"] = vlast
            tok["SCRF"] = [bb_, vlast]
            toks.append(vlast)
        og2h_w[g % 2] = toks
        return toks

    h2q_pend = f2_chains(0)
    for g in range(NG):
        c0 = 2 * g
        hbuf = h2q_buf[g % 2]
        hoff = h2q_off[g % 2]
        h2q_toks = h2q_pend
        ffn_mms = []
        evs = []
        for m in range(PC):
            slot, mmt = tile_pass(wf_q, m, PC,
                                  lambda k, hb=hbuf, bb=hoff:
                                  hb[:, bb + k * GR: bb + (k + 1) * GR],
                                  h2q_toks + [wfq_done])
            ffn_mms.append(mmt)
            ev = pr.op("scalar", lambda e, mm=m, sl2=slot: e.activation(
                gsil[:, mm * GR:(mm + 1) * GR], PB4[sl2][:, 0:GR],
                AF.Identity, bias=bf_s[:, mm:mm + 1], scale=small[:, 15:16]),
                waits=[mmt, e6, misc_tok, tok["gsil_r"]], inc=sA, ns=480)
            pb_free[slot] = ev
            evs.append(ev)
        og2h_r[g % 2] = ffn_mms
        if g + 1 < NG:
            h2q_pend = f2_chains(g + 1)
        rlast = [[], []]
        for m in range(PC):
            for ci in range(2):
                cc = c0 + ci
                rlast[ci].append(pr.op(
                    "vector", lambda e, mm=m, cci=ci, c2=cc, os=OS[ci]:
                    e.scalar_tensor_tensor(
                        os[:, mm * CH:(mm + 1) * CH],
                        gsil[:, mm * GR + cci * CH: mm * GR + cci * CH + CH],
                        gb2[:, mm:mm + 1],
                        x_res[:, mm * T + c2 * CH: mm * T + (c2 + 1) * CH],
                        ALU.mult, ALU.add),
                    waits=[evs[m], t_gb, tok_os[ci]], inc=sV, ns=200))
        tok["gsil_r"] = rlast[0] + rlast[1]
        for ci in range(2):
            cc = c0 + ci
            outap = ap(yT, cc * CH, [[T, 128], [128 * T, PC], [1, CH]])
            out_tok[0] = pr.op("scalar", lambda e, o=outap, i=OS[ci],
                               cci=ci: e.dma_start(
                                   out=o,
                                   in_=i[:, :].rearrange(
                                       "p (a b) -> p a b", a=PC)),
                               waits=rlast[ci], inc=sOout, amt=16)
            tok_os[ci] = out_tok[0]

    pr.op("scalar", lambda e: e.nop(), waits=[out_tok[0]])

    pr.emit()
    return nc, sbuf_left


def _host_constants(T):
    f = np.arange(128, dtype=np.float64)
    inv_freq = ROPE_BASE ** (-f / 128.0)
    t = np.arange(T, dtype=np.float64)
    ang = inv_freq[:, None] * t[None, :]
    lams = np.array([lamh(h) for h in range(H)], dtype=np.float64)
    tau = np.arange(CH, dtype=np.float64)
    lamq = np.zeros((128, 1024), dtype=np.float64)
    for pc in range(PC):
        lamq[:, pc * 128:(pc + 1) * 128] = (lams[pc // 2] ** (tau + 1.0)) / 16.0
    sig = np.arange(128, dtype=np.float64)
    consts = np.zeros((128, 8), dtype=np.float32)
    consts[:, 0] = EPS
    consts[:, 1] = 1.0 / D
    consts[:, 2] = 1.0 / DV
    consts[:, 3] = VELT_C
    consts[:, 4] = 1e-12
    consts[:, 5] = FP8_MAX
    consts[:, 7] = 1.0
    return dict(
        cosT=np.cos(ang).astype(BF), sinT=np.sin(ang).astype(BF),
        lamqI=lamq.astype(BF),
        lamsigI=np.stack([lams[h] ** (-(sig + 1.0)) for h in range(H)],
                         axis=1).astype(np.float32),
        lamkI=np.stack([lams[h] ** (127.0 - sig) for h in range(H)],
                       axis=1).astype(np.float32),
        maskI=np.triu(np.ones((128, 128))).astype(BF),
        identI=np.eye(128).astype(BF),
        onesI=np.ones((128, 128)).astype(BF),
        constsI=consts)


def _tile(WT, mwidth):
    IN, OUT = WT.shape
    nm, nk = OUT // mwidth, IN // 128
    return np.ascontiguousarray(
        WT.reshape(nk, 128, nm, mwidth).transpose(2, 1, 0, 3)
        .reshape(nm * 128, nk * mwidth))


def prep_inputs(inputs, T, n_cores):
    consts = _host_constants(T)
    f32 = lambda k: np.asarray(inputs[k], dtype=np.float32)
    x = f32("L0_attn_x")
    gnw_full = np.tile(f32("L0_attn_g_norm_w"), H)
    shared = dict(
        wq_t=_tile(np.ascontiguousarray(f32("L0_attn_Wq").T), 128),
        wk_t=_tile(np.ascontiguousarray(f32("L0_attn_Wk").T), 128),
        wv_t=_tile(np.ascontiguousarray(f32("L0_attn_Wv").T), 256),
        wg_t=_tile(np.ascontiguousarray(f32("L0_attn_Wg").T), 128),
        woT=np.ascontiguousarray(f32("L0_attn_Wo").T),
        wf_t=_tile(np.ascontiguousarray(f32("L0_ffn_weight").T), 128),
        wmod=np.ascontiguousarray(f32("L0_wmod")),
        gnw=np.ascontiguousarray(gnw_full.reshape(DPC, 128).T),
        bfc=np.ascontiguousarray(f32("L0_ffn_bias").reshape(PC, 128).T),
        **consts)
    cmat = f32("L0_c")
    in_maps = []
    for b in range(n_cores):
        m = dict(shared)
        m["xT"] = np.ascontiguousarray(x[b].T)
        m["cT"] = np.ascontiguousarray(cmat[b].reshape(PC, 128).T)
        in_maps.append(m)
    return in_maps


def kernel(**inputs):
    T = int(np.asarray(inputs["L0_attn_x"]).shape[1])
    n_cores = 8
    in_maps = prep_inputs(inputs, T, n_cores)
    nc, _ = build(T, n_cores)
    res = run_bass_kernel_spmd(nc, in_maps, core_ids=list(range(n_cores)))
    outs = [np.ascontiguousarray(np.asarray(res.results[b]["yT"]).T)
            for b in range(n_cores)]
    return np.stack(outs, axis=0).astype(np.float32)


# version 38
# speedup vs baseline: 1.0577x; 1.0277x over previous
import numpy as np
import ml_dtypes

import concourse.bass as bass
import concourse.mybir as mybir
from concourse.bass_isa import ReduceOp
from concourse import library_config
from concourse.bass_utils import run_bass_kernel_spmd

F32 = mybir.dt.float32
BF16 = mybir.dt.bfloat16
AF = mybir.ActivationFunctionType
ALU = mybir.AluOpType
AX = mybir.AxisListType
BF = ml_dtypes.bfloat16

ENGINES = ("sync", "scalar", "vector", "tensor", "gpsimd")

D = 1024
H = 4
DK = 256
DV = 512
DVT = H * DV
PC = D // 128
DPC = DVT // 128
CH = 128
GR = 256
ROPE_BASE = 10000.0
EPS = 1e-5
VELT_C = float(2 ** 20 + 1)
FP8_MAX = 448.0


def lamh(h):
    return 1.0 - 2.0 ** (-5.0 - h)


class Sem:
    def __init__(self, nc, name, dma=False):
        self.name = name
        self.dma = dma
        self.n = 0


class Prog:

    def __init__(self, nc):
        self.nc = nc
        self.n_ops = 0
        self.engine = []
        self.fn = []
        self.deps = []
        self.dur = []
        self.isdma = []
        self.isasync = []
        self.tag = []
        self.run = []
        self.boost = []
        self.n_runs = 0
        self.sched = None

    def new_run(self):
        self.n_runs += 1
        return self.n_runs

    def sem(self, name, dma=False):
        return Sem(self.nc, name, dma)

    def op(self, eng, fn, waits=(), inc=None, amt=1, ns=None, dma=False,
           async_op=False, run=None, boost=0.0):
        flat = set()

        def _fl(w):
            if w is None:
                return
            if isinstance(w, (list, tuple)):
                for x in w:
                    _fl(x)
            else:
                flat.add(w)
        _fl(list(waits))
        deps = sorted(flat)
        i = self.n_ops
        self.n_ops += 1
        isdma = dma or amt == 16
        self.isasync.append(isdma or async_op)
        self.run.append(None if isdma else run)
        self.boost.append(boost)
        self.engine.append(eng)
        self.fn.append(fn)
        self.deps.append(deps)
        self.dur.append(ns if ns is not None else (1500 if isdma else 350))
        self.isdma.append(isdma)
        self.tag.append(inc if isdma else None)
        return i

    def dma(self, eng, out, in_, waits=(), inc=None, boost=0.0):
        assert out.size() == in_.size(), (
            f"dma size mismatch {out.size()} vs {in_.size()} "
            f"out={out.tensor.name} in={in_.tensor.name}")
        assert inc is not None
        ns = 900 + in_.nbytes() // 300
        return self.op(eng, lambda e, o=out, i=in_: e.dma_start(out=o, in_=i),
                       waits=waits, inc=inc, ns=ns, dma=True, boost=boost)

    def emit(self, reorder=True):
        n = self.n_ops
        mid_of = [0] * n
        members = []
        runmap = {}
        for i in range(n):
            r = self.run[i]
            if r is not None and (self.engine[i], r) in runmap:
                m = runmap[(self.engine[i], r)]
                members[m].append(i)
                mid_of[i] = m
            else:
                m = len(members)
                members.append([i])
                mid_of[i] = m
                if r is not None:
                    runmap[(self.engine[i], r)] = m
        nm = len(members)
        mdeps = [set() for _ in range(nm)]
        for i in range(n):
            for d in self.deps[i]:
                if mid_of[d] != mid_of[i]:
                    mdeps[mid_of[i]].add(mid_of[d])
        mdur = [sum(self.dur[i] for i in mem) for mem in members]
        meng = [self.engine[mem[0]] for mem in members]
        misdma = [self.isdma[mem[0]] for mem in members]
        mchild = [[] for _ in range(nm)]
        for m in range(nm):
            for d in mdeps[m]:
                mchild[d].append(m)
        mprio = [0.0] * nm
        mboost = [max(self.boost[i] for i in mem) for mem in members]
        for m in range(nm - 1, -1, -1):
            mx = 0.0
            for c in mchild[m]:
                if mprio[c] > mx:
                    mx = mprio[c]
            mprio[m] = mdur[m] + mx + mboost[m]
        import heapq
        engs = sorted(set(self.engine))
        etime = {e: 0.0 for e in engs}
        done_t = [0.0] * nm
        ndep = [len(mdeps[m]) for m in range(nm)]
        ready = {e: [] for e in engs}
        for m in range(nm):
            if ndep[m] == 0:
                heapq.heappush(ready[meng[m]], (-mprio[m], m))
        morder = []
        scheduled = 0
        if not reorder:
            morder = list(range(nm))
        else:
            while scheduled < nm:
                best = None
                for e in engs:
                    if not ready[e]:
                        continue
                    pr_, m = ready[e][0]
                    st = max(etime[e],
                             max((done_t[d] + (250.0 if meng[d] != e else 0.0)
                                  for d in mdeps[m]), default=0.0))
                    key = (st, pr_)
                    if best is None or key < best[0]:
                        best = (key, e, m)
                assert best is not None, "scheduler stuck"
                _, e, m = best
                heapq.heappop(ready[e])
                st = max(etime[e],
                         max((done_t[d] + (250.0 if meng[d] != e else 0.0)
                              for d in mdeps[m]), default=0.0))
                if misdma[m]:
                    etime[e] = st + 150.0
                    done_t[m] = st + mdur[m]
                else:
                    etime[e] = st + mdur[m]
                    done_t[m] = etime[e]
                morder.append(m)
                scheduled += 1
                for c in mchild[m]:
                    ndep[c] -= 1
                    if ndep[c] == 0:
                        heapq.heappush(ready[meng[c]], (-mprio[c], c))
        order = []
        for m in morder:
            order.extend(members[m])
        streams = {e: [] for e in engs}
        for i in order:
            streams[self.engine[i]].append(i)

        def semkey(i):
            return (self.engine[i], self.tag[i]) if self.isdma[i] \
                else (self.engine[i], None)

        def sem_dep(d, i):
            return self.isasync[d] or self.isasync[i] or \
                not (self.engine[d] == self.engine[i] == "tensor")

        need_inc = [False] * n
        for i in range(n):
            for d in self.deps[i]:
                if sem_dep(d, i):
                    need_inc[d] = True
        for i in range(n):
            if self.isasync[i]:
                need_inc[i] = True

        sems = {}
        for i in order:
            if not need_inc[i]:
                continue
            k = semkey(i)
            if k not in sems:
                nm = f"s{len(sems)}_{k[0]}" + ("d" if self.isdma[i] else "")
                sems[k] = self.nc.alloc_semaphore(nm)
        cum = [0] * n
        scount = {}
        for e in engs:
            for i in streams[e]:
                k = semkey(i)
                if need_inc[i]:
                    scount[k] = scount.get(k, 0) + \
                        (16 if self.isdma[i] else 1)
                cum[i] = scount.get(k, 0)
        waited = {e: {} for e in engs}

        def wait_list(i):
            out = []
            for d in self.deps[i]:
                if not sem_dep(d, i):
                    continue
                out.append((semkey(d), cum[d]))
            if self.isdma[i] and cum[i] > 16:
                out.append((semkey(i), cum[i] - 16))
            return out

        import os
        if os.environ.get("PROG_DEBUG"):
            import pickle
            with open("/tmp/prog_dbg.pkl", "wb") as f:
                pickle.dump(dict(order=order, streams=streams, cum=cum,
                                 need_inc=need_inc, deps=self.deps,
                                 engine=self.engine, isdma=self.isdma,
                                 tag={i: str(self.tag[i]) for i in range(n)
                                      if self.isdma[i]}), f)
        with self.nc.Block() as blk:
            for eng_name in engs:
                ops_ = streams[eng_name]
                if not ops_:
                    continue

                def make(ops_=ops_, eng_name=eng_name):
                    def body(eng):
                        w8 = waited[eng_name]
                        for i in ops_:
                            for key, v in wait_list(i):
                                if w8.get(key, 0) < v:
                                    eng.wait_ge(sems[key], v)
                                    w8[key] = v
                            ins = self.fn[i](eng)
                            if need_inc[i]:
                                ins.then_inc(sems[semkey(i)],
                                             16 if self.isdma[i] else 1)
                    return body

                getattr(blk, eng_name)(make())
def ap(t, off, dims):
    return bass.AP(t, off, [list(d) for d in dims])


def build(T, n_cores):
    NG = T // GR
    nc = bass.Bass("TRN2", target_bir_lowering=False)

    xT = nc.dram_tensor("xT", [D, T], F32, kind="ExternalInput")
    wq_t = nc.dram_tensor("wq_t", [D, D], F32, kind="ExternalInput")
    wk_t = nc.dram_tensor("wk_t", [D, D], F32, kind="ExternalInput")
    wv_t = nc.dram_tensor("wv_t", [D, DVT], F32, kind="ExternalInput")
    wg_t = nc.dram_tensor("wg_t", [DVT, D], F32, kind="ExternalInput")
    woT = nc.dram_tensor("woT", [DVT, D], F32, kind="ExternalInput")
    wf_t = nc.dram_tensor("wf_t", [D, D], F32, kind="ExternalInput")
    wmod = nc.dram_tensor("wmod", [D, 6 * D], F32, kind="ExternalInput")
    cT = nc.dram_tensor("cT", [128, PC], F32, kind="ExternalInput")
    gnw = nc.dram_tensor("gnw", [128, DPC], F32, kind="ExternalInput")
    bfc = nc.dram_tensor("bfc", [128, PC], F32, kind="ExternalInput")
    cosT = nc.dram_tensor("cosT", [128, T], BF16, kind="ExternalInput")
    sinT = nc.dram_tensor("sinT", [128, T], BF16, kind="ExternalInput")
    lamqI = nc.dram_tensor("lamqI", [128, 1024], BF16, kind="ExternalInput")
    lamsigI = nc.dram_tensor("lamsigI", [128, H], F32, kind="ExternalInput")
    lamkI = nc.dram_tensor("lamkI", [128, H], F32, kind="ExternalInput")
    maskI = nc.dram_tensor("maskI", [128, 128], BF16, kind="ExternalInput")
    identI = nc.dram_tensor("identI", [128, 128], BF16, kind="ExternalInput")
    onesI = nc.dram_tensor("onesI", [128, 128], BF16, kind="ExternalInput")
    constsI = nc.dram_tensor("constsI", [128, 8], F32, kind="ExternalInput")

    yT = nc.dram_tensor("yT", [D, T], F32, kind="ExternalOutput")

    wq_bf = nc.dram_tensor("wq_bf", [D, D], BF16)
    wk_bf = nc.dram_tensor("wk_bf", [D, D], BF16)
    wv_bf = nc.dram_tensor("wv_bf", [D, DVT], BF16)
    wg_bf = nc.dram_tensor("wg_bf", [DVT, D], BF16)
    wo_m = nc.dram_tensor("wo_m", [PC * 128, DVT], BF16)
    wf_q = nc.dram_tensor("wf_q", [D, D], BF16)
    mod_dram = nc.dram_tensor("mod_dram", [1, 6 * D], F32)
    wrow = nc.dram_tensor("wrow", [1, 128], F32)
    srow = nc.dram_tensor("srow", [1, 8], F32)
    cc_in = nc.dram_tensor("cc_in", [1, 128], F32)
    cc_out = nc.dram_tensor("cc_out", [1, 128], F32)
    stats_dram = nc.dram_tensor("stats_dram", [128, (T // CH) * 256], F32)
    ln1_dram = nc.dram_tensor("ln1_dram", [128, (T // CH) * 256], F32)

    A = nc.alloc_sbuf_tensor
    x_res = A("x_res", [128, PC * T], F32)
    slabP = A("slabP", [128, 6144], BF16)
    xh = A("xh", [128, PC * GR], BF16)
    xh2 = xh
    qt = A("qt", [128, PC * GR], BF16)
    kt = A("kt", [128, PC * GR], BF16)
    v_tm = A("v_tm", [128, 2 * DVT], BF16)
    gsil = A("gsil", [128, DPC * GR], BF16)
    og2 = A("og2", [128, DPC * 128], BF16)
    S = A("S", [128, H * 2 * DV], F32)
    k_tm = A("k_tm", [128, H * DK], BF16)
    sco_f = A("sco_f", [128, 256], F32)
    sco = sco_f.ap().bitcast(BF16)
    ropeB = A("ropeB", [128, H * GR], BF16)
    cos_sl = A("cos_sl", [128, GR], BF16)
    sin_sl = A("sin_sl", [128, GR], BF16)
    lamq_s = A("lamq_s", [128, 1024], BF16)
    mask_s = A("mask_s", [128, 128], BF16)
    ident_s = A("ident_s", [128, 128], BF16)
    ones_s = A("ones_s", [128, 128], BF16)
    lamsig_s = A("lamsig_s", [128, H], F32)
    lamk_s = A("lamk_s", [128, H], F32)
    consts = A("consts", [128, 8], F32)
    modv = A("modv", [128, 48], F32)
    one_ca = A("one_ca", [128, PC], F32)
    one_cb = A("one_cb", [128, PC], F32)
    ga2 = A("ga2", [128, PC], F32)
    gb2 = A("gb2", [128, PC], F32)
    ocbs = A("ocbs", [128, PC], F32)
    sbs = A("sbs", [128, PC], F32)
    gnw_s = A("gnw_s", [128, DPC], F32)
    bf_s = A("bf_s", [128, PC], F32)
    cT_s = A("cT_s", [128, PC], F32)
    silu_c = A("silu_c", [128, PC], F32)
    RB = A("RB", [128, 512], F32)
    SCR_F = A("SCR_F", [128, 1024], F32)
    SCR_B = A("SCR_B", [128, 2048], BF16)
    amax_c = A("amax_c", [128, 1], F32)
    red_c = A("red_c", [128, 1], F32)
    small = A("small", [128, 16], F32)
    S_bf = ropeB
    VT = v_tm.ap().bitcast(F32)[:, 0:1024]
    VT2 = v_tm.ap().bitcast(F32)[:, 1024:2048]
    OS = [qt.ap().bitcast(F32), kt.ap().bitcast(F32)]

    sbuf_left = nc.sbuf_bytes_remaining

    P = nc.alloc_psum_tensor
    PB = [P("PB0", [128, 512], F32), P("PB1", [128, 512], F32)]
    PO = [P("POA", [128, 512], F32), P("POB", [128, 512], F32)]
    PX = P("PX", [128, 512], F32)
    PTR = P("PTR", [128, 1024], BF16)
    PXS = [P("PX2", [128, 512], F32), P("PX3", [128, 512], F32)]

    pr = Prog(nc)
    sMisc = pr.sem("sMisc", dma=True)
    sXq = [pr.sem("sX0", dma=True), pr.sem("sX1", dma=True),
           pr.sem("sX2", dma=True)]
    sTab = pr.sem("sTab", dma=True)
    sSlab = [pr.sem("sSlab0", dma=True), pr.sem("sSlab1", dma=True),
             pr.sem("sSlab2", dma=True)]
    sSlabX = [pr.sem("sSlabX0", dma=True), pr.sem("sSlabX1", dma=True),
              pr.sem("sSlabX2", dma=True)]
    sCF = pr.sem("sCF", dma=True)
    sCG = pr.sem("sCG", dma=True)
    sOF = pr.sem("sOF", dma=True)
    sOG = pr.sem("sOG", dma=True)
    sOmod = pr.sem("sOmod", dma=True)
    sOout = pr.sem("sOout", dma=True)
    sG = pr.sem("sG", dma=True)
    sStatO = pr.sem("sStatO", dma=True)
    sStatI = pr.sem("sStatI", dma=True)
    sPO = pr.sem("sPO", dma=True)
    sPI = pr.sem("sPI", dma=True)
    sWM1 = pr.sem("sWM1", dma=True)
    sWM2 = pr.sem("sWM2", dma=True)
    sB1 = pr.sem("sB1", dma=True)
    sB2 = pr.sem("sB2", dma=True)
    sSlabS = [pr.sem("sSlabS0", dma=True), pr.sem("sSlabS1", dma=True),
              pr.sem("sSlabS2", dma=True)]
    sGP = pr.sem("sGP")
    sCC = pr.sem("sCC")
    sT = pr.sem("sT")
    sA = pr.sem("sA")
    sV = pr.sem("sV")

    ZC = lambda: consts[:, 6:7]
    OC = lambda: consts[:, 7:8]

    misc_tok = []
    for dst, src in [(consts, constsI), (mask_s, maskI), (ident_s, identI),
                     (ones_s, onesI), (lamq_s, lamqI), (lamsig_s, lamsigI),
                     (lamk_s, lamkI), (gnw_s, gnw), (bf_s, bfc), (cT_s, cT)]:
        misc_tok.append(pr.dma("sync", dst[:, :], src[:, :], inc=sMisc,
                                boost=1e7))

    xloads = []
    for p in range(PC):
        xq = ["sync", "scalar", "gpsimd"][p % 3]
        xloads.append(pr.dma(xq, x_res[:, p * T:(p + 1) * T],
                             xT[p * 128:(p + 1) * 128, :],
                             inc=sXq[p % 3], boost=1e7))

    z_am = pr.op("vector", lambda e: e.memset(amax_c[:, :], 0.0), inc=sV)

    w_cast = {}
    _bsts = [4e6, 3e6, 2e6, 1e6]
    for _j, (csrc, cdst) in enumerate([(wq_t, wq_bf), (wk_t, wk_bf),
                                       (wv_t, wv_bf), (wg_t, wg_bf)]):
        w_cast[cdst.name] = pr.dma("gpsimd", cdst[:, :], csrc[:, :],
                                   inc=sG, boost=_bsts[_j])
    cast_jobs = []
    for r in range(DVT // 128):
        cast_jobs.append((woT, None, r, 0, r))

    scr = [SCR_F, VT]
    sC = [sCF, sCG]
    sO2 = [sOF, sOG]
    scr_tok = [None, None]
    sb_tok = [None, None]
    cast_last = [None, None]
    for j, (src, dst, r, c0, gcol) in enumerate(cast_jobs):
        s2 = j % 2
        lt = pr.dma("sync", scr[s2][:, :],
                    src[r * 128:(r + 1) * 128, c0:c0 + 1024],
                    waits=[scr_tok[s2]], inc=sC[s2])
        dd = SCR_B[:, s2 * 1024:(s2 + 1) * 1024]
        if gcol is None:
            ct = pr.op("vector",
                       lambda e, o=dd, i=scr[s2][:, :]: e.tensor_copy(o, i),
                       waits=[lt, sb_tok[s2]], inc=sV)
        else:
            ct = pr.op("scalar", lambda e, o=dd, i=scr[s2][:, :],
                       g=gnw_s[:, gcol:gcol + 1]:
                       e.activation(o, i, AF.Identity, bias=ZC(), scale=g),
                       waits=[lt, sb_tok[s2], misc_tok], inc=sA)
        if dst is None:
            dap = ap(wo_m, r * 128, [[DVT, 128], [128 * DVT, 8], [1, 128]])
            st = pr.op("scalar", lambda e, o=dap, i=dd:
                       e.dma_start(out=o, in_=i.rearrange(
                           "p (a b) -> p a b", a=8)),
                       waits=[ct], inc=sO2[s2], amt=16)
        else:
            st = pr.dma("scalar", dst[r * 128:(r + 1) * 128, c0:c0 + 1024],
                        dd, waits=[ct], inc=sO2[s2])
        scr_tok[s2] = ct
        sb_tok[s2] = st
        cast_last[s2] = st

    wm_prev = None
    for _w in range(700):
        if _w % 50 == 0:
            rn_w = pr.new_run()
        wm_prev = pr.op("tensor", lambda e: e.matmul(
            PX[:, 0:128], ones_s[:, :], ones_s[:, :],
            start=True, stop=True),
            waits=[misc_tok, wm_prev], ns=110, run=rn_w)
    warm_prep = wm_prev

    gsf = gsil.ap().bitcast(F32)
    t_sc = pr.op("scalar", lambda e: e.activation(silu_c[:, :], cT_s[:, :],
                                                  AF.Silu),
                 waits=[misc_tok], inc=sA)
    banks4 = [PB[0], PB[1], PO[0], PO[1]]
    bank4_tok = [None] * 4
    mod_st = []
    mm_prev = [None] * 4
    for rnd in range(3):
        mmtoks = [None] * 4
        for k in range(PC):
            lt = pr.dma("scalar", SCR_F[:, :],
                        wmod[k * 128:(k + 1) * 128,
                             rnd * 2048:rnd * 2048 + 1024],
                        waits=[scr_tok[0], mm_prev[0], mm_prev[1]], inc=sWM1)
            lt2 = pr.dma("scalar", VT[:, :],
                         wmod[k * 128:(k + 1) * 128,
                              rnd * 2048 + 1024:rnd * 2048 + 2048],
                         waits=[scr_tok[1], mm_prev[2], mm_prev[3]],
                         inc=sWM2)
            for b4 in range(4):
                src34 = SCR_F if b4 < 2 else VT
                mmtoks[b4] = pr.op(
                    "tensor",
                    lambda e, bk=banks4[b4], kk=k, bb=b4, sc3=src34:
                    e.matmul(bk[0:1, 0:512], silu_c[:, kk:kk + 1],
                             sc3[:, (bb % 2) * 512:(bb % 2 + 1) * 512],
                             start=(kk == 0), stop=(kk == PC - 1)),
                    waits=[lt if b4 < 2 else lt2, t_sc, bank4_tok[b4],
                           mm_prev[b4]], ns=600)
            for _b in range(4):
                mm_prev[_b] = mmtoks[_b]
        for b4 in range(4):
            seg = rnd * 4 + b4
            et = pr.op("scalar", lambda e, bk=banks4[b4]: e.activation(
                RB[0:1, 0:512], bk[0:1, 0:512], AF.Identity,
                bias=consts[0:1, 6:7], scale=consts[0:1, 7:8]),
                waits=[mmtoks[b4], misc_tok,
                       mod_st[-1] if mod_st else None], inc=sA)
            st = pr.dma("scalar", mod_dram[0:1, seg * 512:(seg + 1) * 512],
                        RB[0:1, 0:512], waits=[et], inc=sOmod)
            bank4_tok[b4] = et
            mod_st.append(st)
    zgE = pr.op("vector", lambda e: e.memset(gsf[:, 1024:1152], 0.0),
                inc=sV)
    ltE = pr.dma("sync", gsf[0:16, 1024:1152],
                 ap(mod_dram, 0, [[128, 16], [1, 128]]),
                 waits=mod_st[0:4] + [zgE], inc=sCG)
    idfE = pr.op("scalar", lambda e: e.activation(gsf[:, 1152:1280],
                                                  ident_s[:, :],
                                                  AF.Identity),
                 waits=[misc_tok], inc=sA)
    ttrE = pr.op("tensor", lambda e: e.transpose(PX[:, 256:384],
                                                 gsf[:, 1024:1152],
                                                 gsf[:, 1152:1280]),
                 waits=[ltE, idfE, zgE, warm_prep], inc=sT)
    tmvE = pr.op("scalar", lambda e: e.activation(modv[:, 0:16],
                                                  PX[:, 256:272],
                                                  AF.Identity),
                 waits=[ttrE], inc=sA)
    zg = pr.op("vector", lambda e: e.memset(VT[:, 0:128], 0.0),
               waits=[scr_tok[1], mm_prev[2], mm_prev[3]], inc=sV)
    lt = pr.dma("sync", VT[0:32, 0:128],
                ap(mod_dram, 2048, [[128, 32], [1, 128]]),
                waits=mod_st + [zg], inc=sCG)
    idf = pr.op("scalar", lambda e: e.activation(SCR_F[:, 0:128],
                                                 ident_s[:, :], AF.Identity),
                waits=[misc_tok, scr_tok[0], mm_prev[0], mm_prev[1]], inc=sA)
    ttr = pr.op("tensor", lambda e: e.transpose(PX[:, 0:128], VT[:, 0:128],
                                                SCR_F[:, 0:128]),
                waits=[lt, idf, zg, tmvE], inc=sT)
    tmv = pr.op("scalar", lambda e: e.activation(modv[:, 16:48],
                                                 PX[:, 0:32],
                                                 AF.Identity),
                waits=[ttr], inc=sA)
    scr_tok = [tmv, tmv]

    t_oca = pr.op("vector", lambda e: e.tensor_scalar_add(one_ca[:, :],
                                                          modv[:, 8:16], 1.0),
                  waits=[tmvE], inc=sV)
    t_ocb = pr.op("vector", lambda e: e.tensor_scalar_add(one_cb[:, :],
                                                          modv[:, 32:40], 1.0),
                  waits=[tmv], inc=sV)

    def sign_clamp(dstv, col0, w0):
        a = pr.op("scalar", lambda e: e.activation(SCR_F[:, 0:PC],
                                                   modv[:, col0:col0 + PC],
                                                   AF.Abs),
                  waits=[tmv, w0], inc=sA)
        b = pr.op("vector", lambda e: e.tensor_scalar_max(
            SCR_F[:, 0:PC], SCR_F[:, 0:PC], 0.25), waits=[a], inc=sV)
        c1 = pr.op("vector", lambda e: e.tensor_single_scalar(
            VT[:, 0:PC], modv[:, col0:col0 + PC], 0.0, ALU.is_ge),
            waits=[tmv, w0], inc=sV)
        c2 = pr.op("vector", lambda e: e.tensor_scalar(
            VT[:, 0:PC], VT[:, 0:PC], 2.0, -1.0, ALU.mult, ALU.add),
            waits=[c1], inc=sV)
        return pr.op("vector", lambda e: e.tensor_mul(
            dstv[:, :], SCR_F[:, 0:PC], VT[:, 0:PC]),
            waits=[b, c2], inc=sV)

    t_ga = sign_clamp(ga2, 16, scr_tok[0])
    t_gb = sign_clamp(gb2, 40, t_ga)
    scr_tok = [t_gb, t_gb]

    wfa = z_am
    for k in range(PC):
        lt = pr.dma("sync", SCR_F[:, :], wf_t[k * 128:(k + 1) * 128, :],
                    waits=[scr_tok[0] if k == 0 else wfa], inc=sCF)
        rt = pr.op("vector", lambda e: e.reduce_max(
            red_c[:, :], SCR_F[:, :], axis=AX.X, apply_absolute_value=True),
            waits=[lt, z_am], inc=sV)
        wfa = pr.op("vector", lambda e: e.tensor_max(
            amax_c[:, :], amax_c[:, :], red_c[:, :]), waits=[rt], inc=sV)
    d1 = pr.dma("scalar", wrow[0, :], amax_c[:, 0], waits=[wfa], inc=sB1)
    d2 = pr.dma("sync", RB[0:1, 0:128], wrow[:, :],
                waits=[d1, mod_st[-1]], inc=sB2)
    r1 = pr.op("vector", lambda e: e.reduce_max(small[0:1, 0:1],
                                                RB[0:1, 0:128], axis=AX.X),
               waits=[d2], inc=sV)
    r2 = pr.op("vector", lambda e: e.tensor_scalar_max(
        small[0:1, 1:2], small[0:1, 0:1], 1e-12), waits=[r1], inc=sV)
    r3 = pr.op("vector", lambda e: e.reciprocal(small[0:1, 2:3],
                                                small[0:1, 1:2]),
               waits=[r2], inc=sV)
    r4 = pr.op("vector", lambda e: e.tensor_scalar(
        small[0:1, 3:4], small[0:1, 2:3], FP8_MAX, FP8_MAX, ALU.mult,
        ALU.min), waits=[r3], inc=sV)
    r5 = pr.op("vector", lambda e: e.reciprocal(small[0:1, 4:5],
                                                small[0:1, 3:4]),
               waits=[r4], inc=sV)
    d3 = pr.dma("scalar", srow[0:1, 0:2], small[0:1, 3:5], waits=[r5],
                inc=sB1)
    d4 = pr.dma("sync", small[:, 5:6], ap(srow, 0, [[0, 128], [1, 1]]),
                waits=[d3], inc=sB2)
    wqp = None
    rbtok = d2
    for k in range(PC):
        lt = pr.dma("sync", SCR_F[:, :], wf_t[k * 128:(k + 1) * 128, :],
                    waits=[wqp, d4], inc=sCF)
        y = pr.op("vector", lambda e: e.tensor_scalar_mul(
            VT[:, :], SCR_F[:, :], small[:, 5:6]),
            waits=[lt, scr_tok[1] if k == 0 else None, d4], inc=sV)
        tt = pr.op("vector", lambda e: e.tensor_scalar_mul(
            SCR_F[:, :], VT[:, :], VELT_C), waits=[y], inc=sV)
        v1 = pr.op("vector", lambda e: e.tensor_sub(
            RB[:, 0:512], SCR_F[:, 0:512], VT[:, 0:512]),
            waits=[tt, rbtok], inc=sV)
        h1 = pr.op("vector", lambda e: e.tensor_sub(
            SCR_B[:, 0:512], SCR_F[:, 0:512], RB[:, 0:512]),
            waits=[v1, sb_tok[0]], inc=sV)
        v2 = pr.op("vector", lambda e: e.tensor_sub(
            RB[:, 0:512], SCR_F[:, 512:1024], VT[:, 512:1024]),
            waits=[h1], inc=sV)
        h2_ = pr.op("vector", lambda e: e.tensor_sub(
            SCR_B[:, 512:1024], SCR_F[:, 512:1024], RB[:, 0:512]),
            waits=[v2], inc=sV)
        st = pr.dma("scalar", wf_q[k * 128:(k + 1) * 128, :],
                    SCR_B[:, 0:1024], waits=[h2_], inc=sOF)
        wqp = st
        rbtok = h2_
        sb_tok[0] = st
    wfq_done = wqp

    z_S = pr.op("vector", lambda e: e.memset(S[:, :], 0.0),
                waits=[sb_tok[0], sb_tok[1], cast_last[1]], inc=sV)
    z_am2 = pr.op("vector", lambda e: e.memset(amax_c[:, :], 0.0),
                  waits=[d1], inc=sV)

    NCH = T // CH
    og2f = og2.ap().bitcast(F32)
    prol_rows = {}
    prol_store = {}
    stage_tok = [None, None]
    bank_chain = [None, None]
    for cc in range(NCH):
        sb2 = cc % 2
        stb = og2 if sb2 == 0 else gsil
        stf = og2f if sb2 == 0 else gsf
        bank = (cc // 2) % 2
        PXB = PXS[bank]
        s0 = (cc % 2) * 256
        xapc = ap(x_res, cc * CH, [[PC * T, 128], [T, PC], [1, CH]])
        cst = pr.op("scalar", lambda e, i=xapc, o=stb: e.activation(
            ap(o, 0, [[o.shape[1], 128], [CH, PC], [1, CH]]), i,
            AF.Identity), waits=xloads + [stage_tok[sb2]], inc=sA, ns=900)
        sq = pr.op("vector", lambda e, i=xapc, o=stb: e.tensor_tensor(
            ap(o, 1024, [[o.shape[1], 128], [CH, PC], [1, CH]]), i, i,
            ALU.mult), waits=xloads + [stage_tok[sb2]], inc=sV, ns=1250)
        mm1 = mm2 = None
        rn1 = pr.new_run()
        for k in range(PC):
            mm1 = pr.op("tensor", lambda e, kk=k, o=stb, b=PXB, s=s0:
                        e.matmul(b[:, s:s + 128], ones_s[:, :],
                                 o[:, kk * 128:(kk + 1) * 128],
                                 start=(kk == 0), stop=(kk == PC - 1)),
                        waits=[cst, misc_tok, bank_chain[bank], mm1],
                        ns=230, run=rn1)
        rn2 = pr.new_run()
        for k in range(PC):
            mm2 = pr.op("tensor", lambda e, kk=k, o=stb, b=PXB, s=s0:
                        e.matmul(b[:, s + 128:s + 256], ones_s[:, :],
                                 o[:, 1024 + kk * 128:1024 + (kk + 1) * 128],
                                 start=(kk == 0), stop=(kk == PC - 1)),
                        waits=[sq, mm1, mm2], ns=230, run=rn2)
        rm = pr.op("scalar", lambda e, b=PXB, s=s0: e.activation(
            b[:, s:s + 128], b[:, s:s + 128], AF.Identity, bias=ZC(),
            scale=consts[:, 1:2]), waits=[mm2], inc=sA, ns=420)
        rq = pr.op("scalar", lambda e, b=PXB, s=s0: e.activation(
            b[:, s + 128:s + 256], b[:, s + 128:s + 256], AF.Identity,
            bias=ZC(), scale=consts[:, 1:2]), waits=[mm2, rm], inc=sA,
            ns=420)
        t1 = pr.op("scalar", lambda e, b=PXB, s=s0, f=stf: e.activation(
            f[:, 0:128], b[:, s:s + 128], AF.Square),
            waits=[rm, mm2], inc=sA, ns=420)
        a2 = pr.op("vector", lambda e, b=PXB, s=s0, f=stf: e.tensor_sub(
            f[:, 0:128], b[:, s + 128:s + 256], f[:, 0:128]),
            waits=[t1, rq], inc=sV, ns=200)
        a3 = pr.op("scalar", lambda e, f=stf: e.activation(
            f[:, 0:128], f[:, 0:128], AF.Sqrt, bias=consts[:, 0:1],
            scale=OC()), waits=[a2], inc=sA, ns=520)
        a4 = pr.op("vector", lambda e, b=PXB, s=s0, f=stf: e.reciprocal(
            b[:, s + 128:s + 256], f[:, 0:128]), waits=[a3], inc=sV,
            ns=900)
        cp = pr.op("scalar", lambda e, b=PXB, s=s0, f=stf: e.activation(
            f[:, 128:384], b[:, s:s + 256], AF.Identity),
            waits=[a4, mm2], inc=sA, ns=450)
        st = pr.dma("scalar", ln1_dram[:, cc * 256:(cc + 1) * 256],
                    stf[:, 128:384], waits=[cp], inc=sPO)
        prol_rows[cc] = [rm, a4]
        prol_store[cc] = st
        stage_tok[sb2] = [mm2, a4, st]
        bank_chain[bank] = cp

    PB4 = PB + PXS
    pb_free = [bank4_tok[0], bank4_tok[1],
               [bank_chain[0]], [bank_chain[1]]]
    po_free = [bank4_tok[2], bank4_tok[3]]
    px_chain = [tmv, tmvE]
    ptr_chain = [None]
    pass_i = [0]
    pool_pos = [0]

    tok = {
        "xh": None, "xh2": None, "qt_w": None, "kt_w": None,
        "qt_r": None, "kt_r": None,
        "v_w": [None, None], "v_r": [None, None], "RBr": rbtok,
        "gsil_w": None, "gsil_r": [stage_tok[1], ttrE],
        "og2_w": None, "og2_r": [stage_tok[0]],
        "S": z_S, "Sbf_w": None, "Sbf_r": None,
        "ktm_w": None, "ktm_r": None, "sco_w": None, "sco_r": None,
        "ropeB": None, "tab": None, "RB": rbtok,
        "RBv": rbtok, "RBs": rbtok,
        "SCRF": z_S, "SB": [z_S, z_S, z_S, z_S], "VT": z_S, "VT2": z_S,
        "RBst": rbtok, "redc": d1,
        "slab": [cast_last[0], cast_last[1]] * 3, "xmid": {},
    }
    f1_done = []
    stat_st = {}
    res_toks = {}
    out_tok = [None]

    def ln_stats(xap, waits_read):
        xb = pr.op("scalar", lambda e, i=xap: e.activation(
            ap(SCR_B, 0, [[2048, 128], [CH, PC], [1, CH]]), i, AF.Identity),
            waits=waits_read + [tok["SB"][0], tok["SB"][1]], inc=sA, ns=900)
        x2 = pr.op("vector", lambda e, i=xap: e.tensor_tensor(
            ap(SCR_B, 1024, [[2048, 128], [CH, PC], [1, CH]]), i, i,
            ALU.mult),
            waits=waits_read + [tok["SB"][2], tok["SB"][3], xb], inc=sV,
            ns=1250)
        slA = pass_i[0] % 4
        pass_i[0] += 1
        slB = pass_i[0] % 4
        pass_i[0] += 1
        mm1 = mm2 = None
        rn1 = pr.new_run()
        for k in range(PC):
            mm1 = pr.op("tensor", lambda e, kk=k, s2=slA: e.matmul(
                PB4[s2][:, 0:128], ones_s[:, :],
                SCR_B[:, kk * 128:(kk + 1) * 128],
                start=(kk == 0), stop=(kk == PC - 1)),
                waits=[xb, misc_tok, pb_free[slA], mm1], ns=230, run=rn1)
        rn2 = pr.new_run()
        for k in range(PC):
            mm2 = pr.op("tensor", lambda e, kk=k, s2=slB: e.matmul(
                PB4[s2][:, 0:128], ones_s[:, :],
                SCR_B[:, 1024 + kk * 128:1024 + (kk + 1) * 128],
                start=(kk == 0), stop=(kk == PC - 1)),
                waits=[x2, pb_free[slB], mm2], ns=230, run=rn2)
        rm = pr.op("scalar", lambda e, s2=slA: e.activation(
            RB[:, 0:128], PB4[s2][:, 0:128], AF.Identity, bias=ZC(),
            scale=consts[:, 1:2]), waits=[mm1, tok["RB"], tok["RBst"]],
            inc=sA, ns=420)
        rq = pr.op("scalar", lambda e, s2=slB: e.activation(
            RB[:, 128:256], PB4[s2][:, 0:128], AF.Identity, bias=ZC(),
            scale=consts[:, 1:2]), waits=[mm2, rm], inc=sA, ns=420)
        pb_free[slA] = rm
        pb_free[slB] = rq
        a1 = pr.op("vector", lambda e: e.tensor_mul(
            RB[:, 256:384], RB[:, 0:128], RB[:, 0:128]),
            waits=[rm, tok["RBv"]], inc=sV, ns=200)
        a2 = pr.op("vector", lambda e: e.tensor_sub(
            RB[:, 256:384], RB[:, 128:256], RB[:, 256:384]),
            waits=[a1, rq], inc=sV, ns=200)
        a3 = pr.op("scalar", lambda e: e.activation(
            RB[:, 256:384], RB[:, 256:384], AF.Sqrt, bias=consts[:, 0:1],
            scale=OC()), waits=[a2], inc=sA, ns=520)
        a4 = pr.op("vector", lambda e: e.reciprocal(
            RB[:, 128:256], RB[:, 256:384]), waits=[a3], inc=sV, ns=900)
        tok["RB"] = a4
        tok["RBv"] = a4
        tok["SB"][0] = mm1
        tok["SB"][1] = mm1
        tok["SB"][2] = mm2
        tok["SB"][3] = mm2
        return a4

    def ln_norm(xap, rows_tok):
        mb = ap(RB, 0, [[512, 128], [0, PC], [1, CH]])
        rb_ = ap(RB, 128, [[512, 128], [0, PC], [1, CH]])
        tt0 = pr.op("vector", lambda e, i=xap: e.tensor_tensor(
            ap(SCR_F, 0, [[1024, 128], [CH, PC], [1, CH]]), i, mb,
            ALU.subtract), waits=[rows_tok, tok["SCRF"]], inc=sV, ns=1250)
        tt1 = pr.op("vector", lambda e: e.tensor_tensor(
            ap(SCR_F, 0, [[1024, 128], [CH, PC], [1, CH]]),
            ap(SCR_F, 0, [[1024, 128], [CH, PC], [1, CH]]), rb_, ALU.mult),
            waits=[tt0, rows_tok], inc=sV, ns=1250)
        tok["SCRF"] = tt1
        tok["RB"] = tt1
        return tt1

    def tile_pass(src_bf, m, kcount, rhs_fn, xtra, nw=GR, split=False,
                  xtra_mid=None):
        slot = pass_i[0] % 4
        pass_i[0] += 1
        wide = kcount * 128
        nwin = wide // 1024
        if nwin == 2 and pool_pos[0] % 2 == 1:
            pool_pos[0] += 1
        w0_ = pool_pos[0] % 6
        pool_pos[0] += nwin
        sl = slabP.ap()[:, w0_ * 1024: w0_ * 1024 + wide]
        wt = [tok["slab"][(w0_ + j) % 6] for j in range(nwin)]
        wt = wt + [w_cast.get(src_bf.name)]
        if split and nwin == 2:
            lt = pr.dma("sync", sl[:, 0:1024],
                        src_bf[m * 128:(m + 1) * 128, 0:1024],
                        waits=wt, inc=sSlab[w0_ % 3])
            lt2 = pr.dma("scalar", sl[:, 1024:2048],
                         src_bf[m * 128:(m + 1) * 128, 1024:2048],
                         waits=wt, inc=sSlabS[(w0_ + 1) % 3])
        else:
            lt = pr.dma("sync" if w0_ % 2 == 0 else "scalar",
                        sl[:, 0:wide],
                        src_bf[m * 128:(m + 1) * 128, 0:wide],
                        waits=wt,
                        inc=(sSlab if w0_ % 2 == 0 else sSlabX)[w0_ % 3])
            lt2 = lt
        mmt = None
        rn = pr.new_run()
        for k in range(kcount):
            mmt = pr.op("tensor", lambda e, kk=k, s=sl, sl2=slot, w=nw:
                        e.matmul(PB4[sl2][:, 0:w],
                                 s[:, kk * 128:(kk + 1) * 128], rhs_fn(kk),
                                 start=(kk == 0), stop=(kk == kcount - 1)),
                        waits=[lt if k * 128 < 1024 else lt2,
                               pb_free[slot], mmt] +
                        (xtra if k == 0 else []) +
                        (xtra_mid if (xtra_mid is not None and
                                      k == kcount // 2) else []),
                        ns=(330 if nw == GR else 230), run=rn)
        for j in range(nwin):
            tok["slab"][(w0_ + j) % 6] = [mmt, lt2]
        return slot, mmt

    def v_pass_gen(c0, xh_tok, xb2):
        for m in range(PC):
            if pool_pos[0] % 2 == 1:
                pool_pos[0] += 1
            w0_ = pool_pos[0] % 6
            pool_pos[0] += 2
            sl = slabP.ap()[:, w0_ * 1024: w0_ * 1024 + 2048]
            wt = [tok["slab"][w0_], tok["slab"][(w0_ + 1) % 6],
                  w_cast[wv_bf.name]]
            lt = pr.dma("sync", sl[:, 0:1024],
                        wv_bf[m * 128:(m + 1) * 128, 0:1024],
                        waits=wt, inc=sSlab[w0_ % 3])
            lt2 = pr.dma("scalar", sl[:, 1024:2048],
                         wv_bf[m * 128:(m + 1) * 128, 1024:2048],
                         waits=wt, inc=sSlabS[(w0_ + 1) % 3])
            mm_both = []
            for ci in range(2):
                cc = c0 + ci
                voff = (cc % 2) * DVT
                slot = pass_i[0] % 4
                pass_i[0] += 1
                mmt = None
                rn = pr.new_run()
                for k in range(PC):
                    mmt = pr.op("tensor",
                                lambda e, kk=k, cci=ci, s=sl, sl2=slot,
                                x2_=xb2:
                                e.matmul(PB4[sl2][:, 0:256],
                                         x2_[:, kk * GR + cci * CH:
                                             kk * GR + cci * CH + CH],
                                         s[:, kk * 256:(kk + 1) * 256],
                                         start=(kk == 0),
                                         stop=(kk == PC - 1)),
                                waits=[lt if k < 4 else lt2, xh_tok,
                                       pb_free[slot], mmt],
                                ns=330, run=rn)
                ev = pr.op("scalar", lambda e, mm=m, sl2=slot, vo=voff:
                           e.activation(
                               v_tm[:, vo + mm * 256: vo + (mm + 1) * 256],
                               PB4[sl2][:, 0:256], AF.Identity),
                           waits=[mmt, tok["v_r"][cc % 2]], inc=sA, ns=480)
                pb_free[slot] = ev
                tok["v_w"][cc % 2] = ev
                mm_both.append(mmt)
            tok["slab"][w0_] = [mm_both, lt2]
            tok["slab"][(w0_ + 1) % 6] = [mm_both, lt2]
            yield

    def g_pass(m, xh_tok, xb2):
        slot, mmt = tile_pass(wg_bf, m, PC,
                              lambda k, x2_=xb2: x2_[:, k * GR:(k + 1) * GR],
                              [xh_tok])
        gl = pr.op("scalar", lambda e, mm=m, sl2=slot: e.activation(
            gsil[:, mm * GR:(mm + 1) * GR], PB4[sl2][:, 0:GR], AF.Silu),
            waits=[mmt, tok["gsil_r"]], inc=sA, ns=520)
        pb_free[slot] = gl
        return gl, mmt

    def wo_pass_gen(ci, cc):
        wo_mms = []
        xms = []
        for m in range(PC):
            slot, mmt = tile_pass(wo_m, m, DPC,
                                  lambda k: og2[:, k * 128:(k + 1) * 128],
                                  [tok["og2_w"][0:2]], nw=128, split=True,
                                  xtra_mid=[tok["og2_w"][2:4]])
            xm = pr.op("vector",
                       lambda e, mm=m, sl2=slot, c2=cc:
                       e.scalar_tensor_tensor(
                           x_res[:, mm * T + c2 * CH:
                                 mm * T + (c2 + 1) * CH],
                           PB4[sl2][:, 0:CH],
                           ga2[:, mm:mm + 1],
                           x_res[:, mm * T + c2 * CH:
                                 mm * T + (c2 + 1) * CH],
                           ALU.mult, ALU.add),
                       waits=[mmt, t_ga], inc=sV, ns=200)
            pb_free[slot] = xm
            wo_mms.append(mmt)
            xms.append(xm)
            tok["og2_r"] = wo_mms
            tok["xmid"][cc] = xms
            yield

    def f1_stats(cc):
        xap = ap(x_res, cc * CH, [[PC * T, 128], [T, PC], [1, CH]])
        rows_tok = ln_stats(xap, [tok["xmid"][cc]])
        st = pr.dma("scalar", stats_dram[:, cc * 256:(cc + 1) * 256],
                    RB[:, 0:256], waits=[rows_tok], inc=sStatO)
        tok["RBst"] = st
        stat_st[cc] = st
        return xap, rows_tok

    def f1_chain(cc, xap, rows_tok):
        tt1 = ln_norm(xap, rows_tok)
        hts = []
        for p in range(PC):
            hts.append(pr.op("vector", lambda e, pp=p: e.tensor_scalar(
                SCR_F[:, pp * CH:(pp + 1) * CH],
                SCR_F[:, pp * CH:(pp + 1) * CH],
                one_cb[:, pp:pp + 1], modv[:, 24 + pp:25 + pp],
                ALU.mult, ALU.add), waits=[tt1, t_ocb], inc=sV))
        rd = pr.op("vector", lambda e: e.reduce_max(
            red_c[:, :], SCR_F[:, :], axis=AX.X,
            apply_absolute_value=True), waits=hts + [tok["redc"]], inc=sV)
        am = pr.op("vector", lambda e: e.tensor_max(
            amax_c[:, :], amax_c[:, :], red_c[:, :]),
            waits=[rd, z_am2], inc=sV)
        tok["redc"] = am
        f1_done.append(am)
        tok["SCRF"] = rd

    def retention_chunk(ci, cc, fillers):
        voff = (cc % 2) * DVT
        og_last = None
        og_alls = []
        sco_readers = []
        head_info = []
        sbf_prev = tok["Sbf_w"]
        for hh in range(H):
            sl_s = pass_i[0] % 4
            pass_i[0] += 1
            mmt = None
            rn = pr.new_run()
            for b in range(2):
                pcq = 2 * hh + b
                mmt = pr.op("tensor",
                            lambda e, pq=pcq, bb=b, cci=ci, s2=sl_s:
                            e.matmul(PB4[s2][:, 0:128],
                                     kt[:, pq * GR + cci * CH:
                                        pq * GR + cci * CH + CH],
                                     qt[:, pq * GR + cci * CH:
                                        pq * GR + cci * CH + CH],
                                     start=(bb == 0), stop=(bb == 1)),
                            waits=[tok["qt_w"], tok["kt_w"], pb_free[sl_s],
                                   mmt], ns=230, run=rn)
            sct = pr.op("vector", lambda e, h2=hh, s2=sl_s:
                        e.scalar_tensor_tensor(
                            sco[:, h2 * 128:(h2 + 1) * 128],
                            PB4[s2][:, 0:128],
                            lamsig_s[:, h2:h2 + 1], mask_s[:, :],
                            ALU.mult, ALU.mult),
                        waits=[mmt, misc_tok, tok["sco_r"]], inc=sV, ns=220)
            pb_free[sl_s] = sct
            tok["sco_w"] = sct
            if fillers:
                fillers[min(hh, len(fillers) - 1)](hh)

            if cc > 0:
                sb_w = [tok["S"], sbf_prev, tok["Sbf_r"], tok["ropeB"]]
                sb1 = pr.op("vector", lambda e, h2=hh: e.tensor_copy(
                    S_bf[:, 0:512], S[:, h2 * 1024:h2 * 1024 + 512]),
                    waits=sb_w, inc=sV, ns=650)
                sb2 = pr.op("scalar", lambda e, h2=hh: e.activation(
                    S_bf[:, 512:1024], S[:, h2 * 1024 + 512:(h2 + 1) * 1024],
                    AF.Identity), waits=sb_w, inc=sA, ns=550)
                sbf_prev = [sb1, sb2]
                tok["Sbf_w"] = sbf_prev
                tok["ropeB"] = sbf_prev

            posl = hh % 2
            otok = None
            rn_o = pr.new_run()
            for b in range(4):
                w0 = [sct, tok["v_w"][cc % 2], po_free[posl], otok]
                if cc == 0:
                    otok = pr.op("tensor",
                                 lambda e, h2=hh, bb=b, pp=posl, vo=voff:
                                 e.matmul(
                                     PO[pp][:, bb * 128:(bb + 1) * 128],
                                     v_tm[:, vo + h2 * DV + bb * 128:
                                          vo + h2 * DV + (bb + 1) * 128],
                                     sco[:, h2 * 128:(h2 + 1) * 128],
                                     start=True, stop=True),
                                 waits=w0, ns=230, run=rn_o)
                else:
                    t_a = pr.op("tensor",
                                lambda e, h2=hh, bb=b, pp=posl, vo=voff:
                                e.matmul(PO[pp][:, bb * 128:(bb + 1) * 128],
                                         v_tm[:, vo + h2 * DV + bb * 128:
                                              vo + h2 * DV + (bb + 1) * 128],
                                         sco[:, h2 * 128:(h2 + 1) * 128],
                                         start=True, stop=False),
                                waits=w0 + [sbf_prev], ns=230, run=rn_o)
                    t_b = pr.op("tensor", lambda e, h2=hh, bb=b, pp=posl,
                                cci=ci:
                                e.matmul(PO[pp][:, bb * 128:(bb + 1) * 128],
                                         S_bf[:, bb * 128:(bb + 1) * 128],
                                         qt[:, (2 * h2) * GR + cci * CH:
                                            (2 * h2) * GR + cci * CH + CH],
                                         start=False, stop=False),
                                waits=[sbf_prev, tok["qt_w"], t_a], ns=230,
                                run=rn_o)
                    otok = pr.op("tensor", lambda e, h2=hh, bb=b,
                                 pp=posl, cci=ci:
                                 e.matmul(
                                     PO[pp][:, bb * 128:(bb + 1) * 128],
                                     S_bf[:, 512 + bb * 128:
                                          512 + (bb + 1) * 128],
                                     qt[:, (2 * h2 + 1) * GR + cci * CH:
                                        (2 * h2 + 1) * GR + cci * CH + CH],
                                     start=False, stop=True),
                                 waits=[t_b], ns=230, run=rn_o)
            tok["Sbf_r"] = otok
            sco_readers.append(otok)

            par = hh % 2
            osoff = 1024 + par * 512
            o2off = par * 512
            ssoff = 128 + par * 256
            oev = pr.op("scalar", lambda e, pp=posl, oo=osoff: e.activation(
                SCR_B[:, oo:oo + 512], PO[pp][:, 0:512], AF.Identity),
                waits=[otok, tok["SB"][2 + par]], inc=sA, ns=600)
            po_free[posl] = oev
            o2_ = pr.op("vector", lambda e, oo=osoff, so=o2off: e.tensor_mul(
                SCR_B[:, so:so + 512], SCR_B[:, oo:oo + 512],
                SCR_B[:, oo:oo + 512]), waits=[oev, tok["SB"][par]],
                inc=sV, ns=380)
            ssm = None
            rn_s = pr.new_run()
            for b in range(4):
                ssm = pr.op("tensor", lambda e, bb=b, so=o2off, sso=ssoff:
                            e.matmul(
                    PX[:, sso:sso + 128], ones_s[:, :],
                    SCR_B[:, so + bb * 128:so + (bb + 1) * 128],
                    start=(bb == 0), stop=(bb == 3)),
                    waits=[o2_, px_chain[0], ssm], ns=230, run=rn_s)
            px_chain[0] = ssm
            tok["SB"][par] = ssm
            head_info.append((hh, ssm, oev))
            if par == 1:
                (h0, ssm0, oev0), (h1, ssm1, oev1) = head_info[-2:]
                ssg = ap(PX, 128, [[512, 128], [256, 2], [1, 128]])
                rbg = ap(RB, 256, [[512, 128], [128, 2], [1, 128]])
                sdt2 = pr.op("scalar", lambda e, i=ssg, o=rbg: e.activation(
                    o, i, AF.Sqrt, bias=consts[:, 0:1],
                    scale=consts[:, 2:3]),
                    waits=[ssm0, ssm1, tok["RBv"], tok["RBs"]], inc=sA,
                    ns=560)
                rst2 = pr.op("vector", lambda e: e.reciprocal(
                    RB[:, 256:512], RB[:, 256:512]), waits=[sdt2], inc=sV,
                    ns=900)
                px_chain[0] = rst2
                for h3, oevx in ((h0, oev0), (h1, oev1)):
                    rboff = 256 + (h3 % 2) * 128
                    osx = 1024 + (h3 % 2) * 512
                    rstb = ap(RB, rboff, [[512, 128], [0, 4], [1, 128]])
                    gslice = ap(gsil, h3 * 4 * GR + ci * CH,
                                [[DPC * GR, 128], [GR, 4], [1, CH]])
                    oslice = ap(og2, h3 * 4 * 128,
                                [[DPC * 128, 128], [128, 4], [1, CH]])
                    g1t = pr.op("vector",
                                lambda e, o=oslice, gs=gslice, rb2=rstb:
                                e.tensor_tensor(o, gs, rb2, ALU.mult),
                                waits=[rst2, [tok["gsil_w"][4 * h3 + j]
                                              for j in range(4)],
                                       tok["og2_r"]], inc=sV, ns=620)
                    og_last = pr.op("vector", lambda e, o=oslice, oo=osx:
                                    e.tensor_tensor(
                                        o, o,
                                        ap(SCR_B, oo,
                                           [[2048, 128], [128, 4], [1, 128]]),
                                        ALU.mult),
                                    waits=[g1t, oevx], inc=sV, ns=620)
                    tok["SB"][2 + h3 % 2] = og_last
                    if h3 % 2 == 0:
                        tok["RBv"] = og_last
                    else:
                        tok["RBs"] = og_last
                    tok["gsil_r"] = og_last
                    og_alls.append(og_last)
        tok["og2_w"] = og_alls
        tok["sco_r"] = sco_readers

    def u_fold_chunk(ci, cc, ktm_last, wo_gen):
        voff = (cc % 2) * DVT
        last_umt = None
        umts = []
        for hh in range(H):
            for b in range(2):
                slot = pass_i[0] % 4
                pass_i[0] += 1
                umt = pr.op("tensor",
                            lambda e, h2=hh, bb=b, pp=slot, vo=voff:
                            e.matmul(PB4[pp][:, 0:512],
                                     k_tm[:, h2 * DK + bb * 128:
                                          h2 * DK + (bb + 1) * 128],
                                     v_tm[:, vo + h2 * DV:
                                          vo + (h2 + 1) * DV],
                                     start=True, stop=True),
                            waits=[ktm_last, tok["v_w"][cc % 2],
                                   pb_free[slot]], ns=590)
                lc = float(lamh(hh) ** CH)
                fld = pr.op("vector", lambda e, h2=hh, bb=b, lcv=lc,
                            pp=slot: e.scalar_tensor_tensor(
                                S[:, (2 * h2 + bb) * 512:
                                  (2 * h2 + bb + 1) * 512],
                                S[:, (2 * h2 + bb) * 512:
                                  (2 * h2 + bb + 1) * 512],
                                lcv, PB4[pp][:, 0:512],
                                ALU.mult, ALU.add),
                            waits=[umt, tok["S"], tok["Sbf_w"]], inc=sV,
                            ns=680)
                pb_free[slot] = fld
                tok["S"] = fld
                last_umt = umt
                umts.append(umt)
                if wo_gen is not None:
                    next(wo_gen, None)
        tok["v_r"][cc % 2] = umts
        tok["ktm_r"] = umts
        tok["ropeB"] = tok["S"]
        return last_umt

    def transposes(ci, cc):
        tts = []
        tt_last = None
        rn_t = pr.new_run()
        rn_k = pr.new_run()
        for s8 in range(8):
            tt_last = pr.op("tensor", lambda e, pq=s8, cci=ci:
                            e.transpose(PTR[:, pq * 128:(pq + 1) * 128],
                                        kt[:, pq * GR + cci * CH:
                                           pq * GR + cci * CH + CH],
                                        ident_s[:, :]),
                            waits=[tok["kt_w"], misc_tok, ptr_chain[0],
                                   tt_last], ns=300, run=rn_t)
            tts.append(tt_last)
        ktm_last = None
        for s8 in range(8):
            hh2, bb2 = s8 // 2, s8 % 2
            ktm_last = pr.op("scalar", lambda e, pq=s8, h2=hh2, bb=bb2:
                             e.activation(
                                 k_tm[:, h2 * DK + bb * 128:
                                      h2 * DK + (bb + 1) * 128],
                                 PTR[:, pq * 128:(pq + 1) * 128],
                                 AF.Identity, bias=ZC(),
                                 scale=lamk_s[:, h2:h2 + 1]),
                             waits=[tts[7], misc_tok, tok["ktm_r"],
                                    ktm_last], ns=420, run=rn_k)
        ptr_chain[0] = ktm_last
        return ktm_last

    ln1_xh = {}
    vtn_tok = [z_S]
    pxs_free = {}
    for _b in range(2):
        for _ci in range(2):
            pxs_free[(_b, _ci)] = [bank_chain[_b]]

    def ln1_group(gn):
        bank = gn % 2
        xh_b = xh if gn % 2 == 0 else xh2
        xh_all = []
        for ci in range(2):
            cc = 2 * gn + ci
            ld = pr.dma("sync", RB[:, 0:256],
                        ln1_dram[:, cc * 256:(cc + 1) * 256],
                        waits=[prol_store[cc], tok["RB"], tok["RBst"]],
                        inc=sPI)
            rows_tok = [ld]
            mb = ap(RB, 0, [[512, 128], [0, PC], [1, CH]])
            rb_ = ap(RB, 128, [[512, 128], [0, PC], [1, CH]])
            xapc = ap(x_res, cc * CH, [[PC * T, 128], [T, PC], [1, CH]])
            w_extra = [tok["v_r"][0], tok["og2_w"]] if gn >= 1 else []
            tt0 = pr.op("vector", lambda e, i=xapc, m=mb: e.tensor_tensor(
                VT[:, :].rearrange("p (a b) -> p a b", a=PC), i, m,
                ALU.subtract),
                waits=rows_tok + vtn_tok + w_extra + xloads, inc=sV,
                ns=1250)
            tt1 = pr.op("vector", lambda e, r=rb_: e.tensor_tensor(
                VT[:, :].rearrange("p (a b) -> p a b", a=PC),
                VT[:, :].rearrange("p (a b) -> p a b", a=PC), r, ALU.mult),
                waits=[tt0] + rows_tok, inc=sV, ns=1250)
            tok["RB"] = [tt0, tt1]
            rn_x = pr.new_run()
            xh_ci = []
            for p in range(PC):
                xh_ci.append(pr.op("scalar",
                             lambda e, pp=p, cci=ci, xb2=xh_b:
                             e.activation(
                    xb2[:, pp * GR + cci * CH: pp * GR + cci * CH + CH],
                    VT[:, pp * CH:(pp + 1) * CH], AF.Identity,
                    bias=modv[:, pp:pp + 1], scale=one_ca[:, pp:pp + 1]),
                    waits=[tt1, t_oca, tok["xh2" if gn % 2 else "xh"]],
                    inc=sA, ns=420, run=rn_x))
            xh_all.extend(xh_ci)
            vtn_tok[:] = [xh_ci]
        ln1_xh[gn] = xh_all

    ln1_group(0)

    for g in range(NG):
        t0g = g * GR
        c0 = 2 * g
        tb1 = pr.dma("sync", cos_sl[:, :], cosT[:, t0g:t0g + GR],
                     waits=[tok["tab"]], inc=sTab)
        tb2 = pr.dma("sync", sin_sl[:, :], sinT[:, t0g:t0g + GR],
                     waits=[tok["tab"]], inc=sTab)
        ttab = [tb1, tb2]

        xh_b = xh if g % 2 == 0 else xh2
        xh_tok = ln1_xh[g]

        def proj_rope(wsrc, dst, dst_free, do_lam):
            outs = []
            pend = {}
            for m in range(PC):
                slot, mmt = tile_pass(wsrc, m, PC,
                                      lambda k, x2_=xh_b:
                                      x2_[:, k * GR:(k + 1) * GR],
                                      [xh_tok])
                pend[m] = (slot, mmt)
                if m % 2 == 0:
                    continue
                hh = m // 2
                sA_, mmA = pend[m - 1]
                sB_, mmB = pend[m]
                o1 = dst[:, (2 * hh) * GR:(2 * hh + 1) * GR]
                o2 = dst[:, (2 * hh + 1) * GR:(2 * hh + 2) * GR]
                rB1 = ropeB[:, 0:GR]
                rB2 = ropeB[:, GR:2 * GR]
                p1 = pr.op("vector", lambda e, pa=sA_, o=o1: e.tensor_tensor(
                    o, PB4[pa][:, 0:GR], cos_sl[:, :], ALU.mult),
                    waits=[mmA] + ttab + [dst_free], inc=sV, ns=300)
                p3 = pr.op("vector", lambda e, pa=sA_, o=o2: e.tensor_tensor(
                    o, PB4[pa][:, 0:GR], sin_sl[:, :], ALU.mult),
                    waits=[mmA] + ttab + [dst_free, p1], inc=sV, ns=300)
                q1 = pr.op("vector", lambda e, pb=sB_, r=rB1: e.tensor_tensor(
                    r, PB4[pb][:, 0:GR], sin_sl[:, :], ALU.mult),
                    waits=[mmB] + ttab + [tok["ropeB"]], inc=sV, ns=300)
                q2 = pr.op("vector", lambda e, pb=sB_, r=rB2: e.tensor_tensor(
                    r, PB4[pb][:, 0:GR], cos_sl[:, :], ALU.mult),
                    waits=[mmB] + ttab + [tok["ropeB"], q1], inc=sV, ns=300)
                p2 = pr.op("vector", lambda e, o=o1, r=rB1: e.tensor_tensor(
                    o, o, r, ALU.subtract), waits=[p1, q1], inc=sV, ns=300)
                p4 = pr.op("vector", lambda e, o=o2, r=rB2: e.tensor_tensor(
                    o, o, r, ALU.add), waits=[p3, q2], inc=sV, ns=300)
                pb_free[sA_] = [p1, p3]
                pb_free[sB_] = [q1, q2]
                tok["ropeB"] = [p2, p4]
                outs += [p2, p4]
            if do_lam:
                qall = ap(dst, 0, [[PC * GR, 128], [GR, PC],
                                   [CH, GR // CH], [1, CH]])
                lm = ap(lamq_s, 0, [[1024, 128], [128, PC],
                                    [0, GR // CH], [1, CH]])
                outs = [pr.op("vector", lambda e: e.tensor_tensor(
                    qall, qall, lm, ALU.mult),
                    waits=outs + [misc_tok], inc=sV, ns=800)]
            return outs

        tok["qt_w"] = proj_rope(wq_bf, qt, tok["qt_r"], True)
        tok["kt_w"] = proj_rope(wk_bf, kt, tok["kt_r"], False)

        tok["gsil_w"] = {}
        for m in range(4):
            gl, mmt = g_pass(m, xh_tok, xh_b)
            tok["gsil_w"][m] = gl
            xh_free = mmt
        g_next = [4]

        for _ in v_pass_gen(c0, xh_tok, xh_b):
            pass

        ktm_l = transposes(0, c0)

        def fill_c0(hh):
            while g_next[0] < min(4 * hh + 8, DPC):
                gl, _ = g_pass(g_next[0], xh_tok, xh_b)
                tok["gsil_w"][g_next[0]] = gl
                g_next[0] += 1

        retention_chunk(0, c0, [fill_c0] * 4)
        while g_next[0] < DPC:
            gl, _ = g_pass(g_next[0], xh_tok, xh_b)
            tok["gsil_w"][g_next[0]] = gl
            g_next[0] += 1
        wo0 = wo_pass_gen(0, c0)
        u_fold_chunk(0, c0, ktm_l, wo0)
        for _ in wo0:
            pass
        if g + 1 < NG:
            ln1_group(g + 1)
        xap0, rows0 = f1_stats(c0)

        ktm_l = transposes(1, c0 + 1)
        f1_emitted = [False]

        def fill_c1(hh):
            if not f1_emitted[0]:
                f1_emitted[0] = True
                f1_chain(c0, xap0, rows0)

        retention_chunk(1, c0 + 1, [fill_c1] * 4)
        wo1 = wo_pass_gen(1, c0 + 1)
        u_fold_chunk(1, c0 + 1, ktm_l, wo1)
        for _ in wo1:
            pass
        xap1, rows1 = f1_stats(c0 + 1)
        f1_chain(c0 + 1, xap1, rows1)

        tok["tab"] = tok["kt_w"]
        tok["qt_r"] = tok["v_r"][1]
        tok["kt_r"] = tok["v_r"][1]
        tok["xh"] = tok["v_r"][1]
        tok["xh2"] = tok["v_r"][1]

    e1 = pr.dma("scalar", cc_in[0, :], amax_c[:, 0], waits=f1_done, inc=sB1)
    wm2 = None
    for _w in range(500):
        if _w % 50 == 0:
            rn_w2 = pr.new_run()
        wm2 = pr.op("tensor", lambda e: e.matmul(
            PX[:, 0:128], ones_s[:, :], ones_s[:, :],
            start=True, stop=True),
            waits=[e1, px_chain[0], wm2], ns=110, run=rn_w2)
    cctok = pr.op("gpsimd", lambda e: e.collective_compute(
        "AllReduce", ALU.max, replica_groups=[list(range(n_cores))],
        ins=[cc_in.ap()], outs=[cc_out.ap()]), waits=[e1], inc=sCC,
        async_op=True)
    e2 = pr.dma("sync", RB[0:1, 256:384], cc_out[:, :],
                waits=[cctok, tok["RBv"], tok["RBs"]], inc=sB2)
    q1 = pr.op("vector", lambda e: e.reduce_max(
        small[0:1, 8:9], RB[0:1, 256:384], axis=AX.X), waits=[e2], inc=sV)
    q2 = pr.op("vector", lambda e: e.tensor_scalar_max(
        small[0:1, 9:10], small[0:1, 8:9], 1e-12), waits=[q1], inc=sV)
    q3 = pr.op("vector", lambda e: e.reciprocal(
        small[0:1, 10:11], small[0:1, 9:10]), waits=[q2], inc=sV)
    q4 = pr.op("vector", lambda e: e.tensor_scalar(
        small[0:1, 11:12], small[0:1, 10:11], FP8_MAX, FP8_MAX,
        ALU.mult, ALU.min), waits=[q3], inc=sV)
    q5 = pr.op("vector", lambda e: e.reciprocal(
        small[0:1, 12:13], small[0:1, 11:12]), waits=[q4], inc=sV)
    q6 = pr.op("vector", lambda e: e.tensor_mul(
        small[0:1, 13:14], small[0:1, 12:13], small[0:1, 4:5]),
        waits=[q5], inc=sV)
    e3 = pr.dma("scalar", srow[0:1, 2:3], small[0:1, 11:12], waits=[q4],
                inc=sB1)
    e4 = pr.dma("scalar", srow[0:1, 4:5], small[0:1, 13:14], waits=[q6],
                inc=sB1)
    e5 = pr.dma("sync", small[:, 14:15], ap(srow, 2, [[0, 128], [1, 1]]),
                waits=[e3], inc=sB2)
    e6 = pr.dma("sync", small[:, 15:16], ap(srow, 4, [[0, 128], [1, 1]]),
                waits=[e4], inc=sB2)
    b1 = pr.op("vector", lambda e: e.tensor_scalar_mul(
        ocbs[:, :], one_cb[:, :], small[:, 14:15]), waits=[e5, t_ocb], inc=sV)
    b2 = pr.op("vector", lambda e: e.tensor_scalar_mul(
        sbs[:, :], modv[:, 24:32], small[:, 14:15]), waits=[e5], inc=sV)

    tok["VT"] = tok["v_r"][0] + tok["v_r"][1]
    tok["VT2"] = tok["VT"]
    tok["scoF"] = None
    tok_os = [tok["qt_r"], tok["kt_r"]]
    og2h_r = [tok["og2_r"], tok["gsil_r"]]
    og2h_w = [tok["og2_w"], tok["gsil_r"]]
    h2q_buf = [og2, gsil]
    h2q_off = [0, 2048]

    def f2_chains(g):
        hbuf = h2q_buf[g % 2]
        hoff = h2q_off[g % 2]
        toks = []
        for ci in range(2):
            cc = 2 * g + ci
            xap = ap(x_res, cc * CH, [[PC * T, 128], [T, PC], [1, CH]])
            ld = pr.dma("sync", sco_f[:, 0:256],
                        stats_dram[:, cc * 256:(cc + 1) * 256],
                        waits=[stat_st[cc], tok["scoF"], tok["sco_r"]],
                        inc=sStatI)
            mb = ap(sco_f, 0, [[256, 128], [0, PC], [1, CH]])
            rb_ = ap(sco_f, 128, [[256, 128], [0, PC], [1, CH]])
            tt0 = pr.op("vector", lambda e, i=xap, m=mb: e.tensor_tensor(
                ap(SCR_F, 0, [[1024, 128], [CH, PC], [1, CH]]), i, m,
                ALU.subtract), waits=[ld, tok["SCRF"]], inc=sV,
                ns=1250)
            tt1 = pr.op("vector", lambda e, r=rb_: e.tensor_tensor(
                ap(SCR_F, 0, [[1024, 128], [CH, PC], [1, CH]]),
                ap(SCR_F, 0, [[1024, 128], [CH, PC], [1, CH]]), r, ALU.mult),
                waits=[tt0], inc=sV, ns=1250)
            tok["scoF"] = [tt0, tt1]
            hts = []
            rn_h = pr.new_run()
            for p in range(PC):
                hts.append(pr.op("scalar", lambda e, pp=p: e.activation(
                    VT[:, pp * CH:(pp + 1) * CH],
                    SCR_F[:, pp * CH:(pp + 1) * CH], AF.Identity,
                    bias=sbs[:, pp:pp + 1], scale=ocbs[:, pp:pp + 1]),
                    waits=[tt1, b1, b2, tok["VT"]], inc=sA, ns=420,
                    run=rn_h))
            aa = pr.op("scalar", lambda e: e.activation(
                SCR_F[:, :], VT[:, :], AF.Identity, bias=ZC(),
                scale=consts[:, 3:4]), waits=hts + [tt1], inc=sA, ns=900)
            bb_ = pr.op("vector", lambda e: e.tensor_sub(
                VT2[:, :], SCR_F[:, :], VT[:, :]),
                waits=[aa, tok["VT2"]], inc=sV, ns=1250)
            tok["VT"] = [hts, aa, bb_]
            outap = ap(hbuf, hoff + ci * CH,
                       [[hbuf.shape[1], 128], [GR, PC], [1, CH]])
            vlast = pr.op("vector", lambda e, o=outap: e.tensor_tensor(
                o, SCR_F[:, :].rearrange("p (a b) -> p a b", a=PC),
                VT2[:, :].rearrange("p (a b) -> p a b", a=PC),
                ALU.subtract),
                waits=[bb_, og2h_r[g % 2], og2h_w[g % 2]], inc=sV,
                ns=1250)
            tok["VT2"] = vlast
            tok["SCRF"] = [bb_, vlast]
            toks.append(vlast)
        og2h_w[g % 2] = toks
        return toks

    h2q_pend = f2_chains(0)
    for g in range(NG):
        c0 = 2 * g
        hbuf = h2q_buf[g % 2]
        hoff = h2q_off[g % 2]
        h2q_toks = h2q_pend
        ffn_mms = []
        evs = []
        for m in range(PC):
            slot, mmt = tile_pass(wf_q, m, PC,
                                  lambda k, hb=hbuf, bb=hoff:
                                  hb[:, bb + k * GR: bb + (k + 1) * GR],
                                  h2q_toks + [wfq_done])
            ffn_mms.append(mmt)
            ev = pr.op("scalar", lambda e, mm=m, sl2=slot: e.activation(
                gsil[:, mm * GR:(mm + 1) * GR], PB4[sl2][:, 0:GR],
                AF.Identity, bias=bf_s[:, mm:mm + 1], scale=small[:, 15:16]),
                waits=[mmt, e6, misc_tok, tok["gsil_r"]], inc=sA, ns=480)
            pb_free[slot] = ev
            evs.append(ev)
        og2h_r[g % 2] = ffn_mms
        if g + 1 < NG:
            h2q_pend = f2_chains(g + 1)
        rlast = [[], []]
        for m in range(PC):
            for ci in range(2):
                cc = c0 + ci
                rlast[ci].append(pr.op(
                    "vector", lambda e, mm=m, cci=ci, c2=cc, os=OS[ci]:
                    e.scalar_tensor_tensor(
                        os[:, mm * CH:(mm + 1) * CH],
                        gsil[:, mm * GR + cci * CH: mm * GR + cci * CH + CH],
                        gb2[:, mm:mm + 1],
                        x_res[:, mm * T + c2 * CH: mm * T + (c2 + 1) * CH],
                        ALU.mult, ALU.add),
                    waits=[evs[m], t_gb, tok_os[ci]], inc=sV, ns=200))
        tok["gsil_r"] = rlast[0] + rlast[1]
        for ci in range(2):
            cc = c0 + ci
            outap = ap(yT, cc * CH, [[T, 128], [128 * T, PC], [1, CH]])
            out_tok[0] = pr.op("scalar", lambda e, o=outap, i=OS[ci],
                               cci=ci: e.dma_start(
                                   out=o,
                                   in_=i[:, :].rearrange(
                                       "p (a b) -> p a b", a=PC)),
                               waits=rlast[ci], inc=sOout, amt=16)
            tok_os[ci] = out_tok[0]

    pr.op("scalar", lambda e: e.nop(), waits=[out_tok[0]])

    pr.emit()
    return nc, sbuf_left


def _host_constants(T):
    f = np.arange(128, dtype=np.float64)
    inv_freq = ROPE_BASE ** (-f / 128.0)
    t = np.arange(T, dtype=np.float64)
    ang = inv_freq[:, None] * t[None, :]
    lams = np.array([lamh(h) for h in range(H)], dtype=np.float64)
    tau = np.arange(CH, dtype=np.float64)
    lamq = np.zeros((128, 1024), dtype=np.float64)
    for pc in range(PC):
        lamq[:, pc * 128:(pc + 1) * 128] = (lams[pc // 2] ** (tau + 1.0)) / 16.0
    sig = np.arange(128, dtype=np.float64)
    consts = np.zeros((128, 8), dtype=np.float32)
    consts[:, 0] = EPS
    consts[:, 1] = 1.0 / D
    consts[:, 2] = 1.0 / DV
    consts[:, 3] = VELT_C
    consts[:, 4] = 1e-12
    consts[:, 5] = FP8_MAX
    consts[:, 7] = 1.0
    return dict(
        cosT=np.cos(ang).astype(BF), sinT=np.sin(ang).astype(BF),
        lamqI=lamq.astype(BF),
        lamsigI=np.stack([lams[h] ** (-(sig + 1.0)) for h in range(H)],
                         axis=1).astype(np.float32),
        lamkI=np.stack([lams[h] ** (127.0 - sig) for h in range(H)],
                       axis=1).astype(np.float32),
        maskI=np.triu(np.ones((128, 128))).astype(BF),
        identI=np.eye(128).astype(BF),
        onesI=np.ones((128, 128)).astype(BF),
        constsI=consts)


def _tile(WT, mwidth):
    IN, OUT = WT.shape
    nm, nk = OUT // mwidth, IN // 128
    return np.ascontiguousarray(
        WT.reshape(nk, 128, nm, mwidth).transpose(2, 1, 0, 3)
        .reshape(nm * 128, nk * mwidth))


def prep_inputs(inputs, T, n_cores):
    consts = _host_constants(T)
    f32 = lambda k: np.asarray(inputs[k], dtype=np.float32)
    x = f32("L0_attn_x")
    gnw_full = np.tile(f32("L0_attn_g_norm_w"), H)
    shared = dict(
        wq_t=_tile(np.ascontiguousarray(f32("L0_attn_Wq").T), 128),
        wk_t=_tile(np.ascontiguousarray(f32("L0_attn_Wk").T), 128),
        wv_t=_tile(np.ascontiguousarray(f32("L0_attn_Wv").T), 256),
        wg_t=_tile(np.ascontiguousarray(f32("L0_attn_Wg").T), 128),
        woT=np.ascontiguousarray(f32("L0_attn_Wo").T),
        wf_t=_tile(np.ascontiguousarray(f32("L0_ffn_weight").T), 128),
        wmod=np.ascontiguousarray(f32("L0_wmod")),
        gnw=np.ascontiguousarray(gnw_full.reshape(DPC, 128).T),
        bfc=np.ascontiguousarray(f32("L0_ffn_bias").reshape(PC, 128).T),
        **consts)
    cmat = f32("L0_c")
    in_maps = []
    for b in range(n_cores):
        m = dict(shared)
        m["xT"] = np.ascontiguousarray(x[b].T)
        m["cT"] = np.ascontiguousarray(cmat[b].reshape(PC, 128).T)
        in_maps.append(m)
    return in_maps


def kernel(**inputs):
    T = int(np.asarray(inputs["L0_attn_x"]).shape[1])
    n_cores = 8
    in_maps = prep_inputs(inputs, T, n_cores)
    nc, _ = build(T, n_cores)
    res = run_bass_kernel_spmd(nc, in_maps, core_ids=list(range(n_cores)))
    outs = [np.ascontiguousarray(np.asarray(res.results[b]["yT"]).T)
            for b in range(n_cores)]
    return np.stack(outs, axis=0).astype(np.float32)
